# version 01
# baseline (speedup 1.0000x reference)
import numpy as np
import ml_dtypes
from contextlib import ExitStack

import concourse.bass as bass
import concourse.mybir as mybir
from concourse.alu_op_type import AluOpType
from concourse.bass_utils import run_bass_kernel_spmd

F32 = mybir.dt.float32
BF16 = mybir.dt.bfloat16
AF = mybir.ActivationFunctionType
AX = mybir.AxisListType
OP = AluOpType

NCORE = 8
B = 4
S = 4096
H = 1024
M = 256
NEXP = 8
DFF = 4096
DECAY = 0.99
SC = S // NCORE
T = B * SC
C = 128
NCH = SC // C
NT = T // 128
NG = 2
GT = T // NG
GTT = GT // 128
TB = 256
NTB = GT // TB
HC = H // 128
FC = DFF // 128
LAM128 = float(np.float32(np.float64(DECAY) ** 128))
EPS = 1e-5

ENGS = ("sync", "gpsimd", "tensor", "vector", "scalar")


class Ctr:
    def __init__(self, h):
        self.h = h
        self.n = 0


class Prog:
    def __init__(self):
        self.q = {k: [] for k in ENGS}

    def add(self, eng, f):
        self.q[eng].append(f)

    def wait(self, eng, sem, val):
        assert val <= sem.n, f"wait target {val} > planned {sem.n} ({eng})"
        if val <= 0:
            return
        self.add(eng, lambda e, s=sem.h, v=val: e.wait_ge(s, v))

    def emit(self, eng, meth, *a, sem=None, k=1, **kw):
        if sem is not None:
            sem.n += k
            self.add(eng, lambda e, m=meth, a=a, kw=kw, sh=sem.h, kk=k:
                     getattr(e, m)(*a, **kw).then_inc(sh, kk))
            return sem.n
        self.add(eng, lambda e, m=meth, a=a, kw=kw: getattr(e, m)(*a, **kw))
        return None

    def flush(self, blk):
        for eng in ENGS:
            fns = self.q[eng]
            self.q[eng] = []
            if fns:
                getattr(blk, eng)(lambda e, fns=fns: [f(e) for f in fns])


def build():
    nc = bass.Bass("TRN2", target_bir_lowering=False)
    x_d = nc.dram_tensor("x", [T, H], F32, kind="ExternalInput").ap()
    wqt_d = nc.dram_tensor("wqt", [H, M], F32, kind="ExternalInput").ap()
    wkt_d = nc.dram_tensor("wkt", [H, M], F32, kind="ExternalInput").ap()
    wvt_d = nc.dram_tensor("wvt", [H, M], F32, kind="ExternalInput").ap()
    wot_d = nc.dram_tensor("wot", [M, H], F32, kind="ExternalInput").ap()
    gate_d = nc.dram_tensor("gate", [1, 1], F32, kind="ExternalInput").ap()
    m0t_d = nc.dram_tensor("m0t", [M, M], F32, kind="ExternalInput").ap()
    wr_d = nc.dram_tensor("wr", [H, NEXP], F32, kind="ExternalInput").ap()
    br_d = nc.dram_tensor("br", [1, NEXP], F32, kind="ExternalInput").ap()
    w1_d = nc.dram_tensor("w1", [NEXP, H, DFF], F32, kind="ExternalInput").ap()
    b1_d = nc.dram_tensor("b1", [NEXP, 128, FC], F32, kind="ExternalInput").ap()
    w2_d = nc.dram_tensor("w2", [NEXP, DFF, H], F32, kind="ExternalInput").ap()
    b2_d = nc.dram_tensor("b2", [NEXP, H], F32, kind="ExternalInput").ap()
    mask_d = nc.dram_tensor("mask", [128, 512], F32, kind="ExternalInput").ap()
    dq_d = nc.dram_tensor("dq", [128, 512], F32, kind="ExternalInput").ap()
    lamk_d = nc.dram_tensor("lamk", [128, 1], F32, kind="ExternalInput").ap()
    prefw_d = nc.dram_tensor("prefw", [128, 9], F32, kind="ExternalInput").ap()
    ident_d = nc.dram_tensor("ident", [128, 128], BF16, kind="ExternalInput").ap()
    y_d = nc.dram_tensor("y", [T, H], F32, kind="ExternalOutput").ap()
    x1h = nc.dram_tensor("x1h", [T, H], F32).ap()
    agin = nc.dram_tensor("agin", [M, M], F32).ap()
    agout = nc.dram_tensor("agout", [NCORE * M, M], F32).ap()

    p = Prog()
    with ExitStack() as ctx:
        E = ctx.enter_context
        sLd = Ctr(E(nc.semaphore("sLd")))
        sW = Ctr(E(nc.semaphore("sW")))
        sT = Ctr(E(nc.semaphore("sT")))
        sV = Ctr(E(nc.semaphore("sV")))
        sS = Ctr(E(nc.semaphore("sS")))
        sC = Ctr(E(nc.semaphore("sC")))
        sO = Ctr(E(nc.semaphore("sO")))
        ident = E(nc.sbuf_tensor("ident_sb", [128, 128], BF16))
        ones_col = E(nc.sbuf_tensor("ones_col", [1, 128], BF16))
        c_eps = E(nc.sbuf_tensor("c_eps", [128, 1], F32))
        c_zero = E(nc.sbuf_tensor("c_zero", [128, 1], F32))
        c_one = E(nc.sbuf_tensor("c_one", [128, 1], F32))

        with ExitStack() as ctxA:
            EA = ctxA.enter_context
            x_tok = EA(nc.sbuf_tensor("x_tok", [128, NT, H], F32))
            xln1 = EA(nc.sbuf_tensor("xln1", [128, HC, T], BF16))
            xa = EA(nc.sbuf_tensor("xa", [128, 2, H], BF16))
            stats = EA(nc.sbuf_tensor("stats", [128, 9, NT], F32))
            wq_sb = EA(nc.sbuf_tensor("wq_sb", [128, HC, M], BF16))
            wk_sb = EA(nc.sbuf_tensor("wk_sb", [128, HC, M], BF16))
            wv_sb = EA(nc.sbuf_tensor("wv_sb", [128, HC, M], BF16))
            wot_sb = EA(nc.sbuf_tensor("wot_sb", [128, 2, H], BF16))
            qT = EA(nc.sbuf_tensor("qT", [128, 2, T], BF16))
            kT = EA(nc.sbuf_tensor("kT", [128, 2, T], BF16))
            ktok = EA(nc.sbuf_tensor("ktok", [128, NT, M], BF16))
            vtok = EA(nc.sbuf_tensor("vtok", [128, NT, M], BF16))
            rT = EA(nc.sbuf_tensor("rT", [128, 2, T], BF16))
            sTb = EA(nc.sbuf_tensor("sTb", [128, B, 512], BF16))
            qs = EA(nc.sbuf_tensor("qs", [128, 2, 512], BF16))
            Ast = EA(nc.sbuf_tensor("Ast", [128, 2, M], F32))
            AstBf = EA(nc.sbuf_tensor("AstBf", [128, 2, M], BF16))
            dA = EA(nc.sbuf_tensor("dA", [128, NCH, 2, M], F32))
            Zw = EA(nc.sbuf_tensor("Zw", [128, 2, M], F32))
            Zw2 = EA(nc.sbuf_tensor("Zw2", [128, 2, M], F32))
            m0t_sb = EA(nc.sbuf_tensor("m0t_sb", [128, 2, M], F32))
            Zall = EA(nc.sbuf_tensor("Zall", [128, 2 * NCORE, M], F32))
            mask_sb = EA(nc.sbuf_tensor("mask_sb", [128, 512], F32))
            dq_sb = EA(nc.sbuf_tensor("dq_sb", [128, 512], F32))
            lamk_sb = EA(nc.sbuf_tensor("lamk_sb", [128, 1], F32))
            prefw_sb = EA(nc.sbuf_tensor("prefw_sb", [128, 9], F32))
            gate_sb = EA(nc.sbuf_tensor("gate_sb", [1, 1], F32))
            sig_s = EA(nc.sbuf_tensor("sig_s", [1, 4], F32))
            sig_bf = EA(nc.sbuf_tensor("sig_bf", [1, 1], BF16))
            sig_bc = EA(nc.sbuf_tensor("sig_bc", [128, 1], F32))
            p_tr = EA(nc.psum_tensor("p_tr", [128, 2, 1024], BF16))
            p_pr = EA(nc.psum_tensor("p_pr", [128, 1024], F32))
            p_s = EA(nc.psum_tensor("p_s", [128, 1024], F32))
            p_r = EA(nc.psum_tensor("p_r", [128, 1024], F32))

            with nc.Block("A", no_gpsimd_drain=True) as blkA:
                x_ld = []
                for k in range(4):
                    x_ld.append(p.emit("sync", "dma_start", sem=sLd, k=16,
                                       out=x_tok[:, 4 * k:4 * (k + 1), :],
                                       in_=x_d.rearrange("(i p) h -> p i h", p=128)[:, 4 * k:4 * (k + 1), :]))
                m0_ld = p.emit("sync", "dma_start", sem=sLd, k=16, out=m0t_sb[:, :, :],
                               in_=m0t_d.rearrange("(c p) m -> p c m", p=128))
                msk_ld = p.emit("sync", "dma_start", sem=sLd, k=16, out=mask_sb[:, :], in_=mask_d)
                dq_ld = p.emit("sync", "dma_start", sem=sLd, k=16, out=dq_sb[:, :], in_=dq_d)
                lamk_ld = p.emit("sync", "dma_start", sem=sLd, k=16, out=lamk_sb[:, :], in_=lamk_d)
                prefw_ld = p.emit("sync", "dma_start", sem=sLd, k=16, out=prefw_sb[:, :], in_=prefw_d)
                gate_ld = p.emit("sync", "dma_start", sem=sLd, k=16, out=gate_sb[:, :], in_=gate_d)
                id_ld = p.emit("sync", "dma_start", sem=sLd, k=16, out=ident[:, :], in_=ident_d)

                p.emit("gpsimd", "memset", c_eps[:, :], EPS, sem=sW)
                p.emit("gpsimd", "memset", c_zero[:, :], 0.0, sem=sW)
                p.emit("gpsimd", "memset", c_one[:, :], 1.0, sem=sW)
                p.emit("gpsimd", "memset", ones_col[:, :], 1.0, sem=sW)
                consts_done = sW.n
                wq_ld = p.emit("gpsimd", "dma_start", sem=sW, k=16, out=wq_sb[:, :, :],
                               in_=wqt_d.rearrange("(c p) m -> p c m", p=128))
                wk_ld = p.emit("gpsimd", "dma_start", sem=sW, k=16, out=wk_sb[:, :, :],
                               in_=wkt_d.rearrange("(c p) m -> p c m", p=128))
                wv_ld = p.emit("gpsimd", "dma_start", sem=sW, k=16, out=wv_sb[:, :, :],
                               in_=wvt_d.rearrange("(c p) m -> p c m", p=128))
                wo_ld = p.emit("gpsimd", "dma_start", sem=sW, k=16, out=wot_sb[:, :, :],
                               in_=wot_d.rearrange("(c p) h -> p c h", p=128))

                p.wait("scalar", sLd, gate_ld)
                p.wait("scalar", sW, consts_done)
                p.emit("scalar", "activation", sig_s[:, 0:1], gate_sb[:, :], AF.Exp, bias=c_zero[0:1, :], scale=-1.0)
                p.emit("scalar", "activation", sig_s[:, 1:2], sig_s[:, 0:1], AF.Identity, bias=c_one[0:1, :], scale=1.0)
                p.emit("scalar", "activation", sig_s[:, 2:3], sig_s[:, 1:2], AF.Ln, bias=c_zero[0:1, :], scale=1.0)
                p.emit("scalar", "activation", sig_s[:, 3:4], sig_s[:, 2:3], AF.Exp, bias=c_zero[0:1, :], scale=-1.0)
                p.emit("scalar", "activation", sig_bf[:, :], sig_s[:, 3:4], AF.Identity, bias=c_zero[0:1, :], scale=1.0, sem=sS)
                sigbf_done = sS.n
                p.wait("tensor", sS, sigbf_done)
                p.emit("tensor", "matmul", p_pr[:, 0:1], ones_col[:, :], sig_bf[:, :], start=True, stop=True, sem=sT)
                p.wait("vector", sT, sT.n)
                p.emit("vector", "tensor_copy", sig_bc[:, :], p_pr[:, 0:1], sem=sV)
                sigbc_done = sV.n
                p.wait("scalar", sW, wo_ld)
                p.wait("scalar", sV, sigbc_done)
                p.emit("scalar", "activation", wot_sb[:, 0, :], wot_sb[:, 0, :], AF.Identity, bias=c_zero[:, :], scale=sig_bc[:, 0:1])
                p.emit("scalar", "activation", wot_sb[:, 1, :], wot_sb[:, 1, :], AF.Identity, bias=c_zero[:, :], scale=sig_bc[:, 0:1], sem=sS)
                wotg_done = sS.n

                SUM, SQS, MEAN, EX2, VAR, MSQ, RSTD, MS, NEGMS = range(9)
                for tt in range(NT):
                    p.wait("vector", sLd, x_ld[tt // 4])
                    p.emit("vector", "tensor_reduce", stats[:, SUM, tt:tt + 1], x_tok[:, tt, :], axis=AX.X, op=OP.add, sem=sV)
                    p.wait("scalar", sLd, x_ld[tt // 4])
                    p.emit("scalar", "activation", qT[:, 0, 0:H], x_tok[:, tt, :], AF.Square,
                           bias=c_zero[:, :], scale=1.0, accum_out=stats[:, SQS, tt:tt + 1], sem=sS)
                sum_done = sV.n
                p.wait("scalar", sV, sum_done)
                p.emit("scalar", "activation", stats[:, MEAN, :], stats[:, SUM, :], AF.Identity, bias=c_zero[:, :], scale=1.0 / H)
                p.emit("scalar", "activation", stats[:, EX2, :], stats[:, SQS, :], AF.Identity, bias=c_zero[:, :], scale=1.0 / H)
                p.emit("scalar", "activation", stats[:, MSQ, :], stats[:, MEAN, :], AF.Square, bias=c_zero[:, :], scale=1.0, sem=sS)
                p.wait("vector", sS, sS.n)
                p.emit("vector", "tensor_tensor", stats[:, VAR, :], stats[:, EX2, :], stats[:, MSQ, :], OP.subtract, sem=sV)
                p.wait("scalar", sV, sV.n)
                p.emit("scalar", "activation", stats[:, RSTD, :], stats[:, VAR, :], AF.Ln, bias=c_eps[:, :], scale=1.0)
                p.emit("scalar", "activation", stats[:, RSTD, :], stats[:, RSTD, :], AF.Exp, bias=c_zero[:, :], scale=-0.5, sem=sS)
                p.wait("vector", sS, sS.n)
                p.emit("vector", "tensor_tensor", stats[:, MS, :], stats[:, MEAN, :], stats[:, RSTD, :], OP.mult, sem=sV)
                p.wait("scalar", sV, sV.n)
                p.emit("scalar", "activation", stats[:, NEGMS, :], stats[:, MS, :], AF.Identity, bias=c_zero[:, :], scale=-1.0, sem=sS)
                negms_done = sS.n

                ping_ev = [0, 0]
                aff_free = [0, 0]
                for tt in range(NT):
                    pi = tt % 2
                    p.wait("scalar", sS, negms_done)
                    p.wait("scalar", sT, aff_free[pi])
                    p.emit("scalar", "activation", xa[:, pi, :], x_tok[:, tt, :], AF.Identity,
                           bias=stats[:, NEGMS, tt:tt + 1], scale=stats[:, RSTD, tt:tt + 1], sem=sS)
                    a_n = sS.n
                    for h in range(HC):
                        bi = h % 2
                        p.wait("tensor", sS, a_n)
                        p.wait("tensor", sV, ping_ev[bi])
                        if tt == 0 and h == 0:
                            p.wait("tensor", sLd, id_ld)
                        p.emit("tensor", "transpose", p_tr[:, bi, 0:128], xa[:, pi, h * 128:(h + 1) * 128], ident[:, :], sem=sT)
                        p.wait("vector", sT, sT.n)
                        p.emit("vector", "tensor_copy", xln1[:, h, tt * 128:(tt + 1) * 128], p_tr[:, bi, 0:128], sem=sV)
                        ping_ev[bi] = sV.n
                    aff_free[pi] = sT.n
                xln1_done_v = sV.n

                proj_ping_v = [sV.n, sV.n]
                pp = 0
                for (wsb, dst) in ((wq_sb, qT), (wk_sb, kT)):
                    for mt in range(2):
                        for tb4 in range(4):
                            half = pp % 2
                            p.wait("tensor", sV, proj_ping_v[half])
                            if pp == 0:
                                p.wait("tensor", sV, xln1_done_v)
                                p.wait("tensor", sW, wv_ld)
                            for h in range(HC):
                                p.emit("tensor", "matmul", p_pr[:, half * 512:half * 512 + 512],
                                       wsb[:, h, mt * 128:(mt + 1) * 128], xln1[:, h, tb4 * 512:(tb4 + 1) * 512],
                                       start=(h == 0), stop=(h == HC - 1), sem=(sT if h == HC - 1 else None))
                            p.wait("vector", sT, sT.n)
                            p.emit("vector", "tensor_copy", dst[:, mt, tb4 * 512:(tb4 + 1) * 512],
                                   p_pr[:, half * 512:half * 512 + 512], sem=sV)
                            proj_ping_v[half] = sV.n
                            pp += 1
                qk_done_v = sV.n
                proj_ping_s = [0, 0]
                for tt in range(NT):
                    for (wsb, dst, scale_ap) in ((wk_sb, ktok, lamk_sb[:, 0:1]), (wv_sb, vtok, None)):
                        half = pp % 2
                        p.wait("tensor", sS, proj_ping_s[half])
                        p.wait("tensor", sV, proj_ping_v[half])
                        for h in range(HC):
                            p.emit("tensor", "matmul", p_pr[:, half * 512:half * 512 + 256],
                                   xln1[:, h, tt * 128:(tt + 1) * 128], wsb[:, h, :],
                                   start=(h == 0), stop=(h == HC - 1), sem=(sT if h == HC - 1 else None))
                        p.wait("scalar", sT, sT.n)
                        if tt == 0 and scale_ap is not None:
                            p.wait("scalar", sLd, lamk_ld)
                        p.emit("scalar", "activation", dst[:, tt, :], p_pr[:, half * 512:half * 512 + 256],
                               AF.Identity, bias=c_zero[:, :],
                               scale=(scale_ap if scale_ap is not None else 1.0), sem=sS)
                        proj_ping_s[half] = sS.n
                        pp += 1
                kv_done_s = sS.n

                p.wait("tensor", sS, kv_done_s)
                da_ping = [0, 0]
                for cl in range(NCH):
                    for mk in range(2):
                        reg = (0, 256) if mk == 0 else (512, 768)
                        p.wait("tensor", sV, da_ping[mk])
                        for bp in range(B):
                            tt = bp * NCH + cl
                            p.emit("tensor", "matmul", p_pr[:, reg[0]:reg[1]],
                                   ktok[:, tt, mk * 128:(mk + 1) * 128], vtok[:, tt, :],
                                   start=(bp == 0), stop=(bp == B - 1), sem=(sT if bp == B - 1 else None))
                        p.wait("vector", sT, sT.n)
                        p.emit("vector", "tensor_copy", dA[:, cl, mk, :], p_pr[:, reg[0]:reg[1]], sem=sV)
                        da_ping[mk] = sV.n
                p.emit("vector", "tensor_copy", Zw[:, :, :], dA[:, 0, :, :], sem=sV)
                for cl in range(1, NCH):
                    p.wait("scalar", sV, sV.n)
                    p.emit("scalar", "activation", Zw2[:, :, :], Zw[:, :, :], AF.Identity, bias=c_zero[:, :], scale=LAM128, sem=sS)
                    p.wait("vector", sS, sS.n)
                    p.emit("vector", "tensor_tensor", Zw[:, :, :], Zw2[:, :, :], dA[:, cl, :, :], OP.add, sem=sV)
                z_done_v = sV.n

                p.wait("gpsimd", sV, z_done_v)
                p.emit("gpsimd", "dma_start", sem=sW, k=16, out=agin.rearrange("(c p) m -> p c m", p=128), in_=Zw[:, :, :])
                p.wait("gpsimd", sW, sW.n)
                p.emit("gpsimd", "collective_compute", "AllGather", OP.bypass,
                       replica_groups=[list(range(NCORE))], ins=[agin.opt()], outs=[agout.opt()], sem=sC)
                p.wait("gpsimd", sC, sC.n)
                zall_ld = p.emit("gpsimd", "dma_start", sem=sW, k=16, out=Zall[:, :, :],
                                 in_=agout.rearrange("(c p) m -> p c m", p=128))

                p.wait("scalar", sW, zall_ld)
                p.wait("scalar", sLd, prefw_ld)
                p.emit("scalar", "activation", Ast[:, :, :], m0t_sb[:, :, :], AF.Identity, bias=c_zero[:, :],
                       scale=prefw_sb[:, 8:9], sem=sS)
                for k in range(NCORE):
                    p.emit("scalar", "activation", Zw2[:, :, :], Zall[:, 2 * k:2 * k + 2, :], AF.Identity,
                           bias=c_zero[:, :], scale=prefw_sb[:, k:k + 1], sem=sS)
                    p.wait("vector", sS, sS.n)
                    p.emit("vector", "tensor_tensor", Ast[:, :, :], Zw2[:, :, :], Ast[:, :, :], OP.add, sem=sV)
                    p.wait("scalar", sV, sV.n)
                p.emit("vector", "tensor_copy", AstBf[:, :, :], Ast[:, :, :], sem=sV)
                astbf_done = sV.n

                s_ping = [sV.n, sV.n]
                for cl in range(NCH):
                    stb_ev = {}
                    for bp in range(B):
                        half = bp % 2
                        p.wait("tensor", sV, s_ping[half])
                        if cl == 0 and bp == 0:
                            p.wait("tensor", sV, qk_done_v)
                        for mk in range(2):
                            p.emit("tensor", "matmul", p_s[:, half * 512:half * 512 + 512],
                                   kT[:, mk, bp * SC + cl * 128: bp * SC + cl * 128 + 128],
                                   qT[:, mk, :].rearrange("p (b s) -> p b s", b=B)[:, :, cl * 128:(cl + 1) * 128],
                                   start=(mk == 0), stop=(mk == 1), sem=(sT if mk == 1 else None))
                        p.wait("vector", sT, sT.n)
                        if cl == 0 and bp == 0:
                            p.wait("vector", sLd, msk_ld)
                            p.wait("vector", sLd, dq_ld)
                        p.emit("vector", "tensor_tensor", sTb[:, bp, :],
                               p_s[:, half * 512:half * 512 + 512], mask_sb[:, :], OP.mult, sem=sV)
                        stb_ev[bp] = sV.n
                        s_ping[half] = sV.n
                    for mk in range(2):
                        p.emit("vector", "tensor_tensor",
                               qs[:, mk, :].rearrange("p (b s) -> p b s", b=B),
                               qT[:, mk, :].rearrange("p (b s) -> p b s", b=B)[:, :, cl * 128:(cl + 1) * 128],
                               dq_sb.rearrange("p (b s) -> p b s", b=B), OP.mult, sem=sV)
                    qs_done = sV.n
                    for mv in range(2):
                        p.wait("tensor", sV, qs_done)
                        p.wait("tensor", sV, astbf_done)
                        p.emit("tensor", "matmul", p_r[:, mv * 512:mv * 512 + 512],
                               AstBf[:, 0, mv * 128:(mv + 1) * 128], qs[:, 0, :], start=True, stop=False)
                        p.emit("tensor", "matmul", p_r[:, mv * 512:mv * 512 + 512],
                               AstBf[:, 1, mv * 128:(mv + 1) * 128], qs[:, 1, :], start=False, stop=False)
                        for bp in range(B):
                            tt = bp * NCH + cl
                            p.wait("tensor", sV, stb_ev[bp])
                            p.emit("tensor", "matmul", p_r[:, mv * 512:mv * 512 + 512],
                                   vtok[:, tt, mv * 128:(mv + 1) * 128], sTb[:, bp, :],
                                   start=False, stop=(bp == B - 1), sem=(sT if bp == B - 1 else None))
                        p.wait("vector", sT, sT.n)
                        p.emit("vector", "tensor_copy",
                               rT[:, mv, :].rearrange("p (b s) -> p b s", b=B)[:, :, cl * 128:(cl + 1) * 128],
                               p_r[:, mv * 512:mv * 512 + 512].rearrange("p (b s) -> p b s", b=B), sem=sV)
                    if cl < NCH - 1:
                        p.wait("scalar", sT, sT.n)
                        p.emit("scalar", "activation", Zw2[:, :, :], Ast[:, :, :], AF.Identity, bias=c_zero[:, :],
                               scale=LAM128, sem=sS)
                        p.wait("vector", sS, sS.n)
                        p.emit("vector", "tensor_tensor", Ast[:, :, :], Zw2[:, :, :], dA[:, cl, :, :], OP.add, sem=sV)
                        p.emit("vector", "tensor_copy", AstBf[:, :, :], Ast[:, :, :], sem=sV)
                        astbf_done = sV.n
                r_done_v = sV.n

                x1_add = {}
                for tt in range(NT):
                    if tt == 0:
                        p.wait("tensor", sV, r_done_v)
                        p.wait("tensor", sS, wotg_done)
                    else:
                        p.wait("tensor", sV, x1_add[tt - 1])
                    for half in range(2):
                        for mt in range(2):
                            p.emit("tensor", "matmul", p_pr[:, half * 512:half * 512 + 512],
                                   rT[:, mt, tt * 128:(tt + 1) * 128], wot_sb[:, mt, half * 512:half * 512 + 512],
                                   start=(mt == 0), stop=(mt == 1),
                                   sem=(sT if (half == 1 and mt == 1) else None))
                    p.wait("vector", sT, sT.n)
                    p.emit("vector", "tensor_tensor", x_tok[:, tt, :], p_pr[:, 0:1024], x_tok[:, tt, :], OP.add, sem=sV)
                    x1_add[tt] = sV.n

                for k in range(4):
                    p.wait("sync", sV, x1_add[4 * k + 3])
                    p.emit("sync", "dma_start", sem=sO, k=16,
                           out=x1h.rearrange("(i p) h -> p i h", p=128)[:, 4 * k:4 * (k + 1), :],
                           in_=x_tok[:, 4 * k:4 * (k + 1), :])
                x1h_done = sO.n
                p.wait("sync", sO, x1h_done)
                p.flush(blkA)

        with ExitStack() as ctxB:
            EB = ctxB.enter_context
            y_buf = EB(nc.sbuf_tensor("y_buf", [128, GTT, H], F32))
            xln2 = EB(nc.sbuf_tensor("xln2", [128, HC, GT], BF16))
            xa2 = EB(nc.sbuf_tensor("xa2", [128, 2, H], BF16))
            stats2 = EB(nc.sbuf_tensor("stats2", [128, 9, GTT], F32))
            w1b = EB(nc.sbuf_tensor("w1b", [128, HC, DFF], BF16))
            w2b = EB(nc.sbuf_tensor("w2b", [128, FC, H], BF16))
            h1 = EB(nc.sbuf_tensor("h1", [128, FC, TB], BF16))
            b1c = EB(nc.sbuf_tensor("b1c", [128, FC], F32))
            b2r = EB(nc.sbuf_tensor("b2r", [1, H], BF16))
            wr_sb = EB(nc.sbuf_tensor("wr_sb", [128, HC, NEXP], BF16))
            br_r = EB(nc.sbuf_tensor("br_r", [1, NEXP], BF16))
            probs = EB(nc.sbuf_tensor("probs", [128, GTT, NEXP], F32))
            expt = EB(nc.sbuf_tensor("expt", [128, NEXP], F32))
            rstat = EB(nc.sbuf_tensor("rstat", [128, 5, GTT], F32))
            mcol = EB(nc.sbuf_tensor("mcol", [128, GTT, NEXP], F32))
            indc = EB(nc.sbuf_tensor("indc", [128, NEXP], F32))
            tmp_epi = EB(nc.sbuf_tensor("tmp_epi", [128, H], F32))
            p_yA = EB(nc.psum_tensor("p_yA", [128, 1024], F32))
            p_yB = EB(nc.psum_tensor("p_yB", [128, 1024], F32))
            p_h = EB(nc.psum_tensor("p_h", [128, 1024], F32))
            p_rt = EB(nc.psum_tensor("p_rt", [128, 512], F32))
            p_tr2 = EB(nc.psum_tensor("p_tr2", [128, 1024], BF16))

            with nc.Block("Bf") as blkB:
                SUM, SQS, MEAN, EX2, VAR, MSQ, RSTD, MS, NEGMS = range(9)
                MX, SE, LNSE, RINV, GMX = range(5)

                wr_ld = p.emit("gpsimd", "dma_start", sem=sW, k=16, out=wr_sb[:, :, :],
                               in_=wr_d.rearrange("(c p) e -> p c e", p=128))
                br_ld = p.emit("gpsimd", "dma_start", sem=sW, k=16, out=br_r[:, :], in_=br_d)

                w1_thr = {}
                w2_thr = {}
                b_thr = {}

                def emit_weight_loads(g, e, w1_free, w2_free, b_free):
                    for qp in range(4):
                        if w1_free is not None:
                            p.wait("gpsimd", sT, w1_free[qp])
                        w1_thr[(g, e, qp)] = p.emit(
                            "gpsimd", "dma_start", sem=sW, k=16,
                            out=w1b[:, :, qp * 1024:(qp + 1) * 1024],
                            in_=w1_d[e].rearrange("(c p) f -> p c f", p=128)[:, :, qp * 1024:(qp + 1) * 1024])
                    for qp in range(4):
                        if w2_free is not None:
                            p.wait("gpsimd", sT, w2_free[qp])
                        w2_thr[(g, e, qp)] = p.emit(
                            "gpsimd", "dma_start", sem=sW, k=16,
                            out=w2b[:, 8 * qp:8 * (qp + 1), :],
                            in_=w2_d[e].rearrange("(c p) h -> p c h", p=128)[:, 8 * qp:8 * (qp + 1), :])
                    if b_free is not None:
                        p.wait("gpsimd", sS, b_free[0])
                        p.wait("gpsimd", sV, b_free[1])
                        p.wait("gpsimd", sT, b_free[2])
                    b_thr[(g, e)] = (
                        p.emit("gpsimd", "dma_start", sem=sW, k=16, out=b1c[:, :], in_=b1_d[e]),
                        p.emit("gpsimd", "dma_start", sem=sW, k=16, out=b2r[:, :], in_=b2_d[e:e + 1, :]),
                    )

                emit_weight_loads(0, 0, None, None, None)

                p.wait("sync", sO, x1h_done)
                yld = {0: p.emit("sync", "dma_start", sem=sLd, k=16, out=y_buf[:, :, :],
                                 in_=x1h.rearrange("(i p) h -> p i h", p=128)[:, 0:GTT, :])}

                h_free = [(sS, 0), (sV, 0)]
                last_epi = {0: 0, 1: 0}
                prev_epi_v = 0
                y_store = {}
                for g in range(NG):
                    for tt in range(GTT):
                        if tt == 0:
                            p.wait("vector", sLd, yld[g])
                            p.wait("scalar", sLd, yld[g])
                        p.emit("vector", "tensor_reduce", stats2[:, SUM, tt:tt + 1], y_buf[:, tt, :], axis=AX.X, op=OP.add, sem=sV)
                        p.emit("scalar", "activation", xa2[:, 0, :], y_buf[:, tt, :], AF.Square,
                               bias=c_zero[:, :], scale=1.0, accum_out=stats2[:, SQS, tt:tt + 1], sem=sS)
                    p.wait("scalar", sV, sV.n)
                    p.emit("scalar", "activation", stats2[:, MEAN, :], stats2[:, SUM, :], AF.Identity, bias=c_zero[:, :], scale=1.0 / H)
                    p.emit("scalar", "activation", stats2[:, EX2, :], stats2[:, SQS, :], AF.Identity, bias=c_zero[:, :], scale=1.0 / H)
                    p.emit("scalar", "activation", stats2[:, MSQ, :], stats2[:, MEAN, :], AF.Square, bias=c_zero[:, :], scale=1.0, sem=sS)
                    p.wait("vector", sS, sS.n)
                    p.emit("vector", "tensor_tensor", stats2[:, VAR, :], stats2[:, EX2, :], stats2[:, MSQ, :], OP.subtract, sem=sV)
                    p.wait("scalar", sV, sV.n)
                    p.emit("scalar", "activation", stats2[:, RSTD, :], stats2[:, VAR, :], AF.Ln, bias=c_eps[:, :], scale=1.0)
                    p.emit("scalar", "activation", stats2[:, RSTD, :], stats2[:, RSTD, :], AF.Exp, bias=c_zero[:, :], scale=-0.5, sem=sS)
                    p.wait("vector", sS, sS.n)
                    p.emit("vector", "tensor_tensor", stats2[:, MS, :], stats2[:, MEAN, :], stats2[:, RSTD, :], OP.mult, sem=sV)
                    p.wait("scalar", sV, sV.n)
                    p.emit("scalar", "activation", stats2[:, NEGMS, :], stats2[:, MS, :], AF.Identity, bias=c_zero[:, :], scale=-1.0, sem=sS)
                    negms2 = sS.n

                    tr_free = sV.n
                    aff2_free = 0
                    for tt in range(GTT):
                        p.wait("scalar", sS, negms2)
                        p.wait("scalar", sT, aff2_free)
                        p.emit("scalar", "activation", xa2[:, 1, :], y_buf[:, tt, :], AF.Identity,
                               bias=stats2[:, NEGMS, tt:tt + 1], scale=stats2[:, RSTD, tt:tt + 1], sem=sS)
                        a_n = sS.n
                        for h in range(HC):
                            p.wait("tensor", sS, a_n)
                            p.wait("tensor", sV, tr_free)
                            p.emit("tensor", "transpose", p_tr2[:, 0:128], xa2[:, 1, h * 128:(h + 1) * 128], ident[:, :], sem=sT)
                            p.wait("vector", sT, sT.n)
                            p.emit("vector", "tensor_copy", xln2[:, h, tt * 128:(tt + 1) * 128], p_tr2[:, 0:128], sem=sV)
                            tr_free = sV.n
                        aff2_free = sT.n
                    xln2_done = sV.n

                    rt_free = 0
                    for tt in range(GTT):
                        if tt == 0:
                            p.wait("tensor", sV, xln2_done)
                            p.wait("tensor", sW, br_ld)
                        p.wait("tensor", sS, rt_free)
                        p.emit("tensor", "matmul", p_rt[:, 0:NEXP], ones_col[:, :], br_r[:, :], start=True, stop=False)
                        for h in range(HC):
                            p.emit("tensor", "matmul", p_rt[:, 0:NEXP], xln2[:, h, tt * 128:(tt + 1) * 128],
                                   wr_sb[:, h, :], start=False, stop=(h == HC - 1), sem=(sT if h == HC - 1 else None))
                        p.wait("vector", sT, sT.n)
                        p.emit("vector", "tensor_reduce", rstat[:, MX, tt:tt + 1], p_rt[:, 0:NEXP], axis=AX.X, op=OP.max, negate=True, sem=sV)
                        p.wait("scalar", sV, sV.n)
                        p.emit("scalar", "activation", expt[:, :], p_rt[:, 0:NEXP], AF.Exp,
                               bias=rstat[:, MX, tt:tt + 1], scale=1.0, accum_out=rstat[:, SE, tt:tt + 1], sem=sS)
                        rt_free = sS.n
                        p.emit("scalar", "activation", rstat[:, LNSE, tt:tt + 1], rstat[:, SE, tt:tt + 1], AF.Ln, bias=c_zero[:, :], scale=1.0)
                        p.emit("scalar", "activation", rstat[:, RINV, tt:tt + 1], rstat[:, LNSE, tt:tt + 1], AF.Exp, bias=c_zero[:, :], scale=-1.0)
                        p.emit("scalar", "activation", probs[:, tt, :], expt[:, :], AF.Identity, bias=c_zero[:, :],
                               scale=rstat[:, RINV, tt:tt + 1], sem=sS)
                        p.wait("vector", sS, sS.n)
                        p.emit("vector", "tensor_reduce", rstat[:, GMX, tt:tt + 1], probs[:, tt, :], axis=AX.X, op=OP.max, sem=sV)
                    for tt in range(GTT):
                        for e in range(NEXP):
                            p.emit("vector", "tensor_tensor", indc[:, e:e + 1], probs[:, tt, e:e + 1], rstat[:, GMX, tt:tt + 1], OP.is_ge)
                        for e in range(NEXP):
                            p.emit("vector", "tensor_tensor", mcol[:, tt, e:e + 1], indc[:, e:e + 1], rstat[:, GMX, tt:tt + 1], OP.mult, sem=sV)
                    m_done = sV.n

                    for e in range(NEXP):
                        w1_free = [0, 0, 0, 0]
                        w2_free = [0, 0, 0, 0]
                        fgrp = 0
                        for tb in range(NTB):
                            for f in range(FC):
                                hb = fgrp % 2
                                reg = (0, 256) if hb == 0 else (512, 768)
                                if tb == 0 and f % 8 == 0:
                                    p.wait("tensor", sW, w1_thr[(g, e, f // 8)])
                                if tb == 0 and f == 0 and e == 0:
                                    p.wait("tensor", sV, xln2_done)
                                if tb == 0 and f == 0:
                                    p.wait("tensor", sW, b_thr[(g, e)][0])
                                semref, cnt = h_free[hb]
                                p.wait("tensor", semref, cnt)
                                for h in range(HC):
                                    p.emit("tensor", "matmul", p_h[:, reg[0]:reg[1]],
                                           w1b[:, h, f * 128:(f + 1) * 128],
                                           xln2[:, h, tb * TB:(tb + 1) * TB],
                                           start=(h == 0), stop=(h == HC - 1), sem=(sT if h == HC - 1 else None))
                                g_n = sT.n
                                if tb == NTB - 1 and f % 8 == 7:
                                    w1_free[f // 8] = g_n
                                if hb == 0:
                                    p.wait("scalar", sT, g_n)
                                    p.emit("scalar", "activation", h1[:, f, :], p_h[:, reg[0]:reg[1]], AF.Relu,
                                           bias=b1c[:, f:f + 1], scale=1.0, sem=sS)
                                    h_free[0] = (sS, sS.n)
                                else:
                                    p.wait("vector", sT, g_n)
                                    p.emit("vector", "tensor_scalar", h1[:, f, :], p_h[:, reg[0]:reg[1]],
                                           b1c[:, f:f + 1], 0.0, OP.add, OP.max, sem=sV)
                                    h_free[1] = (sV, sV.n)
                                fgrp += 1
                            h1_s = sS.n
                            h1_v = sV.n
                            last_expert_tb = (tb == NTB - 1)
                            for sub in range(2):
                                tt = 2 * tb + sub
                                yp = p_yA if (tt % 2 == 0) else p_yB
                                p.wait("tensor", sS, last_epi[tt % 2])
                                p.wait("tensor", sS, h1_s)
                                p.wait("tensor", sV, h1_v)
                                p.wait("tensor", sW, b_thr[(g, e)][1])
                                p.emit("tensor", "matmul", yp[:, 0:512], ones_col[:, :], b2r[:, 0:512], start=True, stop=False)
                                p.emit("tensor", "matmul", yp[:, 512:1024], ones_col[:, :], b2r[:, 512:1024], start=True, stop=False)
                                for f in range(FC):
                                    if tb == 0 and sub == 0 and f % 8 == 0:
                                        p.wait("tensor", sW, w2_thr[(g, e, f // 8)])
                                    p.emit("tensor", "matmul", yp[:, 0:512], h1[:, f, sub * 128:(sub + 1) * 128],
                                           w2b[:, f, 0:512], start=False, stop=False)
                                    lastf = (f == FC - 1)
                                    relq = (last_expert_tb and sub == 1 and f % 8 == 7)
                                    p.emit("tensor", "matmul", yp[:, 512:1024], h1[:, f, sub * 128:(sub + 1) * 128],
                                           w2b[:, f, 512:1024], start=False, stop=lastf,
                                           sem=(sT if (lastf or relq) else None))
                                    if relq:
                                        w2_free[f // 8] = sT.n
                                mm2_n = sT.n
                                p.wait("scalar", sT, mm2_n)
                                if e == 0 and tt == 0:
                                    p.wait("scalar", sV, m_done)
                                p.wait("scalar", sV, prev_epi_v)
                                p.emit("scalar", "activation", tmp_epi[:, :], yp[:, 0:1024], AF.Identity,
                                       bias=c_zero[:, :], scale=mcol[:, tt, e:e + 1], sem=sS)
                                last_epi[tt % 2] = sS.n
                                p.wait("vector", sS, sS.n)
                                p.emit("vector", "tensor_tensor", y_buf[:, tt, :], tmp_epi[:, :], y_buf[:, tt, :], OP.add, sem=sV)
                                prev_epi_v = sV.n
                        b_free = (sS.n, sV.n, sT.n)
                        if e < NEXP - 1:
                            emit_weight_loads(g, e + 1, w1_free, w2_free, b_free)
                        elif g == 0:
                            emit_weight_loads(1, 0, w1_free, w2_free, b_free)
                    last_add = sV.n
                    p.wait("sync", sV, last_add)
                    y_store[g] = p.emit("sync", "dma_start", sem=sO, k=16,
                                        out=y_d.rearrange("(i p) h -> p i h", p=128)[:, g * GTT:(g + 1) * GTT, :],
                                        in_=y_buf[:, :, :])
                    if g + 1 < NG:
                        p.wait("sync", sO, y_store[g])
                        yld[g + 1] = p.emit("sync", "dma_start", sem=sLd, k=16, out=y_buf[:, :, :],
                                            in_=x1h.rearrange("(i p) h -> p i h", p=128)[:, (g + 1) * GTT:(g + 2) * GTT, :])
                p.wait("sync", sO, sO.n)
                p.flush(blkB)
    return nc


def _consts():
    lam = np.float64(DECAY)
    jl = np.arange(128)
    tl = np.arange(128)
    mk = np.where(jl[:, None] < tl[None, :], lam ** (tl[None, :] - 1 - jl[:, None]), 0.0) / B
    mask = np.concatenate([mk] * B, axis=1).astype(np.float32)
    dq = np.concatenate([np.broadcast_to(lam ** tl, (128, 128))] * B, axis=1).astype(np.float32)
    lamk = (lam ** (127 - jl) / B).astype(np.float32).reshape(128, 1)
    ident = np.eye(128, dtype=ml_dtypes.bfloat16)
    return mask, np.ascontiguousarray(dq), lamk, ident


_NC_CACHE = {}
LAST_RES = None


def kernel(L0_attn_x, L0_attn_Wq, L0_attn_Wk, L0_attn_Wv, L0_attn_Wo, L0_attn_gate, L0_attn_M0,
           L0_ffn_Wr, L0_ffn_br, L0_ffn_W1, L0_ffn_b1, L0_ffn_W2, L0_ffn_b2):
    x = np.asarray(L0_attn_x, dtype=np.float32)
    wqt = np.ascontiguousarray(np.asarray(L0_attn_Wq, np.float32).T)
    wkt = np.ascontiguousarray(np.asarray(L0_attn_Wk, np.float32).T)
    wvt = np.ascontiguousarray(np.asarray(L0_attn_Wv, np.float32).T)
    wot = np.ascontiguousarray(np.asarray(L0_attn_Wo, np.float32).T)
    gate = np.asarray(L0_attn_gate, np.float32).reshape(1, 1)
    m0t = np.ascontiguousarray(np.asarray(L0_attn_M0, np.float32).T)
    wr = np.ascontiguousarray(np.asarray(L0_ffn_Wr, np.float32))
    br = np.asarray(L0_ffn_br, np.float32).reshape(1, NEXP)
    w1 = np.ascontiguousarray(np.asarray(L0_ffn_W1, np.float32))
    b1 = np.ascontiguousarray(np.asarray(L0_ffn_b1, np.float32).reshape(NEXP, FC, 128).transpose(0, 2, 1))
    w2 = np.ascontiguousarray(np.asarray(L0_ffn_W2, np.float32))
    b2 = np.ascontiguousarray(np.asarray(L0_ffn_b2, np.float32))
    mask, dq, lamk, ident = _consts()

    if "nc" not in _NC_CACHE:
        _NC_CACHE["nc"] = build()
    nc = _NC_CACHE["nc"]

    lam = np.float64(DECAY)
    in_maps = []
    for c in range(NCORE):
        xs = np.ascontiguousarray(x[:, c * SC:(c + 1) * SC, :]).reshape(T, H)
        w = np.zeros(9, np.float64)
        for k in range(c):
            w[k] = lam ** (SC * (c - 1 - k))
        w[8] = lam ** (SC * c)
        prefw = np.ascontiguousarray(np.tile(w.astype(np.float32)[None, :], (128, 1)))
        in_maps.append({
            "x": xs, "wqt": wqt, "wkt": wkt, "wvt": wvt, "wot": wot, "gate": gate, "m0t": m0t,
            "wr": wr, "br": br, "w1": w1, "b1": b1, "w2": w2, "b2": b2,
            "mask": mask, "dq": dq, "lamk": lamk, "prefw": prefw, "ident": ident,
        })
    res = run_bass_kernel_spmd(nc, in_maps, core_ids=list(range(NCORE)))
    global LAST_RES
    LAST_RES = res
    outs = [np.asarray(res.results[c]["y"], dtype=np.float32).reshape(B, SC, H) for c in range(NCORE)]
    return np.concatenate(outs, axis=1)


# version 02
# speedup vs baseline: 1.0951x; 1.0951x over previous
import numpy as np
import ml_dtypes
from contextlib import ExitStack

import concourse.bass as bass
import concourse.mybir as mybir
from concourse.alu_op_type import AluOpType
from concourse.bass_utils import run_bass_kernel_spmd

F32 = mybir.dt.float32
BF16 = mybir.dt.bfloat16
AF = mybir.ActivationFunctionType
AX = mybir.AxisListType
OP = AluOpType

NCORE = 8
B = 4
S = 4096
H = 1024
M = 256
NEXP = 8
DFF = 4096
DECAY = 0.99
SC = S // NCORE
T = B * SC
C = 128
NCH = SC // C
NT = T // 128
TB = 512
NTB = T // TB
HC = H // 128
FC = DFF // 128
FH = FC // 2
LAM128 = float(np.float32(np.float64(DECAY) ** 128))
EPS = 1e-5

ENGS = ("sync", "gpsimd", "tensor", "vector", "scalar")


class Ctr:
    def __init__(self, h):
        self.h = h
        self.n = 0


class Prog:
    def __init__(self):
        self.q = {k: [] for k in ENGS}

    def add(self, eng, f):
        self.q[eng].append(f)

    def wait(self, eng, sem, val):
        assert val <= sem.n, f"wait target {val} > planned {sem.n} ({eng})"
        if val <= 0:
            return
        self.add(eng, lambda e, s=sem.h, v=val: e.wait_ge(s, v))

    def emit(self, eng, meth, *a, sem=None, k=1, **kw):
        if sem is not None:
            sem.n += k
            self.add(eng, lambda e, m=meth, a=a, kw=kw, sh=sem.h, kk=k:
                     getattr(e, m)(*a, **kw).then_inc(sh, kk))
            return sem.n
        self.add(eng, lambda e, m=meth, a=a, kw=kw: getattr(e, m)(*a, **kw))
        return None

    def flush(self, blk):
        for eng in ENGS:
            fns = self.q[eng]
            self.q[eng] = []
            if fns:
                getattr(blk, eng)(lambda e, fns=fns: [f(e) for f in fns])


def build():
    nc = bass.Bass("TRN2", target_bir_lowering=False)
    x_d = nc.dram_tensor("x", [T, H], F32, kind="ExternalInput").ap()
    wqt_d = nc.dram_tensor("wqt", [H, M], F32, kind="ExternalInput").ap()
    wkt_d = nc.dram_tensor("wkt", [H, M], F32, kind="ExternalInput").ap()
    wvt_d = nc.dram_tensor("wvt", [H, M], F32, kind="ExternalInput").ap()
    wot_d = nc.dram_tensor("wot", [M, H], F32, kind="ExternalInput").ap()
    gate_d = nc.dram_tensor("gate", [1, 1], F32, kind="ExternalInput").ap()
    m0t_d = nc.dram_tensor("m0t", [M, M], F32, kind="ExternalInput").ap()
    wr_d = nc.dram_tensor("wr", [H, NEXP], F32, kind="ExternalInput").ap()
    br_d = nc.dram_tensor("br", [1, NEXP], F32, kind="ExternalInput").ap()
    w1_d = nc.dram_tensor("w1", [NEXP, H, DFF], F32, kind="ExternalInput").ap()
    b1_d = nc.dram_tensor("b1", [NEXP, 128, FC], F32, kind="ExternalInput").ap()
    w2_d = nc.dram_tensor("w2", [NEXP, DFF, H], F32, kind="ExternalInput").ap()
    b2_d = nc.dram_tensor("b2", [NEXP, H], F32, kind="ExternalInput").ap()
    mask_d = nc.dram_tensor("mask", [128, 512], F32, kind="ExternalInput").ap()
    dq_d = nc.dram_tensor("dq", [128, 512], F32, kind="ExternalInput").ap()
    lamk_d = nc.dram_tensor("lamk", [128, 1], F32, kind="ExternalInput").ap()
    prefw_d = nc.dram_tensor("prefw", [128, 9], F32, kind="ExternalInput").ap()
    ident_d = nc.dram_tensor("ident", [128, 128], BF16, kind="ExternalInput").ap()
    y_d = nc.dram_tensor("y", [T, H], F32, kind="ExternalOutput").ap()
    agin = nc.dram_tensor("agin", [M, M], F32).ap()
    agout = nc.dram_tensor("agout", [NCORE * M, M], F32).ap()

    p = Prog()
    with ExitStack() as ctx:
        E = ctx.enter_context
        sLd = Ctr(E(nc.semaphore("sLd")))
        sW = Ctr(E(nc.semaphore("sW")))
        sT = Ctr(E(nc.semaphore("sT")))
        sV = Ctr(E(nc.semaphore("sV")))
        sS = Ctr(E(nc.semaphore("sS")))
        sC = Ctr(E(nc.semaphore("sC")))
        sO = Ctr(E(nc.semaphore("sO")))
        x_tok = E(nc.sbuf_tensor("x_tok", [128, NT, H], F32))
        ident = E(nc.sbuf_tensor("ident_sb", [128, 128], BF16))
        ones_col = E(nc.sbuf_tensor("ones_col", [1, 128], BF16))
        c_eps = E(nc.sbuf_tensor("c_eps", [128, 1], F32))
        c_zero = E(nc.sbuf_tensor("c_zero", [128, 1], F32))
        c_one = E(nc.sbuf_tensor("c_one", [128, 1], F32))

        with ExitStack() as ctxA:
            EA = ctxA.enter_context
            xln1 = EA(nc.sbuf_tensor("xln1", [128, HC, T], BF16))
            xa = EA(nc.sbuf_tensor("xa", [128, 2, H], BF16))
            stats = EA(nc.sbuf_tensor("stats", [128, 9, NT], F32))
            wq_sb = EA(nc.sbuf_tensor("wq_sb", [128, HC, M], BF16))
            wk_sb = EA(nc.sbuf_tensor("wk_sb", [128, HC, M], BF16))
            wv_sb = EA(nc.sbuf_tensor("wv_sb", [128, HC, M], BF16))
            wot_sb = EA(nc.sbuf_tensor("wot_sb", [128, 2, H], BF16))
            qT = EA(nc.sbuf_tensor("qT", [128, 2, T], BF16))
            kT = EA(nc.sbuf_tensor("kT", [128, 2, T], BF16))
            ktok = EA(nc.sbuf_tensor("ktok", [128, NT, M], BF16))
            vtok = EA(nc.sbuf_tensor("vtok", [128, NT, M], BF16))
            rT = EA(nc.sbuf_tensor("rT", [128, 2, T], BF16))
            sTb = EA(nc.sbuf_tensor("sTb", [128, B, 512], BF16))
            qs = EA(nc.sbuf_tensor("qs", [128, 2, 512], BF16))
            Ast = EA(nc.sbuf_tensor("Ast", [128, 2, M], F32))
            AstBf = EA(nc.sbuf_tensor("AstBf", [128, 2, M], BF16))
            dA = EA(nc.sbuf_tensor("dA", [128, NCH, 2, M], F32))
            Zw = EA(nc.sbuf_tensor("Zw", [128, 2, M], F32))
            Zw2 = EA(nc.sbuf_tensor("Zw2", [128, 2, M], F32))
            m0t_sb = EA(nc.sbuf_tensor("m0t_sb", [128, 2, M], F32))
            Zall = EA(nc.sbuf_tensor("Zall", [128, 2 * NCORE, M], F32))
            mask_sb = EA(nc.sbuf_tensor("mask_sb", [128, 512], F32))
            dq_sb = EA(nc.sbuf_tensor("dq_sb", [128, 512], F32))
            lamk_sb = EA(nc.sbuf_tensor("lamk_sb", [128, 1], F32))
            prefw_sb = EA(nc.sbuf_tensor("prefw_sb", [128, 9], F32))
            gate_sb = EA(nc.sbuf_tensor("gate_sb", [1, 1], F32))
            sig_s = EA(nc.sbuf_tensor("sig_s", [1, 4], F32))
            sig_bf = EA(nc.sbuf_tensor("sig_bf", [1, 1], BF16))
            sig_bc = EA(nc.sbuf_tensor("sig_bc", [128, 1], F32))
            p_tr = EA(nc.psum_tensor("p_tr", [128, 2, 1024], BF16))
            p_pr = EA(nc.psum_tensor("p_pr", [128, 1024], F32))
            p_s = EA(nc.psum_tensor("p_s", [128, 1024], F32))
            p_r = EA(nc.psum_tensor("p_r", [128, 1024], F32))

            with nc.Block("A", no_gpsimd_drain=True) as blkA:
                x_ld = []
                for k in range(4):
                    x_ld.append(p.emit("sync", "dma_start", sem=sLd, k=16,
                                       out=x_tok[:, 4 * k:4 * (k + 1), :],
                                       in_=x_d.rearrange("(i p) h -> p i h", p=128)[:, 4 * k:4 * (k + 1), :]))
                m0_ld = p.emit("sync", "dma_start", sem=sLd, k=16, out=m0t_sb[:, :, :],
                               in_=m0t_d.rearrange("(c p) m -> p c m", p=128))
                msk_ld = p.emit("sync", "dma_start", sem=sLd, k=16, out=mask_sb[:, :], in_=mask_d)
                dq_ld = p.emit("sync", "dma_start", sem=sLd, k=16, out=dq_sb[:, :], in_=dq_d)
                lamk_ld = p.emit("sync", "dma_start", sem=sLd, k=16, out=lamk_sb[:, :], in_=lamk_d)
                prefw_ld = p.emit("sync", "dma_start", sem=sLd, k=16, out=prefw_sb[:, :], in_=prefw_d)
                gate_ld = p.emit("sync", "dma_start", sem=sLd, k=16, out=gate_sb[:, :], in_=gate_d)
                id_ld = p.emit("sync", "dma_start", sem=sLd, k=16, out=ident[:, :], in_=ident_d)

                p.emit("gpsimd", "memset", c_eps[:, :], EPS, sem=sW)
                p.emit("gpsimd", "memset", c_zero[:, :], 0.0, sem=sW)
                p.emit("gpsimd", "memset", c_one[:, :], 1.0, sem=sW)
                p.emit("gpsimd", "memset", ones_col[:, :], 1.0, sem=sW)
                consts_done = sW.n
                wq_ld = p.emit("gpsimd", "dma_start", sem=sW, k=16, out=wq_sb[:, :, :],
                               in_=wqt_d.rearrange("(c p) m -> p c m", p=128))
                wk_ld = p.emit("gpsimd", "dma_start", sem=sW, k=16, out=wk_sb[:, :, :],
                               in_=wkt_d.rearrange("(c p) m -> p c m", p=128))
                wv_ld = p.emit("gpsimd", "dma_start", sem=sW, k=16, out=wv_sb[:, :, :],
                               in_=wvt_d.rearrange("(c p) m -> p c m", p=128))
                wo_ld = p.emit("gpsimd", "dma_start", sem=sW, k=16, out=wot_sb[:, :, :],
                               in_=wot_d.rearrange("(c p) h -> p c h", p=128))

                p.wait("scalar", sLd, gate_ld)
                p.wait("scalar", sW, consts_done)
                p.emit("scalar", "activation", sig_s[:, 0:1], gate_sb[:, :], AF.Exp, bias=c_zero[0:1, :], scale=-1.0)
                p.emit("scalar", "activation", sig_s[:, 1:2], sig_s[:, 0:1], AF.Identity, bias=c_one[0:1, :], scale=1.0)
                p.emit("scalar", "activation", sig_s[:, 2:3], sig_s[:, 1:2], AF.Ln, bias=c_zero[0:1, :], scale=1.0)
                p.emit("scalar", "activation", sig_s[:, 3:4], sig_s[:, 2:3], AF.Exp, bias=c_zero[0:1, :], scale=-1.0)
                p.emit("scalar", "activation", sig_bf[:, :], sig_s[:, 3:4], AF.Identity, bias=c_zero[0:1, :], scale=1.0, sem=sS)
                sigbf_done = sS.n
                p.wait("tensor", sS, sigbf_done)
                p.emit("tensor", "matmul", p_pr[:, 0:1], ones_col[:, :], sig_bf[:, :], start=True, stop=True, sem=sT)
                p.wait("vector", sT, sT.n)
                p.emit("vector", "tensor_copy", sig_bc[:, :], p_pr[:, 0:1], sem=sV)
                sigbc_done = sV.n
                p.wait("scalar", sW, wo_ld)
                p.wait("scalar", sV, sigbc_done)
                p.emit("scalar", "activation", wot_sb[:, 0, :], wot_sb[:, 0, :], AF.Identity, bias=c_zero[:, :], scale=sig_bc[:, 0:1])
                p.emit("scalar", "activation", wot_sb[:, 1, :], wot_sb[:, 1, :], AF.Identity, bias=c_zero[:, :], scale=sig_bc[:, 0:1], sem=sS)
                wotg_done = sS.n

                SUM, SQS, MEAN, EX2, VAR, MSQ, RSTD, MS, NEGMS = range(9)
                for tt in range(NT):
                    p.wait("vector", sLd, x_ld[tt // 4])
                    p.emit("vector", "tensor_reduce", stats[:, SUM, tt:tt + 1], x_tok[:, tt, :], axis=AX.X, op=OP.add, sem=sV)
                    p.wait("scalar", sLd, x_ld[tt // 4])
                    p.emit("scalar", "activation", qT[:, 0, 0:H], x_tok[:, tt, :], AF.Square,
                           bias=c_zero[:, :], scale=1.0, accum_out=stats[:, SQS, tt:tt + 1], sem=sS)
                sum_done = sV.n
                p.wait("scalar", sV, sum_done)
                p.emit("scalar", "activation", stats[:, MEAN, :], stats[:, SUM, :], AF.Identity, bias=c_zero[:, :], scale=1.0 / H)
                p.emit("scalar", "activation", stats[:, EX2, :], stats[:, SQS, :], AF.Identity, bias=c_zero[:, :], scale=1.0 / H)
                p.emit("scalar", "activation", stats[:, MSQ, :], stats[:, MEAN, :], AF.Square, bias=c_zero[:, :], scale=1.0, sem=sS)
                p.wait("vector", sS, sS.n)
                p.emit("vector", "tensor_tensor", stats[:, VAR, :], stats[:, EX2, :], stats[:, MSQ, :], OP.subtract, sem=sV)
                p.wait("scalar", sV, sV.n)
                p.emit("scalar", "activation", stats[:, RSTD, :], stats[:, VAR, :], AF.Ln, bias=c_eps[:, :], scale=1.0)
                p.emit("scalar", "activation", stats[:, RSTD, :], stats[:, RSTD, :], AF.Exp, bias=c_zero[:, :], scale=-0.5, sem=sS)
                p.wait("vector", sS, sS.n)
                p.emit("vector", "tensor_tensor", stats[:, MS, :], stats[:, MEAN, :], stats[:, RSTD, :], OP.mult, sem=sV)
                p.wait("scalar", sV, sV.n)
                p.emit("scalar", "activation", stats[:, NEGMS, :], stats[:, MS, :], AF.Identity, bias=c_zero[:, :], scale=-1.0, sem=sS)
                negms_done = sS.n

                ping_ev = [0, 0]
                aff_free = [0, 0]
                for tt in range(NT):
                    pi = tt % 2
                    p.wait("scalar", sS, negms_done)
                    p.wait("scalar", sT, aff_free[pi])
                    p.emit("scalar", "activation", xa[:, pi, :], x_tok[:, tt, :], AF.Identity,
                           bias=stats[:, NEGMS, tt:tt + 1], scale=stats[:, RSTD, tt:tt + 1], sem=sS)
                    a_n = sS.n
                    for h in range(HC):
                        bi = h % 2
                        p.wait("tensor", sS, a_n)
                        p.wait("tensor", sV, ping_ev[bi])
                        if tt == 0 and h == 0:
                            p.wait("tensor", sLd, id_ld)
                        p.emit("tensor", "transpose", p_tr[:, bi, 0:128], xa[:, pi, h * 128:(h + 1) * 128], ident[:, :], sem=sT)
                        p.wait("vector", sT, sT.n)
                        p.emit("vector", "tensor_copy", xln1[:, h, tt * 128:(tt + 1) * 128], p_tr[:, bi, 0:128], sem=sV)
                        ping_ev[bi] = sV.n
                    aff_free[pi] = sT.n
                xln1_done_v = sV.n

                proj_ping_v = [sV.n, sV.n]
                pp = 0
                for (wsb, dst) in ((wq_sb, qT), (wk_sb, kT)):
                    for mt in range(2):
                        for tb4 in range(4):
                            half = pp % 2
                            p.wait("tensor", sV, proj_ping_v[half])
                            if pp == 0:
                                p.wait("tensor", sV, xln1_done_v)
                                p.wait("tensor", sW, wv_ld)
                            for h in range(HC):
                                p.emit("tensor", "matmul", p_pr[:, half * 512:half * 512 + 512],
                                       wsb[:, h, mt * 128:(mt + 1) * 128], xln1[:, h, tb4 * 512:(tb4 + 1) * 512],
                                       start=(h == 0), stop=(h == HC - 1), sem=(sT if h == HC - 1 else None))
                            p.wait("vector", sT, sT.n)
                            p.emit("vector", "tensor_copy", dst[:, mt, tb4 * 512:(tb4 + 1) * 512],
                                   p_pr[:, half * 512:half * 512 + 512], sem=sV)
                            proj_ping_v[half] = sV.n
                            pp += 1
                qk_done_v = sV.n
                proj_ping_s = [0, 0]
                for tt in range(NT):
                    for (wsb, dst, scale_ap) in ((wk_sb, ktok, lamk_sb[:, 0:1]), (wv_sb, vtok, None)):
                        half = pp % 2
                        p.wait("tensor", sS, proj_ping_s[half])
                        p.wait("tensor", sV, proj_ping_v[half])
                        for h in range(HC):
                            p.emit("tensor", "matmul", p_pr[:, half * 512:half * 512 + 256],
                                   xln1[:, h, tt * 128:(tt + 1) * 128], wsb[:, h, :],
                                   start=(h == 0), stop=(h == HC - 1), sem=(sT if h == HC - 1 else None))
                        p.wait("scalar", sT, sT.n)
                        if tt == 0 and scale_ap is not None:
                            p.wait("scalar", sLd, lamk_ld)
                        p.emit("scalar", "activation", dst[:, tt, :], p_pr[:, half * 512:half * 512 + 256],
                               AF.Identity, bias=c_zero[:, :],
                               scale=(scale_ap if scale_ap is not None else 1.0), sem=sS)
                        proj_ping_s[half] = sS.n
                        pp += 1
                kv_done_s = sS.n

                p.wait("tensor", sS, kv_done_s)
                da_ping = [0, 0]
                for cl in range(NCH):
                    for mk in range(2):
                        reg = (0, 256) if mk == 0 else (512, 768)
                        p.wait("tensor", sV, da_ping[mk])
                        for bp in range(B):
                            tt = bp * NCH + cl
                            p.emit("tensor", "matmul", p_pr[:, reg[0]:reg[1]],
                                   ktok[:, tt, mk * 128:(mk + 1) * 128], vtok[:, tt, :],
                                   start=(bp == 0), stop=(bp == B - 1), sem=(sT if bp == B - 1 else None))
                        p.wait("vector", sT, sT.n)
                        p.emit("vector", "tensor_copy", dA[:, cl, mk, :], p_pr[:, reg[0]:reg[1]], sem=sV)
                        da_ping[mk] = sV.n
                p.emit("vector", "tensor_copy", Zw[:, :, :], dA[:, 0, :, :], sem=sV)
                for cl in range(1, NCH):
                    p.wait("scalar", sV, sV.n)
                    p.emit("scalar", "activation", Zw2[:, :, :], Zw[:, :, :], AF.Identity, bias=c_zero[:, :], scale=LAM128, sem=sS)
                    p.wait("vector", sS, sS.n)
                    p.emit("vector", "tensor_tensor", Zw[:, :, :], Zw2[:, :, :], dA[:, cl, :, :], OP.add, sem=sV)
                z_done_v = sV.n

                p.wait("gpsimd", sV, z_done_v)
                p.emit("gpsimd", "dma_start", sem=sW, k=16, out=agin.rearrange("(c p) m -> p c m", p=128), in_=Zw[:, :, :])
                p.wait("gpsimd", sW, sW.n)
                p.emit("gpsimd", "collective_compute", "AllGather", OP.bypass,
                       replica_groups=[list(range(NCORE))], ins=[agin.opt()], outs=[agout.opt()], sem=sC)
                p.wait("gpsimd", sC, sC.n)
                zall_ld = p.emit("gpsimd", "dma_start", sem=sW, k=16, out=Zall[:, :, :],
                                 in_=agout.rearrange("(c p) m -> p c m", p=128))

                p.wait("scalar", sW, zall_ld)
                p.wait("scalar", sLd, prefw_ld)
                p.emit("scalar", "activation", Ast[:, :, :], m0t_sb[:, :, :], AF.Identity, bias=c_zero[:, :],
                       scale=prefw_sb[:, 8:9], sem=sS)
                for k in range(NCORE):
                    p.emit("scalar", "activation", Zw2[:, :, :], Zall[:, 2 * k:2 * k + 2, :], AF.Identity,
                           bias=c_zero[:, :], scale=prefw_sb[:, k:k + 1], sem=sS)
                    p.wait("vector", sS, sS.n)
                    p.emit("vector", "tensor_tensor", Ast[:, :, :], Zw2[:, :, :], Ast[:, :, :], OP.add, sem=sV)
                    p.wait("scalar", sV, sV.n)
                p.emit("vector", "tensor_copy", AstBf[:, :, :], Ast[:, :, :], sem=sV)
                astbf_done = sV.n

                s_ping = [sV.n, sV.n]
                for cl in range(NCH):
                    stb_ev = {}
                    for bp in range(B):
                        half = bp % 2
                        p.wait("tensor", sV, s_ping[half])
                        if cl == 0 and bp == 0:
                            p.wait("tensor", sV, qk_done_v)
                        for mk in range(2):
                            p.emit("tensor", "matmul", p_s[:, half * 512:half * 512 + 512],
                                   kT[:, mk, bp * SC + cl * 128: bp * SC + cl * 128 + 128],
                                   qT[:, mk, :].rearrange("p (b s) -> p b s", b=B)[:, :, cl * 128:(cl + 1) * 128],
                                   start=(mk == 0), stop=(mk == 1), sem=(sT if mk == 1 else None))
                        p.wait("vector", sT, sT.n)
                        if cl == 0 and bp == 0:
                            p.wait("vector", sLd, msk_ld)
                            p.wait("vector", sLd, dq_ld)
                        p.emit("vector", "tensor_tensor", sTb[:, bp, :],
                               p_s[:, half * 512:half * 512 + 512], mask_sb[:, :], OP.mult, sem=sV)
                        stb_ev[bp] = sV.n
                        s_ping[half] = sV.n
                    for mk in range(2):
                        p.emit("vector", "tensor_tensor",
                               qs[:, mk, :].rearrange("p (b s) -> p b s", b=B),
                               qT[:, mk, :].rearrange("p (b s) -> p b s", b=B)[:, :, cl * 128:(cl + 1) * 128],
                               dq_sb.rearrange("p (b s) -> p b s", b=B), OP.mult, sem=sV)
                    qs_done = sV.n
                    for mv in range(2):
                        p.wait("tensor", sV, qs_done)
                        p.wait("tensor", sV, astbf_done)
                        p.emit("tensor", "matmul", p_r[:, mv * 512:mv * 512 + 512],
                               AstBf[:, 0, mv * 128:(mv + 1) * 128], qs[:, 0, :], start=True, stop=False)
                        p.emit("tensor", "matmul", p_r[:, mv * 512:mv * 512 + 512],
                               AstBf[:, 1, mv * 128:(mv + 1) * 128], qs[:, 1, :], start=False, stop=False)
                        for bp in range(B):
                            tt = bp * NCH + cl
                            p.wait("tensor", sV, stb_ev[bp])
                            p.emit("tensor", "matmul", p_r[:, mv * 512:mv * 512 + 512],
                                   vtok[:, tt, mv * 128:(mv + 1) * 128], sTb[:, bp, :],
                                   start=False, stop=(bp == B - 1), sem=(sT if bp == B - 1 else None))
                        p.wait("vector", sT, sT.n)
                        p.emit("vector", "tensor_copy",
                               rT[:, mv, :].rearrange("p (b s) -> p b s", b=B)[:, :, cl * 128:(cl + 1) * 128],
                               p_r[:, mv * 512:mv * 512 + 512].rearrange("p (b s) -> p b s", b=B), sem=sV)
                    if cl < NCH - 1:
                        p.wait("scalar", sT, sT.n)
                        p.emit("scalar", "activation", Zw2[:, :, :], Ast[:, :, :], AF.Identity, bias=c_zero[:, :],
                               scale=LAM128, sem=sS)
                        p.wait("vector", sS, sS.n)
                        p.emit("vector", "tensor_tensor", Ast[:, :, :], Zw2[:, :, :], dA[:, cl, :, :], OP.add, sem=sV)
                        p.emit("vector", "tensor_copy", AstBf[:, :, :], Ast[:, :, :], sem=sV)
                        astbf_done = sV.n
                r_done_v = sV.n

                x1_add = {}
                for tt in range(NT):
                    if tt == 0:
                        p.wait("tensor", sV, r_done_v)
                        p.wait("tensor", sS, wotg_done)
                    else:
                        p.wait("tensor", sV, x1_add[tt - 1])
                    for half in range(2):
                        for mt in range(2):
                            p.emit("tensor", "matmul", p_pr[:, half * 512:half * 512 + 512],
                                   rT[:, mt, tt * 128:(tt + 1) * 128], wot_sb[:, mt, half * 512:half * 512 + 512],
                                   start=(mt == 0), stop=(mt == 1),
                                   sem=(sT if (half == 1 and mt == 1) else None))
                    p.wait("vector", sT, sT.n)
                    p.emit("vector", "tensor_tensor", x_tok[:, tt, :], p_pr[:, 0:1024], x_tok[:, tt, :], OP.add, sem=sV)
                    x1_add[tt] = sV.n
                x1_done = sV.n
                p.flush(blkA)

        with ExitStack() as ctxB:
            EB = ctxB.enter_context
            xln2 = EB(nc.sbuf_tensor("xln2", [128, HC, T], BF16))
            xa2 = EB(nc.sbuf_tensor("xa2", [128, 2, H], BF16))
            stats2 = EB(nc.sbuf_tensor("stats2", [128, 9, NT], F32))
            w1b = EB(nc.sbuf_tensor("w1b", [128, HC, FH * 128], BF16))
            w2b = EB(nc.sbuf_tensor("w2b", [128, FH, H], BF16))
            h1 = EB(nc.sbuf_tensor("h1", [128, FH, TB], BF16))
            b1c = EB(nc.sbuf_tensor("b1c", [128, FC], F32))
            b2r = EB(nc.sbuf_tensor("b2r", [1, H], BF16))
            wr_sb = EB(nc.sbuf_tensor("wr_sb", [128, HC, NEXP], BF16))
            br_r = EB(nc.sbuf_tensor("br_r", [1, NEXP], BF16))
            probs = EB(nc.sbuf_tensor("probs", [128, NT, NEXP], F32))
            expt = EB(nc.sbuf_tensor("expt", [128, NEXP], F32))
            rstat = EB(nc.sbuf_tensor("rstat", [128, 5, NT], F32))
            mcol = EB(nc.sbuf_tensor("mcol", [128, NT, NEXP], F32))
            indc = EB(nc.sbuf_tensor("indc", [128, NEXP], F32))
            tmp_epi = EB(nc.sbuf_tensor("tmp_epi", [128, H], F32))
            p_yA = EB(nc.psum_tensor("p_yA", [128, 1024], F32))
            p_yB = EB(nc.psum_tensor("p_yB", [128, 1024], F32))
            p_h = EB(nc.psum_tensor("p_h", [128, 1024], F32))
            p_rt = EB(nc.psum_tensor("p_rt", [128, 512], F32))
            p_tr2 = EB(nc.psum_tensor("p_tr2", [128, 1024], BF16))

            with nc.Block("Bf") as blkB:
                SUM, SQS, MEAN, EX2, VAR, MSQ, RSTD, MS, NEGMS = range(9)
                MX, SE, LNSE, RINV, GMX = range(5)

                wr_ld = p.emit("gpsimd", "dma_start", sem=sW, k=16, out=wr_sb[:, :, :],
                               in_=wr_d.rearrange("(c p) e -> p c e", p=128))
                br_ld = p.emit("gpsimd", "dma_start", sem=sW, k=16, out=br_r[:, :], in_=br_d)

                w1_thr = {}
                w2_thr = {}
                b_thr = {}

                def emit_weight_loads(e, hf, w1_free, w2_free, b_free):
                    for qp in range(2):
                        if w1_free:
                            p.wait("gpsimd", sT, w1_free)
                        w1_thr[(e, hf, qp)] = p.emit(
                            "gpsimd", "dma_start", sem=sW, k=16,
                            out=w1b[:, :, qp * 1024:(qp + 1) * 1024],
                            in_=w1_d[e].rearrange("(c p) f -> p c f", p=128)[:, :, hf * 2048 + qp * 1024: hf * 2048 + (qp + 1) * 1024])
                    for qp in range(2):
                        if w2_free:
                            p.wait("gpsimd", sT, w2_free)
                        w2_thr[(e, hf, qp)] = p.emit(
                            "gpsimd", "dma_start", sem=sW, k=16,
                            out=w2b[:, 8 * qp:8 * (qp + 1), :],
                            in_=w2_d[e].rearrange("(c p) h -> p c h", p=128)[:, hf * FH + 8 * qp: hf * FH + 8 * (qp + 1), :])
                    if hf == 0:
                        if b_free is not None:
                            p.wait("gpsimd", sS, b_free[0])
                            p.wait("gpsimd", sV, b_free[1])
                            p.wait("gpsimd", sT, b_free[2])
                        b_thr[e] = (
                            p.emit("gpsimd", "dma_start", sem=sW, k=16, out=b1c[:, :], in_=b1_d[e]),
                            p.emit("gpsimd", "dma_start", sem=sW, k=16, out=b2r[:, :], in_=b2_d[e:e + 1, :]),
                        )

                emit_weight_loads(0, 0, 0, 0, None)

                for tt in range(NT):
                    if tt == 0:
                        p.wait("vector", sV, x1_done)
                    p.emit("vector", "tensor_reduce", stats2[:, SUM, tt:tt + 1], x_tok[:, tt, :], axis=AX.X, op=OP.add, sem=sV)
                    p.emit("scalar", "activation", xa2[:, 0, :], x_tok[:, tt, :], AF.Square,
                           bias=c_zero[:, :], scale=1.0, accum_out=stats2[:, SQS, tt:tt + 1], sem=sS)
                p.wait("scalar", sV, sV.n)
                p.emit("scalar", "activation", stats2[:, MEAN, :], stats2[:, SUM, :], AF.Identity, bias=c_zero[:, :], scale=1.0 / H)
                p.emit("scalar", "activation", stats2[:, EX2, :], stats2[:, SQS, :], AF.Identity, bias=c_zero[:, :], scale=1.0 / H)
                p.emit("scalar", "activation", stats2[:, MSQ, :], stats2[:, MEAN, :], AF.Square, bias=c_zero[:, :], scale=1.0, sem=sS)
                p.wait("vector", sS, sS.n)
                p.emit("vector", "tensor_tensor", stats2[:, VAR, :], stats2[:, EX2, :], stats2[:, MSQ, :], OP.subtract, sem=sV)
                p.wait("scalar", sV, sV.n)
                p.emit("scalar", "activation", stats2[:, RSTD, :], stats2[:, VAR, :], AF.Ln, bias=c_eps[:, :], scale=1.0)
                p.emit("scalar", "activation", stats2[:, RSTD, :], stats2[:, RSTD, :], AF.Exp, bias=c_zero[:, :], scale=-0.5, sem=sS)
                p.wait("vector", sS, sS.n)
                p.emit("vector", "tensor_tensor", stats2[:, MS, :], stats2[:, MEAN, :], stats2[:, RSTD, :], OP.mult, sem=sV)
                p.wait("scalar", sV, sV.n)
                p.emit("scalar", "activation", stats2[:, NEGMS, :], stats2[:, MS, :], AF.Identity, bias=c_zero[:, :], scale=-1.0, sem=sS)
                negms2 = sS.n

                tr_free = 0
                aff2_free = 0
                for tt in range(NT):
                    p.wait("scalar", sS, negms2)
                    p.wait("scalar", sT, aff2_free)
                    p.emit("scalar", "activation", xa2[:, 1, :], x_tok[:, tt, :], AF.Identity,
                           bias=stats2[:, NEGMS, tt:tt + 1], scale=stats2[:, RSTD, tt:tt + 1], sem=sS)
                    a_n = sS.n
                    for h in range(HC):
                        p.wait("tensor", sS, a_n)
                        p.wait("tensor", sV, tr_free)
                        p.emit("tensor", "transpose", p_tr2[:, 0:128], xa2[:, 1, h * 128:(h + 1) * 128], ident[:, :], sem=sT)
                        p.wait("vector", sT, sT.n)
                        p.emit("vector", "tensor_copy", xln2[:, h, tt * 128:(tt + 1) * 128], p_tr2[:, 0:128], sem=sV)
                        tr_free = sV.n
                    aff2_free = sT.n
                xln2_done = sV.n

                rt_free = 0
                for tt in range(NT):
                    if tt == 0:
                        p.wait("tensor", sV, xln2_done)
                        p.wait("tensor", sW, br_ld)
                    p.wait("tensor", sS, rt_free)
                    p.emit("tensor", "matmul", p_rt[:, 0:NEXP], ones_col[:, :], br_r[:, :], start=True, stop=False)
                    for h in range(HC):
                        p.emit("tensor", "matmul", p_rt[:, 0:NEXP], xln2[:, h, tt * 128:(tt + 1) * 128],
                               wr_sb[:, h, :], start=False, stop=(h == HC - 1), sem=(sT if h == HC - 1 else None))
                    p.wait("vector", sT, sT.n)
                    p.emit("vector", "tensor_reduce", rstat[:, MX, tt:tt + 1], p_rt[:, 0:NEXP], axis=AX.X, op=OP.max, negate=True, sem=sV)
                    p.wait("scalar", sV, sV.n)
                    p.emit("scalar", "activation", expt[:, :], p_rt[:, 0:NEXP], AF.Exp,
                           bias=rstat[:, MX, tt:tt + 1], scale=1.0, accum_out=rstat[:, SE, tt:tt + 1], sem=sS)
                    rt_free = sS.n
                    p.emit("scalar", "activation", rstat[:, LNSE, tt:tt + 1], rstat[:, SE, tt:tt + 1], AF.Ln, bias=c_zero[:, :], scale=1.0)
                    p.emit("scalar", "activation", rstat[:, RINV, tt:tt + 1], rstat[:, LNSE, tt:tt + 1], AF.Exp, bias=c_zero[:, :], scale=-1.0)
                    p.emit("scalar", "activation", probs[:, tt, :], expt[:, :], AF.Identity, bias=c_zero[:, :],
                           scale=rstat[:, RINV, tt:tt + 1], sem=sS)
                    p.wait("vector", sS, sS.n)
                    p.emit("vector", "tensor_reduce", rstat[:, GMX, tt:tt + 1], probs[:, tt, :], axis=AX.X, op=OP.max, sem=sV)
                for tt in range(NT):
                    for e in range(NEXP):
                        p.emit("vector", "tensor_tensor", indc[:, e:e + 1], probs[:, tt, e:e + 1], rstat[:, GMX, tt:tt + 1], OP.is_ge)
                    for e in range(NEXP):
                        p.emit("vector", "tensor_tensor", mcol[:, tt, e:e + 1], indc[:, e:e + 1], rstat[:, GMX, tt:tt + 1], OP.mult, sem=sV)
                m_done = sV.n

                h_free = [(sS, 0), (sV, 0)]
                last_epi = {0: 0, 1: 0}
                prev_epi_v = 0
                y_done = {}
                for e in range(NEXP):
                    for hf in range(2):
                        w1_free = 0
                        w2_free = 0
                        fgrp = 0
                        for tb in range(NTB):
                            for f in range(FH):
                                hb = fgrp % 2
                                reg = (0, 512) if hb == 0 else (512, 1024)
                                if tb == 0 and f % 8 == 0:
                                    p.wait("tensor", sW, w1_thr[(e, hf, f // 8)])
                                if tb == 0 and f == 0:
                                    if e == 0 and hf == 0:
                                        p.wait("tensor", sV, xln2_done)
                                    p.wait("tensor", sW, b_thr[e][0])
                                semref, cnt = h_free[hb]
                                p.wait("tensor", semref, cnt)
                                for h in range(HC):
                                    p.emit("tensor", "matmul", p_h[:, reg[0]:reg[1]],
                                           w1b[:, h, f * 128:(f + 1) * 128],
                                           xln2[:, h, tb * TB:(tb + 1) * TB],
                                           start=(h == 0), stop=(h == HC - 1), sem=(sT if h == HC - 1 else None))
                                g_n = sT.n
                                if tb == NTB - 1 and f == FH - 1:
                                    w1_free = g_n
                                if hb == 0:
                                    p.wait("scalar", sT, g_n)
                                    p.emit("scalar", "activation", h1[:, f, :], p_h[:, reg[0]:reg[1]], AF.Relu,
                                           bias=b1c[:, hf * FH + f:hf * FH + f + 1], scale=1.0, sem=sS)
                                    h_free[0] = (sS, sS.n)
                                else:
                                    p.wait("vector", sT, g_n)
                                    p.emit("vector", "tensor_scalar", h1[:, f, :], p_h[:, reg[0]:reg[1]],
                                           b1c[:, hf * FH + f:hf * FH + f + 1], 0.0, OP.add, OP.max, sem=sV)
                                    h_free[1] = (sV, sV.n)
                                fgrp += 1
                            h1_s = sS.n
                            h1_v = sV.n
                            last_b = (tb == NTB - 1)
                            for sub in range(4):
                                tt = 4 * tb + sub
                                yp = p_yA if (tt % 2 == 0) else p_yB
                                p.wait("tensor", sS, last_epi[tt % 2])
                                p.wait("tensor", sS, h1_s)
                                p.wait("tensor", sV, h1_v)
                                if hf == 0:
                                    p.wait("tensor", sW, b_thr[e][1])
                                    p.emit("tensor", "matmul", yp[:, 0:512], ones_col[:, :], b2r[:, 0:512], start=True, stop=False)
                                    p.emit("tensor", "matmul", yp[:, 512:1024], ones_col[:, :], b2r[:, 512:1024], start=True, stop=False)
                                for f in range(FH):
                                    if tb == 0 and sub == 0 and f % 8 == 0:
                                        p.wait("tensor", sW, w2_thr[(e, hf, f // 8)])
                                    st0 = (hf == 1 and f == 0)
                                    p.emit("tensor", "matmul", yp[:, 0:512], h1[:, f, sub * 128:(sub + 1) * 128],
                                           w2b[:, f, 0:512], start=st0, stop=False)
                                    lastf = (f == FH - 1)
                                    p.emit("tensor", "matmul", yp[:, 512:1024], h1[:, f, sub * 128:(sub + 1) * 128],
                                           w2b[:, f, 512:1024], start=st0, stop=lastf,
                                           sem=(sT if lastf else None))
                                mm2_n = sT.n
                                if last_b and sub == 3:
                                    w2_free = mm2_n
                                p.wait("scalar", sT, mm2_n)
                                if e == 0 and hf == 0 and tt == 0:
                                    p.wait("scalar", sV, m_done)
                                p.wait("scalar", sV, prev_epi_v)
                                p.emit("scalar", "activation", tmp_epi[:, :], yp[:, 0:1024], AF.Identity,
                                       bias=c_zero[:, :], scale=mcol[:, tt, e:e + 1], sem=sS)
                                last_epi[tt % 2] = sS.n
                                p.wait("vector", sS, sS.n)
                                p.emit("vector", "tensor_tensor", x_tok[:, tt, :], tmp_epi[:, :], x_tok[:, tt, :], OP.add, sem=sV)
                                prev_epi_v = sV.n
                                if e == NEXP - 1 and hf == 1:
                                    y_done[tt] = sV.n
                        b_free = (sS.n, sV.n, sT.n)
                        if hf == 0:
                            emit_weight_loads(e, 1, w1_free, w2_free, b_free)
                        elif e < NEXP - 1:
                            emit_weight_loads(e + 1, 0, w1_free, w2_free, b_free)

                for k in range(4):
                    p.wait("sync", sV, max(y_done[4 * k + i] for i in range(4)))
                    p.emit("sync", "dma_start", sem=sO, k=16,
                           out=y_d.rearrange("(i p) h -> p i h", p=128)[:, 4 * k:4 * (k + 1), :],
                           in_=x_tok[:, 4 * k:4 * (k + 1), :])
                p.wait("sync", sO, sO.n)
                p.flush(blkB)
    return nc


def _consts():
    lam = np.float64(DECAY)
    jl = np.arange(128)
    tl = np.arange(128)
    mk = np.where(jl[:, None] < tl[None, :], lam ** (tl[None, :] - 1 - jl[:, None]), 0.0) / B
    mask = np.concatenate([mk] * B, axis=1).astype(np.float32)
    dq = np.concatenate([np.broadcast_to(lam ** tl, (128, 128))] * B, axis=1).astype(np.float32)
    lamk = (lam ** (127 - jl) / B).astype(np.float32).reshape(128, 1)
    ident = np.eye(128, dtype=ml_dtypes.bfloat16)
    return mask, np.ascontiguousarray(dq), lamk, ident


_NC_CACHE = {}
LAST_RES = None


def kernel(L0_attn_x, L0_attn_Wq, L0_attn_Wk, L0_attn_Wv, L0_attn_Wo, L0_attn_gate, L0_attn_M0,
           L0_ffn_Wr, L0_ffn_br, L0_ffn_W1, L0_ffn_b1, L0_ffn_W2, L0_ffn_b2):
    x = np.asarray(L0_attn_x, dtype=np.float32)
    wqt = np.ascontiguousarray(np.asarray(L0_attn_Wq, np.float32).T)
    wkt = np.ascontiguousarray(np.asarray(L0_attn_Wk, np.float32).T)
    wvt = np.ascontiguousarray(np.asarray(L0_attn_Wv, np.float32).T)
    wot = np.ascontiguousarray(np.asarray(L0_attn_Wo, np.float32).T)
    gate = np.asarray(L0_attn_gate, np.float32).reshape(1, 1)
    m0t = np.ascontiguousarray(np.asarray(L0_attn_M0, np.float32).T)
    wr = np.ascontiguousarray(np.asarray(L0_ffn_Wr, np.float32))
    br = np.asarray(L0_ffn_br, np.float32).reshape(1, NEXP)
    w1 = np.ascontiguousarray(np.asarray(L0_ffn_W1, np.float32))
    b1 = np.ascontiguousarray(np.asarray(L0_ffn_b1, np.float32).reshape(NEXP, FC, 128).transpose(0, 2, 1))
    w2 = np.ascontiguousarray(np.asarray(L0_ffn_W2, np.float32))
    b2 = np.ascontiguousarray(np.asarray(L0_ffn_b2, np.float32))
    mask, dq, lamk, ident = _consts()

    if "nc" not in _NC_CACHE:
        _NC_CACHE["nc"] = build()
    nc = _NC_CACHE["nc"]

    lam = np.float64(DECAY)
    in_maps = []
    for c in range(NCORE):
        xs = np.ascontiguousarray(x[:, c * SC:(c + 1) * SC, :]).reshape(T, H)
        w = np.zeros(9, np.float64)
        for k in range(c):
            w[k] = lam ** (SC * (c - 1 - k))
        w[8] = lam ** (SC * c)
        prefw = np.ascontiguousarray(np.tile(w.astype(np.float32)[None, :], (128, 1)))
        in_maps.append({
            "x": xs, "wqt": wqt, "wkt": wkt, "wvt": wvt, "wot": wot, "gate": gate, "m0t": m0t,
            "wr": wr, "br": br, "w1": w1, "b1": b1, "w2": w2, "b2": b2,
            "mask": mask, "dq": dq, "lamk": lamk, "prefw": prefw, "ident": ident,
        })
    res = run_bass_kernel_spmd(nc, in_maps, core_ids=list(range(NCORE)))
    global LAST_RES
    LAST_RES = res
    outs = [np.asarray(res.results[c]["y"], dtype=np.float32).reshape(B, SC, H) for c in range(NCORE)]
    return np.concatenate(outs, axis=1)


# version 07
# speedup vs baseline: 2.6999x; 2.4655x over previous
import numpy as np
import ml_dtypes
from contextlib import ExitStack

import concourse.bass as bass
import concourse.mybir as mybir
from concourse.alu_op_type import AluOpType
from concourse.bass_utils import run_bass_kernel_spmd

F32 = mybir.dt.float32
BF16 = mybir.dt.bfloat16
AF = mybir.ActivationFunctionType
AX = mybir.AxisListType
OP = AluOpType
I16 = mybir.dt.int16
U32 = mybir.dt.uint32

NCORE = 8
B = 4
S = 4096
H = 1024
M = 256
NEXP = 8
DFF = 4096
DECAY = 0.99
SC = S // NCORE
T = B * SC
C = 128
NCH = SC // C
NT = T // 128
TB = 512
NTB = T // TB
HC = H // 128
FC = DFF // 128
FH = FC // 2
CAP = 512
NSLOT = NEXP * CAP
LAM128 = float(np.float32(np.float64(DECAY) ** 128))
EPS = 1e-5

ENGS = ("sync", "gpsimd", "tensor", "vector", "scalar")


class Ctr:
    def __init__(self, h):
        self.h = h
        self.n = 0


class Prog:
    def __init__(self):
        self.q = {k: [] for k in ENGS}

    def add(self, eng, f):
        self.q[eng].append(f)

    def wait(self, eng, sem, val):
        assert val <= sem.n, f"wait target {val} > planned {sem.n} ({eng})"
        if val <= 0:
            return
        self.add(eng, lambda e, s=sem.h, v=val: e.wait_ge(s, v))

    def emit(self, eng, meth, *a, sem=None, k=1, **kw):
        if sem is not None:
            sem.n += k
            self.add(eng, lambda e, m=meth, a=a, kw=kw, sh=sem.h, kk=k:
                     getattr(e, m)(*a, **kw).then_inc(sh, kk))
            return sem.n
        self.add(eng, lambda e, m=meth, a=a, kw=kw: getattr(e, m)(*a, **kw))
        return None

    def flush(self, blk):
        for eng in ENGS:
            fns = self.q[eng]
            self.q[eng] = []
            if fns:
                getattr(blk, eng)(lambda e, fns=fns: [f(e) for f in fns])


def build():
    nc = bass.Bass("TRN2", target_bir_lowering=False, detect_race_conditions=False)
    x_d = nc.dram_tensor("x", [T, H], F32, kind="ExternalInput").ap()
    wqt_d = nc.dram_tensor("wqt", [H, M], F32, kind="ExternalInput").ap()
    wkt_d = nc.dram_tensor("wkt", [H, M], F32, kind="ExternalInput").ap()
    wvt_d = nc.dram_tensor("wvt", [H, M], F32, kind="ExternalInput").ap()
    wot_d = nc.dram_tensor("wot", [M, H], F32, kind="ExternalInput").ap()
    gate_d = nc.dram_tensor("gate", [1, 1], F32, kind="ExternalInput").ap()
    m0t_d = nc.dram_tensor("m0t", [M, M], F32, kind="ExternalInput").ap()
    wr_d = nc.dram_tensor("wr", [H, NEXP], F32, kind="ExternalInput").ap()
    br_d = nc.dram_tensor("br", [1, NEXP], F32, kind="ExternalInput").ap()
    w1_d = nc.dram_tensor("w1", [NEXP, H, DFF], F32, kind="ExternalInput").ap()
    b1_d = nc.dram_tensor("b1", [NEXP, 128, FC], F32, kind="ExternalInput").ap()
    w2_d = nc.dram_tensor("w2", [NEXP, DFF, H], F32, kind="ExternalInput").ap()
    b2_d = nc.dram_tensor("b2", [NEXP, H], F32, kind="ExternalInput").ap()
    mask_d = nc.dram_tensor("mask", [128, 512], F32, kind="ExternalInput").ap()
    dq_d = nc.dram_tensor("dq", [128, 512], F32, kind="ExternalInput").ap()
    lamk_d = nc.dram_tensor("lamk", [128, 1], F32, kind="ExternalInput").ap()
    prefw_d = nc.dram_tensor("prefw", [128, 9], F32, kind="ExternalInput").ap()
    ident_d = nc.dram_tensor("ident", [128, 128], BF16, kind="ExternalInput").ap()
    y_d = nc.dram_tensor("y", [T, H], F32, kind="ExternalOutput").ap()
    lstrict_d = nc.dram_tensor("lstrict", [128, 128], BF16, kind="ExternalInput").ap()
    ltile_d = nc.dram_tensor("ltile", [128, 128], BF16, kind="ExternalInput").ap()
    eoff_d = nc.dram_tensor("eoff", [128, 128], F32, kind="ExternalInput").ap()
    identf_d = nc.dram_tensor("identf", [128, 128], F32, kind="ExternalInput").ap()
    tok16_d = nc.dram_tensor("tok16", [128, NT], mybir.dt.int16, kind="ExternalInput").ap()
    tok32_d = nc.dram_tensor("tok32", [128, NT], mybir.dt.uint32, kind="ExternalInput").ap()
    xlt_d = nc.dram_tensor("xlt", [T + 16, H], BF16).ap()
    yscr = nc.dram_tensor("yscr", [T, H], F32).ap()
    inv16 = nc.dram_tensor("inv16", [NSLOT, 1], mybir.dt.int16).ap()
    inv32 = nc.dram_tensor("inv32", [NSLOT, 1], mybir.dt.uint32).ap()
    agin = nc.dram_tensor("agin", [M, M], F32).ap()
    agout = nc.dram_tensor("agout", [NCORE * M, M], F32).ap()

    p = Prog()
    with ExitStack() as ctx:
        E = ctx.enter_context
        sLd = Ctr(E(nc.semaphore("sLd")))
        sW = Ctr(E(nc.semaphore("sW")))
        sT = Ctr(E(nc.semaphore("sT")))
        sV = Ctr(E(nc.semaphore("sV")))
        sS = Ctr(E(nc.semaphore("sS")))
        sC = Ctr(E(nc.semaphore("sC")))
        sG = Ctr(E(nc.semaphore("sG")))
        sL2 = Ctr(E(nc.semaphore("sL2")))
        sO = Ctr(E(nc.semaphore("sO")))
        x_tok = E(nc.sbuf_tensor("x_tok", [128, NT, H], F32))
        ident = E(nc.sbuf_tensor("ident_sb", [128, 128], BF16))
        ones_col = E(nc.sbuf_tensor("ones_col", [1, 128], BF16))
        c_eps = E(nc.sbuf_tensor("c_eps", [128, 1], F32))
        c_zero = E(nc.sbuf_tensor("c_zero", [128, 1], F32))
        c_one = E(nc.sbuf_tensor("c_one", [128, 1], F32))

        with ExitStack() as ctxA:
            EA = ctxA.enter_context
            xln1 = EA(nc.sbuf_tensor("xln1", [128, HC, T], BF16))
            xa = EA(nc.sbuf_tensor("xa", [128, 2, H], BF16))
            stats = EA(nc.sbuf_tensor("stats", [128, 9, NT], F32))
            wq_sb = EA(nc.sbuf_tensor("wq_sb", [128, HC, M], BF16))
            wk_sb = EA(nc.sbuf_tensor("wk_sb", [128, HC, M], BF16))
            wv_sb = EA(nc.sbuf_tensor("wv_sb", [128, HC, M], BF16))
            wot_sb = EA(nc.sbuf_tensor("wot_sb", [128, 2, H], BF16))
            qT = EA(nc.sbuf_tensor("qT", [128, 2, T], BF16))
            kT = EA(nc.sbuf_tensor("kT", [128, 2, T], BF16))
            ktok = EA(nc.sbuf_tensor("ktok", [128, NT, M], BF16))
            vtok = EA(nc.sbuf_tensor("vtok", [128, NT, M], BF16))
            rT = EA(nc.sbuf_tensor("rT", [128, 2, T], BF16))
            sTb = EA(nc.sbuf_tensor("sTb", [128, B, 512], BF16))
            qs = EA(nc.sbuf_tensor("qs", [128, 2, 512], BF16))
            Ast = EA(nc.sbuf_tensor("Ast", [128, 2, M], F32))
            AstBf = EA(nc.sbuf_tensor("AstBf", [128, 2, M], BF16))
            dA = EA(nc.sbuf_tensor("dA", [128, NCH, 2, M], F32))
            Zw = EA(nc.sbuf_tensor("Zw", [128, 2, M], F32))
            Zw2 = EA(nc.sbuf_tensor("Zw2", [128, 2, M], F32))
            m0t_sb = EA(nc.sbuf_tensor("m0t_sb", [128, 2, M], F32))
            Zall = EA(nc.sbuf_tensor("Zall", [128, 2 * NCORE, M], F32))
            mask_sb = EA(nc.sbuf_tensor("mask_sb", [128, 512], F32))
            dq_sb = EA(nc.sbuf_tensor("dq_sb", [128, 512], F32))
            lamk_sb = EA(nc.sbuf_tensor("lamk_sb", [128, 1], F32))
            prefw_sb = EA(nc.sbuf_tensor("prefw_sb", [128, 9], F32))
            gate_sb = EA(nc.sbuf_tensor("gate_sb", [1, 1], F32))
            sig_s = EA(nc.sbuf_tensor("sig_s", [1, 4], F32))
            sig_bf = EA(nc.sbuf_tensor("sig_bf", [1, 1], BF16))
            sig_bc = EA(nc.sbuf_tensor("sig_bc", [128, 1], F32))
            p_tr = EA(nc.psum_tensor("p_tr", [128, 2, 1024], BF16))
            p_pr = EA(nc.psum_tensor("p_pr", [128, 1024], F32))
            p_s = EA(nc.psum_tensor("p_s", [128, 1024], F32))
            p_r = EA(nc.psum_tensor("p_r", [128, 1024], F32))

            with nc.Block("A", no_gpsimd_drain=True) as blkA:
                x_ld = []
                for k in range(4):
                    x_ld.append(p.emit("sync", "dma_start", sem=sLd, k=16,
                                       out=x_tok[:, 4 * k:4 * (k + 1), :],
                                       in_=x_d.rearrange("(i p) h -> p i h", p=128)[:, 4 * k:4 * (k + 1), :]))
                m0_ld = p.emit("sync", "dma_start", sem=sLd, k=16, out=m0t_sb[:, :, :],
                               in_=m0t_d.rearrange("(c p) m -> p c m", p=128))
                msk_ld = p.emit("sync", "dma_start", sem=sLd, k=16, out=mask_sb[:, :], in_=mask_d)
                dq_ld = p.emit("sync", "dma_start", sem=sLd, k=16, out=dq_sb[:, :], in_=dq_d)
                lamk_ld = p.emit("sync", "dma_start", sem=sLd, k=16, out=lamk_sb[:, :], in_=lamk_d)
                prefw_ld = p.emit("sync", "dma_start", sem=sLd, k=16, out=prefw_sb[:, :], in_=prefw_d)
                gate_ld = p.emit("sync", "dma_start", sem=sLd, k=16, out=gate_sb[:, :], in_=gate_d)
                id_ld = p.emit("sync", "dma_start", sem=sLd, k=16, out=ident[:, :], in_=ident_d)
                a_total = sLd.n

                p.emit("gpsimd", "memset", c_eps[:, :], EPS, sem=sG)
                p.emit("gpsimd", "memset", c_zero[:, :], 0.0, sem=sG)
                p.emit("gpsimd", "memset", c_one[:, :], 1.0, sem=sG)
                p.emit("gpsimd", "memset", ones_col[:, :], 1.0, sem=sG)
                consts_done = sG.n
                wq_ld = p.emit("gpsimd", "dma_start", sem=sW, k=16, out=wq_sb[:, :, :],
                               in_=wqt_d.rearrange("(c p) m -> p c m", p=128))
                wk_ld = p.emit("gpsimd", "dma_start", sem=sW, k=16, out=wk_sb[:, :, :],
                               in_=wkt_d.rearrange("(c p) m -> p c m", p=128))
                wv_ld = p.emit("gpsimd", "dma_start", sem=sW, k=16, out=wv_sb[:, :, :],
                               in_=wvt_d.rearrange("(c p) m -> p c m", p=128))
                wo_ld = p.emit("gpsimd", "dma_start", sem=sW, k=16, out=wot_sb[:, :, :],
                               in_=wot_d.rearrange("(c p) h -> p c h", p=128))

                p.wait("scalar", sLd, a_total)
                p.wait("scalar", sG, consts_done)
                p.emit("scalar", "activation", sig_s[:, 0:1], gate_sb[:, :], AF.Exp, bias=c_zero[0:1, :], scale=-1.0)
                p.emit("scalar", "activation", sig_s[:, 1:2], sig_s[:, 0:1], AF.Identity, bias=c_one[0:1, :], scale=1.0)
                p.emit("scalar", "activation", sig_s[:, 2:3], sig_s[:, 1:2], AF.Ln, bias=c_zero[0:1, :], scale=1.0)
                p.emit("scalar", "activation", sig_s[:, 3:4], sig_s[:, 2:3], AF.Exp, bias=c_zero[0:1, :], scale=-1.0)
                p.emit("scalar", "activation", sig_bf[:, :], sig_s[:, 3:4], AF.Identity, bias=c_zero[0:1, :], scale=1.0, sem=sS)
                sigbf_done = sS.n
                p.wait("tensor", sS, sigbf_done)
                p.emit("tensor", "matmul", p_pr[:, 0:1], ones_col[:, :], sig_bf[:, :], start=True, stop=True, sem=sT)
                p.wait("vector", sT, sT.n)
                p.emit("vector", "tensor_copy", sig_bc[:, :], p_pr[:, 0:1], sem=sV)
                sigbc_done = sV.n
                p.wait("scalar", sW, wo_ld)
                p.wait("scalar", sV, sigbc_done)
                p.emit("scalar", "activation", wot_sb[:, 0, :], wot_sb[:, 0, :], AF.Identity, bias=c_zero[:, :], scale=sig_bc[:, 0:1])
                p.emit("scalar", "activation", wot_sb[:, 1, :], wot_sb[:, 1, :], AF.Identity, bias=c_zero[:, :], scale=sig_bc[:, 0:1], sem=sS)
                wotg_done = sS.n

                SUM, SQS, MEAN, EX2, VAR, MSQ, RSTD, MS, NEGMS = range(9)
                for tt in range(NT):
                    p.wait("vector", sLd, a_total)
                    p.emit("vector", "tensor_reduce", stats[:, SUM, tt:tt + 1], x_tok[:, tt, :], axis=AX.X, op=OP.add, sem=sV)
                    p.wait("scalar", sLd, a_total)
                    p.emit("scalar", "activation", qT[:, 0, 0:H], x_tok[:, tt, :], AF.Square,
                           bias=c_zero[:, :], scale=1.0, accum_out=stats[:, SQS, tt:tt + 1], sem=sS)
                sum_done = sV.n
                p.wait("scalar", sV, sum_done)
                p.emit("scalar", "activation", stats[:, MEAN, :], stats[:, SUM, :], AF.Identity, bias=c_zero[:, :], scale=1.0 / H)
                p.emit("scalar", "activation", stats[:, EX2, :], stats[:, SQS, :], AF.Identity, bias=c_zero[:, :], scale=1.0 / H)
                p.emit("scalar", "activation", stats[:, MSQ, :], stats[:, MEAN, :], AF.Square, bias=c_zero[:, :], scale=1.0, sem=sS)
                p.wait("vector", sS, sS.n)
                p.emit("vector", "tensor_tensor", stats[:, VAR, :], stats[:, EX2, :], stats[:, MSQ, :], OP.subtract, sem=sV)
                p.wait("scalar", sV, sV.n)
                p.emit("scalar", "activation", stats[:, RSTD, :], stats[:, VAR, :], AF.Ln, bias=c_eps[:, :], scale=1.0)
                p.emit("scalar", "activation", stats[:, RSTD, :], stats[:, RSTD, :], AF.Exp, bias=c_zero[:, :], scale=-0.5, sem=sS)
                p.wait("vector", sS, sS.n)
                p.emit("vector", "tensor_tensor", stats[:, MS, :], stats[:, MEAN, :], stats[:, RSTD, :], OP.mult, sem=sV)
                p.wait("scalar", sV, sV.n)
                p.emit("scalar", "activation", stats[:, NEGMS, :], stats[:, MS, :], AF.Identity, bias=c_zero[:, :], scale=-1.0, sem=sS)
                negms_done = sS.n

                ping_ev = [0, 0]
                aff_free = [0, 0]
                for tt in range(NT):
                    pi = tt % 2
                    p.wait("scalar", sS, negms_done)
                    p.wait("scalar", sT, aff_free[pi])
                    p.emit("scalar", "activation", xa[:, pi, :], x_tok[:, tt, :], AF.Identity,
                           bias=stats[:, NEGMS, tt:tt + 1], scale=stats[:, RSTD, tt:tt + 1], sem=sS)
                    a_n = sS.n
                    for h in range(HC):
                        bi = h % 2
                        p.wait("tensor", sS, a_n)
                        p.wait("tensor", sV, ping_ev[bi])
                        if tt == 0 and h == 0:
                            p.wait("tensor", sLd, a_total)
                        p.emit("tensor", "transpose", p_tr[:, bi, 0:128], xa[:, pi, h * 128:(h + 1) * 128], ident[:, :], sem=sT)
                        p.wait("vector", sT, sT.n)
                        p.emit("vector", "tensor_copy", xln1[:, h, tt * 128:(tt + 1) * 128], p_tr[:, bi, 0:128], sem=sV)
                        ping_ev[bi] = sV.n
                    aff_free[pi] = sT.n
                xln1_done_v = sV.n

                proj_ping_v = [sV.n, sV.n]
                pp = 0
                for (wsb, dst) in ((wq_sb, qT), (wk_sb, kT)):
                    for mt in range(2):
                        for tb4 in range(4):
                            half = pp % 2
                            p.wait("tensor", sV, proj_ping_v[half])
                            if pp == 0:
                                p.wait("tensor", sV, xln1_done_v)
                                p.wait("tensor", sW, wv_ld)
                            for h in range(HC):
                                p.emit("tensor", "matmul", p_pr[:, half * 512:half * 512 + 512],
                                       wsb[:, h, mt * 128:(mt + 1) * 128], xln1[:, h, tb4 * 512:(tb4 + 1) * 512],
                                       start=(h == 0), stop=(h == HC - 1), sem=(sT if h == HC - 1 else None))
                            p.wait("vector", sT, sT.n)
                            p.emit("vector", "tensor_copy", dst[:, mt, tb4 * 512:(tb4 + 1) * 512],
                                   p_pr[:, half * 512:half * 512 + 512], sem=sV)
                            proj_ping_v[half] = sV.n
                            pp += 1
                qk_done_v = sV.n
                proj_ping_s = [0, 0]
                for tt in range(NT):
                    for (wsb, dst, scale_ap) in ((wk_sb, ktok, lamk_sb[:, 0:1]), (wv_sb, vtok, None)):
                        half = pp % 2
                        p.wait("tensor", sS, proj_ping_s[half])
                        p.wait("tensor", sV, proj_ping_v[half])
                        for h in range(HC):
                            p.emit("tensor", "matmul", p_pr[:, half * 512:half * 512 + 256],
                                   xln1[:, h, tt * 128:(tt + 1) * 128], wsb[:, h, :],
                                   start=(h == 0), stop=(h == HC - 1), sem=(sT if h == HC - 1 else None))
                        p.wait("scalar", sT, sT.n)
                        if tt == 0 and scale_ap is not None:
                            p.wait("scalar", sLd, a_total)
                        p.emit("scalar", "activation", dst[:, tt, :], p_pr[:, half * 512:half * 512 + 256],
                               AF.Identity, bias=c_zero[:, :],
                               scale=(scale_ap if scale_ap is not None else 1.0), sem=sS)
                        proj_ping_s[half] = sS.n
                        pp += 1
                kv_done_s = sS.n

                p.wait("tensor", sS, kv_done_s)
                da_ping = [0, 0]
                for cl in range(NCH):
                    for mk in range(2):
                        reg = (0, 256) if mk == 0 else (512, 768)
                        p.wait("tensor", sV, da_ping[mk])
                        for bp in range(B):
                            tt = bp * NCH + cl
                            p.emit("tensor", "matmul", p_pr[:, reg[0]:reg[1]],
                                   ktok[:, tt, mk * 128:(mk + 1) * 128], vtok[:, tt, :],
                                   start=(bp == 0), stop=(bp == B - 1), sem=(sT if bp == B - 1 else None))
                        p.wait("vector", sT, sT.n)
                        p.emit("vector", "tensor_copy", dA[:, cl, mk, :], p_pr[:, reg[0]:reg[1]], sem=sV)
                        da_ping[mk] = sV.n
                p.emit("vector", "tensor_copy", Zw[:, :, :], dA[:, 0, :, :], sem=sV)
                for cl in range(1, NCH):
                    p.wait("scalar", sV, sV.n)
                    p.emit("scalar", "activation", Zw2[:, :, :], Zw[:, :, :], AF.Identity, bias=c_zero[:, :], scale=LAM128, sem=sS)
                    p.wait("vector", sS, sS.n)
                    p.emit("vector", "tensor_tensor", Zw[:, :, :], Zw2[:, :, :], dA[:, cl, :, :], OP.add, sem=sV)
                z_done_v = sV.n

                p.wait("gpsimd", sV, z_done_v)
                p.emit("gpsimd", "dma_start", sem=sW, k=16, out=agin.rearrange("(c p) m -> p c m", p=128), in_=Zw[:, :, :])
                p.wait("gpsimd", sW, sW.n)
                p.emit("gpsimd", "collective_compute", "AllGather", OP.bypass,
                       replica_groups=[list(range(NCORE))], ins=[agin.opt()], outs=[agout.opt()], sem=sC)
                p.wait("gpsimd", sC, sC.n)
                zall_ld = p.emit("gpsimd", "dma_start", sem=sW, k=16, out=Zall[:, :, :],
                                 in_=agout.rearrange("(c p) m -> p c m", p=128))

                p.wait("scalar", sW, zall_ld)
                p.wait("scalar", sLd, a_total)
                p.emit("scalar", "activation", Ast[:, :, :], m0t_sb[:, :, :], AF.Identity, bias=c_zero[:, :],
                       scale=prefw_sb[:, 8:9], sem=sS)
                for k in range(NCORE):
                    p.emit("scalar", "activation", Zw2[:, :, :], Zall[:, 2 * k:2 * k + 2, :], AF.Identity,
                           bias=c_zero[:, :], scale=prefw_sb[:, k:k + 1], sem=sS)
                    p.wait("vector", sS, sS.n)
                    p.emit("vector", "tensor_tensor", Ast[:, :, :], Zw2[:, :, :], Ast[:, :, :], OP.add, sem=sV)
                    p.wait("scalar", sV, sV.n)
                p.emit("vector", "tensor_copy", AstBf[:, :, :], Ast[:, :, :], sem=sV)
                astbf_done = sV.n

                s_ping = [sV.n, sV.n]
                for cl in range(NCH):
                    stb_ev = {}
                    for bp in range(B):
                        half = bp % 2
                        p.wait("tensor", sV, s_ping[half])
                        if cl == 0 and bp == 0:
                            p.wait("tensor", sV, qk_done_v)
                        for mk in range(2):
                            p.emit("tensor", "matmul", p_s[:, half * 512:half * 512 + 512],
                                   kT[:, mk, bp * SC + cl * 128: bp * SC + cl * 128 + 128],
                                   qT[:, mk, :].rearrange("p (b s) -> p b s", b=B)[:, :, cl * 128:(cl + 1) * 128],
                                   start=(mk == 0), stop=(mk == 1), sem=(sT if mk == 1 else None))
                        p.wait("vector", sT, sT.n)
                        if cl == 0 and bp == 0:
                            p.wait("vector", sLd, a_total)
                            p.wait("vector", sLd, a_total)
                        p.emit("vector", "tensor_tensor", sTb[:, bp, :],
                               p_s[:, half * 512:half * 512 + 512], mask_sb[:, :], OP.mult, sem=sV)
                        stb_ev[bp] = sV.n
                        s_ping[half] = sV.n
                    for mk in range(2):
                        p.emit("vector", "tensor_tensor",
                               qs[:, mk, :].rearrange("p (b s) -> p b s", b=B),
                               qT[:, mk, :].rearrange("p (b s) -> p b s", b=B)[:, :, cl * 128:(cl + 1) * 128],
                               dq_sb.rearrange("p (b s) -> p b s", b=B), OP.mult, sem=sV)
                    qs_done = sV.n
                    for mv in range(2):
                        p.wait("tensor", sV, qs_done)
                        p.wait("tensor", sV, astbf_done)
                        p.emit("tensor", "matmul", p_r[:, mv * 512:mv * 512 + 512],
                               AstBf[:, 0, mv * 128:(mv + 1) * 128], qs[:, 0, :], start=True, stop=False)
                        p.emit("tensor", "matmul", p_r[:, mv * 512:mv * 512 + 512],
                               AstBf[:, 1, mv * 128:(mv + 1) * 128], qs[:, 1, :], start=False, stop=False)
                        for bp in range(B):
                            tt = bp * NCH + cl
                            p.wait("tensor", sV, stb_ev[bp])
                            p.emit("tensor", "matmul", p_r[:, mv * 512:mv * 512 + 512],
                                   vtok[:, tt, mv * 128:(mv + 1) * 128], sTb[:, bp, :],
                                   start=False, stop=(bp == B - 1), sem=(sT if bp == B - 1 else None))
                        p.wait("vector", sT, sT.n)
                        p.emit("vector", "tensor_copy",
                               rT[:, mv, :].rearrange("p (b s) -> p b s", b=B)[:, :, cl * 128:(cl + 1) * 128],
                               p_r[:, mv * 512:mv * 512 + 512].rearrange("p (b s) -> p b s", b=B), sem=sV)
                    if cl < NCH - 1:
                        p.wait("scalar", sT, sT.n)
                        p.emit("scalar", "activation", Zw2[:, :, :], Ast[:, :, :], AF.Identity, bias=c_zero[:, :],
                               scale=LAM128, sem=sS)
                        p.wait("vector", sS, sS.n)
                        p.emit("vector", "tensor_tensor", Ast[:, :, :], Zw2[:, :, :], dA[:, cl, :, :], OP.add, sem=sV)
                        p.emit("vector", "tensor_copy", AstBf[:, :, :], Ast[:, :, :], sem=sV)
                        astbf_done = sV.n
                r_done_v = sV.n

                x1_add = {}
                for tt in range(NT):
                    if tt == 0:
                        p.wait("tensor", sV, r_done_v)
                        p.wait("tensor", sS, wotg_done)
                    else:
                        p.wait("tensor", sV, x1_add[tt - 1])
                    for half in range(2):
                        for mt in range(2):
                            p.emit("tensor", "matmul", p_pr[:, half * 512:half * 512 + 512],
                                   rT[:, mt, tt * 128:(tt + 1) * 128], wot_sb[:, mt, half * 512:half * 512 + 512],
                                   start=(mt == 0), stop=(mt == 1),
                                   sem=(sT if (half == 1 and mt == 1) else None))
                    p.wait("vector", sT, sT.n)
                    p.emit("vector", "tensor_tensor", x_tok[:, tt, :], p_pr[:, 0:1024], x_tok[:, tt, :], OP.add, sem=sV)
                    x1_add[tt] = sV.n
                x1_done = sV.n
                p.flush(blkA)

        with ExitStack() as ctxB:
            EB = ctxB.enter_context
            stats2 = EB(nc.sbuf_tensor("stats2", [128, 9, NT], F32))
            wr_sb = EB(nc.sbuf_tensor("wr_sb", [128, HC, NEXP], BF16))
            br_r = EB(nc.sbuf_tensor("br_r", [1, NEXP], BF16))
            probs = EB(nc.sbuf_tensor("probs", [128, NT, NEXP], F32))
            expt = EB(nc.sbuf_tensor("expt", [128, NEXP], F32))
            rstat = EB(nc.sbuf_tensor("rstat", [128, 5, NT], F32))
            selm = EB(nc.sbuf_tensor("selm", [128, NT, NEXP], BF16))
            tmp_epi = EB(nc.sbuf_tensor("tmp_epi", [128, H], F32))
            tok16_sb = EB(nc.sbuf_tensor("tok16_sb", [128, NT], I16))
            tok32_sb = EB(nc.sbuf_tensor("tok32_sb", [128, NT], U32))
            ones128 = EB(nc.sbuf_tensor("ones128", [128, 1], BF16))
            b1c = EB(nc.sbuf_tensor("b1c", [128, FC], F32))
            b2r = EB(nc.sbuf_tensor("b2r", [1, H], BF16))
            w1b = EB(nc.sbuf_tensor("w1b", [128, HC, 2048], BF16))
            w2b = EB(nc.sbuf_tensor("w2b", [128, FH, H], BF16))
            p_yA = EB(nc.psum_tensor("p_yA", [128, 1024], F32))
            p_yB = EB(nc.psum_tensor("p_yB", [128, 1024], F32))
            p_h = EB(nc.psum_tensor("p_h", [128, 1024], F32))
            p_rt = EB(nc.psum_tensor("p_rt", [128, 512], F32))
            p_tr2 = EB(nc.psum_tensor("p_tr2", [128, 1024], BF16))

            with nc.Block("Bf") as blkB:
                SUM, SQS, MEAN, EX2, VAR, MSQ, RSTD, MS, NEGMS = range(9)
                MX, SE, LNSE, RINV, GMX = range(5)

                regbox = {}

                def _mkregs(g):
                    r1 = g.register("bnd_slot").__enter__()
                    g.reg_mov(r1, NSLOT - 1)
                    r2 = g.register("bnd_tok").__enter__()
                    g.reg_mov(r2, T - 1)
                    regbox["slot"] = r1
                    regbox["tok"] = r2
                p.add("gpsimd", _mkregs)

                def emit_indirect(out_ap, idx_ap, in_ap, bkey, sem, k=16):
                    sem.n += k
                    def f(g, out_ap=out_ap, idx_ap=idx_ap, in_ap=in_ap, bkey=bkey, sh=sem.h, kk=k):
                        g.indirect_dma_start(out_ap, bass.IndirectOffsetOnAxis(ap=idx_ap, axis=0),
                                             in_ap, None, bounds_check=regbox[bkey],
                                             oob_is_err=False).then_inc(sh, kk)
                    p.add("gpsimd", f)
                    return sem.n

                wr_ld = p.emit("gpsimd", "dma_start", sem=sW, k=16, out=wr_sb[:, :, :],
                               in_=wr_d.rearrange("(c p) e -> p c e", p=128))
                br_ld = p.emit("gpsimd", "dma_start", sem=sW, k=16, out=br_r[:, :], in_=br_d)
                tk16_ld = p.emit("gpsimd", "dma_start", sem=sW, k=16, out=tok16_sb[:, :], in_=tok16_d)
                tk32_ld = p.emit("gpsimd", "dma_start", sem=sW, k=16, out=tok32_sb[:, :], in_=tok32_d)
                p.emit("gpsimd", "memset", ones128[:, :], 1.0, sem=sG)

                w1_thr = {}
                w2_thr = {}
                b_thr = {}

                def emit_weight_loads(e, hf, w1_free, w2_free, b_free):
                    for qp in range(2):
                        if w1_free:
                            p.wait("gpsimd", sT, w1_free)
                        w1_thr[(e, hf, qp)] = p.emit(
                            "gpsimd", "dma_start", sem=sW, k=16,
                            out=w1b[:, :, qp * 1024:(qp + 1) * 1024],
                            in_=w1_d[e].rearrange("(c p) f -> p c f", p=128)[:, :, hf * 2048 + qp * 1024: hf * 2048 + (qp + 1) * 1024])
                    for qp in range(2):
                        if w2_free:
                            p.wait("gpsimd", sT, w2_free)
                        w2_thr[(e, hf, qp)] = p.emit(
                            "gpsimd", "dma_start", sem=sW, k=16,
                            out=w2b[:, 8 * qp:8 * (qp + 1), :],
                            in_=w2_d[e].rearrange("(c p) h -> p c h", p=128)[:, hf * FH + 8 * qp: hf * FH + 8 * (qp + 1), :])
                    if hf == 0:
                        if b_free is not None:
                            p.wait("gpsimd", sS, b_free[0])
                            p.wait("gpsimd", sV, b_free[1])
                            p.wait("gpsimd", sT, b_free[2])
                        b_thr[e] = (
                            p.emit("gpsimd", "dma_start", sem=sW, k=16, out=b1c[:, :], in_=b1_d[e]),
                            p.emit("gpsimd", "dma_start", sem=sW, k=16, out=b2r[:, :], in_=b2_d[e:e + 1, :]),
                        )

                emit_weight_loads(0, 0, 0, 0, None)

                p.emit("vector", "memset", tmp_epi[:, :], 0.0, sem=sV)
                zv = sV.n
                p.wait("sync", sV, zv)
                ysz = []
                for k in range(NT):
                    ysz.append(p.emit("sync", "dma_start", sem=sO, k=16,
                                      out=yscr.rearrange("(i p) h -> p i h", p=128)[:, k:k + 1, :],
                                      in_=tmp_epi[:, :].rearrange("p (i h) -> p i h", i=1)))
                p.wait("gpsimd", sV, zv)
                xpz = p.emit("gpsimd", "dma_start", sem=sW, k=16,
                             out=xlt_d[T:T + 16, :], in_=tmp_epi[0:16, 0:H])
                yz_done = sO.n

                with ExitStack() as ctxR:
                    ER = ctxR.enter_context
                    xln2 = ER(nc.sbuf_tensor("xln2", [128, HC, T], BF16))
                    xa2 = ER(nc.sbuf_tensor("xa2", [128, 2, H], BF16))
                    sqs_scr = ER(nc.sbuf_tensor("sqs_scr", [128, H], F32))
                    Lst_sb = ER(nc.sbuf_tensor("Lst_sb", [128, 128], BF16))
                    Lti_sb = ER(nc.sbuf_tensor("Lti_sb", [128, 128], BF16))
                    eoff_sb = ER(nc.sbuf_tensor("eoff_sb", [128, 128], F32))
                    rnk = ER(nc.sbuf_tensor("rnk", [128, 128], F32))
                    cnt_bf = ER(nc.sbuf_tensor("cnt_bf", [128, 1], BF16))
                    orow16 = ER(nc.sbuf_tensor("orow16", [1, 128], mybir.dt.float16))
                    ocol = ER(nc.sbuf_tensor("ocol", [1, 128], mybir.dt.float16))
                    obc = ER(nc.sbuf_tensor("obc", [128, 128], F32))
                    slotf = ER(nc.sbuf_tensor("slotf", [128, 128], F32))
                    slotg = ER(nc.sbuf_tensor("slotg", [128, NT], F32))
                    slotu = ER(nc.sbuf_tensor("slotu", [128, NT], U32))
                    pr16 = ER(nc.sbuf_tensor("pr16", [128, 32], I16))
                    pr32 = ER(nc.sbuf_tensor("pr32", [128, 32], U32))
                    identf_sb = ER(nc.sbuf_tensor("identf_sb", [128, 128], F32))
                    cntrow = ER(nc.sbuf_tensor("cntrow", [1, 128], F32))
                    ovf = ER(nc.sbuf_tensor("ovf", [128, 128], F32))
                    m8 = ER(nc.sbuf_tensor("m8", [128, 128], F32))
                    ri32 = ER(nc.sbuf_tensor("ri32", [128, 128], mybir.dt.int32))
                    pi2 = ER(nc.sbuf_tensor("pi2", [128, 128], F32))
                    slotg2 = ER(nc.sbuf_tensor("slotg2", [128, NT], F32))
                    slotu2 = ER(nc.sbuf_tensor("slotu2", [128, NT], U32))

                    idf_ld = p.emit("sync", "dma_start", sem=sL2, k=16, out=identf_sb[:, :], in_=identf_d)
                    lst_ld = p.emit("sync", "dma_start", sem=sL2, k=16, out=Lst_sb[:, :], in_=lstrict_d)
                    lti_ld = p.emit("sync", "dma_start", sem=sL2, k=16, out=Lti_sb[:, :], in_=ltile_d)
                    eo_ld = p.emit("sync", "dma_start", sem=sL2, k=16, out=eoff_sb[:, :], in_=eoff_d)
                    b_total = sL2.n

                    for tt in range(NT):
                        if tt == 0:
                            p.wait("vector", sV, x1_done)
                        p.emit("vector", "tensor_reduce", stats2[:, SUM, tt:tt + 1], x_tok[:, tt, :], axis=AX.X, op=OP.add, sem=sV)
                        p.emit("scalar", "activation", sqs_scr[:, :], x_tok[:, tt, :], AF.Square,
                               bias=c_zero[:, :], scale=1.0, accum_out=stats2[:, SQS, tt:tt + 1], sem=sS)
                    p.wait("scalar", sV, sV.n)
                    p.emit("scalar", "activation", stats2[:, MEAN, :], stats2[:, SUM, :], AF.Identity, bias=c_zero[:, :], scale=1.0 / H)
                    p.emit("scalar", "activation", stats2[:, EX2, :], stats2[:, SQS, :], AF.Identity, bias=c_zero[:, :], scale=1.0 / H)
                    p.emit("scalar", "activation", stats2[:, MSQ, :], stats2[:, MEAN, :], AF.Square, bias=c_zero[:, :], scale=1.0, sem=sS)
                    p.wait("vector", sS, sS.n)
                    p.emit("vector", "tensor_tensor", stats2[:, VAR, :], stats2[:, EX2, :], stats2[:, MSQ, :], OP.subtract, sem=sV)
                    p.wait("scalar", sV, sV.n)
                    p.emit("scalar", "activation", stats2[:, RSTD, :], stats2[:, VAR, :], AF.Ln, bias=c_eps[:, :], scale=1.0)
                    p.emit("scalar", "activation", stats2[:, RSTD, :], stats2[:, RSTD, :], AF.Exp, bias=c_zero[:, :], scale=-0.5, sem=sS)
                    p.wait("vector", sS, sS.n)
                    p.emit("vector", "tensor_tensor", stats2[:, MS, :], stats2[:, MEAN, :], stats2[:, RSTD, :], OP.mult, sem=sV)
                    p.wait("scalar", sV, sV.n)
                    p.emit("scalar", "activation", stats2[:, NEGMS, :], stats2[:, MS, :], AF.Identity, bias=c_zero[:, :], scale=-1.0, sem=sS)
                    negms2 = sS.n

                    tr_free = 0
                    aff2_free = {0: 0, 1: 0}
                    xs_done = []
                    for tt in range(NT):
                        pi = tt % 2
                        p.wait("scalar", sS, negms2)
                        p.wait("scalar", sT, aff2_free[pi])
                        if tt >= 2:
                            p.wait("scalar", sO, xst[tt - 2])
                        p.emit("scalar", "activation", xa2[:, pi, :], x_tok[:, tt, :], AF.Identity,
                               bias=stats2[:, NEGMS, tt:tt + 1], scale=stats2[:, RSTD, tt:tt + 1], sem=sS)
                        a_n = sS.n
                        if tt == 0:
                            xst = {}
                        p.wait("sync", sS, a_n)
                        xst[tt] = p.emit("sync", "dma_start", sem=sO, k=16,
                                         out=xlt_d[0:T, :].rearrange("(i p) h -> p i h", p=128)[:, tt:tt + 1, :],
                                         in_=xa2[:, pi, :].rearrange("p (i h) -> p i h", i=1))
                        for h in range(HC):
                            p.wait("tensor", sS, a_n)
                            p.wait("tensor", sV, tr_free)
                            p.emit("tensor", "transpose", p_tr2[:, 0:128], xa2[:, pi, h * 128:(h + 1) * 128], ident[:, :], sem=sT)
                            p.wait("vector", sT, sT.n)
                            p.emit("vector", "tensor_copy", xln2[:, h, tt * 128:(tt + 1) * 128], p_tr2[:, 0:128], sem=sV)
                            tr_free = sV.n
                        aff2_free[pi] = sT.n
                    xln2_done = sV.n
                    xst_all = sO.n

                    rt_free = 0
                    for tt in range(NT):
                        if tt == 0:
                            p.wait("tensor", sV, xln2_done)
                            p.wait("tensor", sW, br_ld)
                        p.wait("tensor", sS, rt_free)
                        p.emit("tensor", "matmul", p_rt[:, 0:NEXP], ones_col[:, :], br_r[:, :], start=True, stop=False)
                        for h in range(HC):
                            p.emit("tensor", "matmul", p_rt[:, 0:NEXP], xln2[:, h, tt * 128:(tt + 1) * 128],
                                   wr_sb[:, h, :], start=False, stop=(h == HC - 1), sem=(sT if h == HC - 1 else None))
                        p.wait("vector", sT, sT.n)
                        p.emit("vector", "tensor_reduce", rstat[:, MX, tt:tt + 1], p_rt[:, 0:NEXP], axis=AX.X, op=OP.max, negate=True, sem=sV)
                        p.wait("scalar", sV, sV.n)
                        p.emit("scalar", "activation", expt[:, :], p_rt[:, 0:NEXP], AF.Exp,
                               bias=rstat[:, MX, tt:tt + 1], scale=1.0, accum_out=rstat[:, SE, tt:tt + 1], sem=sS)
                        rt_free = sS.n
                        p.emit("scalar", "activation", rstat[:, LNSE, tt:tt + 1], rstat[:, SE, tt:tt + 1], AF.Ln, bias=c_zero[:, :], scale=1.0)
                        p.emit("scalar", "activation", rstat[:, RINV, tt:tt + 1], rstat[:, LNSE, tt:tt + 1], AF.Exp, bias=c_zero[:, :], scale=-1.0)
                        p.emit("scalar", "activation", probs[:, tt, :], expt[:, :], AF.Identity, bias=c_zero[:, :],
                               scale=rstat[:, RINV, tt:tt + 1], sem=sS)
                        p.wait("vector", sS, sS.n)
                        p.emit("vector", "tensor_reduce", rstat[:, GMX, tt:tt + 1], probs[:, tt, :], axis=AX.X, op=OP.max, sem=sV)
                    for tt in range(NT):
                        for e in range(NEXP):
                            p.emit("vector", "tensor_tensor", selm[:, tt, e:e + 1], probs[:, tt, e:e + 1],
                                   rstat[:, GMX, tt:tt + 1], OP.is_ge, sem=(sV if e == NEXP - 1 else None))
                    sel_done = sV.n

                    p.wait("tensor", sV, sel_done)
                    p.wait("tensor", sS, rt_free)
                    p.wait("tensor", sL2, b_total)
                    p.emit("tensor", "matmul", p_rt[:, 0:128], Lst_sb[:, :],
                           selm[:, :, :].rearrange("p t e -> p (t e)"), start=True, stop=True, sem=sT)
                    p.wait("vector", sT, sT.n)
                    p.emit("vector", "tensor_copy", rnk[:, :], p_rt[:, 0:128], sem=sV)
                    p.emit("tensor", "matmul", p_rt[0:1, 128:256], ones128[:, :],
                           selm[:, :, :].rearrange("p t e -> p (t e)"), start=True, stop=True, sem=sT)
                    cnt_mm = sT.n
                    p.wait("vector", sT, cnt_mm)
                    p.emit("vector", "tensor_copy", cntrow[:, :], p_rt[0:1, 128:256], sem=sV)
                    p.wait("tensor", sV, sV.n)
                    p.wait("tensor", sL2, b_total)
                    p.emit("tensor", "transpose", p_h[:, 0:1], cntrow[:, :], identf_sb[0:1, 0:1], sem=sT)
                    ct_mm = sT.n
                    p.wait("vector", sT, ct_mm)
                    p.emit("vector", "tensor_copy", cnt_bf[:, :], p_h[:, 0:1], sem=sV)
                    p.wait("tensor", sV, sV.n)
                    p.emit("tensor", "matmul", p_rt[0:1, 384:512], cnt_bf[:, :], Lti_sb[:, :], start=True, stop=True, sem=sT)
                    p.wait("vector", sT, sT.n)
                    p.emit("vector", "tensor_copy", orow16[:, :], p_rt[0:1, 384:512], sem=sV)
                    p.wait("tensor", sV, sV.n)
                    p.emit("tensor", "matmul", p_h[:, 128:256], ones_col[:, :], orow16[:, :], start=True, stop=True, sem=sT)
                    p.wait("vector", sT, sT.n)
                    p.emit("vector", "tensor_copy", obc[:, :], p_h[:, 128:256], sem=sV)
                    p.emit("vector", "tensor_tensor", slotf[:, :], rnk[:, :], obc[:, :], OP.add)
                    p.emit("vector", "tensor_scalar", ovf[:, :], slotf[:, :], float(CAP) - 0.5,
                           65536.0, OP.is_ge, OP.mult)
                    p.emit("vector", "tensor_scalar", pi2[:, :], slotf[:, :], -3.5, 0.125, OP.add, OP.mult)
                    p.emit("vector", "tensor_copy", ri32[:, :], pi2[:, :])
                    p.emit("vector", "tensor_copy", m8[:, :], ri32[:, :])
                    p.emit("vector", "tensor_scalar", pi2[:, :], m8[:, :], -8.0, 0.0, OP.mult, OP.add)
                    p.emit("vector", "tensor_tensor", pi2[:, :], pi2[:, :], slotf[:, :], OP.add)
                    p.emit("vector", "tensor_scalar", pi2[:, :], pi2[:, :], float(16 * (CAP // 128)), 0.0, OP.mult, OP.add)
                    p.emit("vector", "tensor_tensor", pi2[:, :], pi2[:, :], m8[:, :], OP.add)
                    p.emit("vector", "tensor_tensor", pi2[:, :], pi2[:, :], eoff_sb[:, :], OP.add)
                    p.emit("vector", "tensor_tensor", pi2[:, :], pi2[:, :], ovf[:, :], OP.add)
                    p.emit("vector", "tensor_tensor", pi2[:, :], pi2[:, :], selm[:, :, :].rearrange("p t e -> p (t e)"), OP.mult)
                    p.emit("vector", "tensor_reduce", slotg2[:, :],
                           pi2[:, :].rearrange("p (t e) -> p t e", e=NEXP), axis=AX.X, op=OP.add, sem=sV)
                    p.emit("vector", "tensor_tensor", slotf[:, :], slotf[:, :], eoff_sb[:, :], OP.add)
                    p.emit("vector", "tensor_tensor", slotf[:, :], slotf[:, :], ovf[:, :], OP.add)
                    p.emit("vector", "tensor_tensor", slotf[:, :], slotf[:, :], selm[:, :, :].rearrange("p t e -> p (t e)"), OP.mult)
                    p.emit("vector", "tensor_reduce", slotg[:, :],
                           slotf[:, :].rearrange("p (t e) -> p t e", e=NEXP), axis=AX.X, op=OP.add, sem=sV)
                    p.wait("vector", sV, sV.n)
                    p.emit("vector", "tensor_copy", slotu2[:, :], slotg2[:, :])
                    p.emit("vector", "tensor_copy", slotu[:, :], slotg[:, :], sem=sV)
                    slot_done = sV.n

                    p.emit("gpsimd", "memset", pr16[:, :], float(T), sem=sG)
                    p.emit("gpsimd", "memset", pr32[:, :], float(T), sem=sG)
                    p.emit("gpsimd", "dma_start", sem=sW, k=16,
                           out=inv16.rearrange("(p f) one -> p (f one)", p=128), in_=pr16[:, :])
                    p.emit("gpsimd", "dma_start", sem=sW, k=16,
                           out=inv32.rearrange("(p f) one -> p (f one)", p=128), in_=pr32[:, :])
                    p.wait("gpsimd", sW, sW.n)
                    p.wait("gpsimd", sV, slot_done)
                    for tt in range(NT):
                        emit_indirect(inv16, slotu[:, tt:tt + 1], tok16_sb[:, tt:tt + 1], "slot", sW)
                        emit_indirect(inv32, slotu2[:, tt:tt + 1], tok32_sb[:, tt:tt + 1], "slot", sW)
                    inv_done = sW.n
                    rank_T = sT.n
                    rank_V = sV.n
                    rank_S = sS.n

                C32 = CAP // 16
                NST = CAP // 128
                h1 = EB(nc.sbuf_tensor("h1", [128, FH, CAP], BF16))
                Xs0 = EB(nc.sbuf_tensor("Xs0", [128, HC, CAP], BF16))
                Xs1 = EB(nc.sbuf_tensor("Xs1", [128, HC, CAP], BF16))
                XsB = [Xs0, Xs1]
                Ys = EB(nc.sbuf_tensor("Ys", [128, NST, H], F32))
                g16 = EB(nc.sbuf_tensor("g16", [128, C32], I16))
                s32a = EB(nc.sbuf_tensor("s32a", [128, NST], U32))
                s32b = EB(nc.sbuf_tensor("s32b", [128, NST], U32))
                s32B = [s32a, s32b]

                idx_thr = {}
                xs_ld_by = {}
                s32_ld_by = {}
                xs_free_by = {}
                scat_done = {}

                def emit_idx_sync(e):
                    p.wait("gpsimd", sW, inv_done)
                    if e >= 1:
                        p.wait("gpsimd", sW, xs_ld_by[e - 1])
                    t = p.emit("gpsimd", "dma_start", sem=sW, k=16, out=g16[0:16, :],
                               in_=bass.AP(inv16.tensor, e * CAP, [[C32, 16], [1, C32]]))
                    p.wait("gpsimd", sW, t)
                    t = p.emit("gpsimd", "dma_start", sem=sW, k=16, out=g16[16:32, :], in_=g16[0:16, :])
                    p.wait("gpsimd", sW, t)
                    t = p.emit("gpsimd", "dma_start", sem=sW, k=16, out=g16[32:64, :], in_=g16[0:32, :])
                    p.wait("gpsimd", sW, t)
                    g16rep = p.emit("gpsimd", "dma_start", sem=sW, k=16, out=g16[64:128, :], in_=g16[0:64, :])
                    idx_thr[e] = g16rep

                def emit_gather(e):
                    p.wait("gpsimd", sW, idx_thr[e])
                    if e == 0:
                        p.wait("gpsimd", sT, rank_T)
                        p.wait("gpsimd", sV, rank_V)
                        p.wait("gpsimd", sS, rank_S)
                        p.wait("gpsimd", sO, xst_all)
                        p.wait("gpsimd", sW, inv_done)
                    elif e >= 2:
                        p.wait("gpsimd", sT, xs_free_by[e - 2])
                        p.wait("gpsimd", sW, scat_done[e - 2])
                    xs_ld_by[e] = p.emit("gpsimd", "dma_gather", XsB[e % 2][:, :, :], xlt_d,
                                         g16[:, :], CAP, CAP, H, transpose=True, sem=sW, k=16)
                    s32_ld_by[e] = p.emit("gpsimd", "dma_start", sem=sW, k=16, out=s32B[e % 2][:, :],
                                          in_=bass.AP(inv32.tensor, e * CAP, [[NST, 128], [1, NST]]))

                emit_idx_sync(0)
                emit_gather(0)

                h_free = [(sS, 0), (sV, 0)]
                ys_scat = {}
                ys_evac = {}
                for e in range(NEXP):
                    Xs = XsB[e % 2]
                    xs_ld = xs_ld_by[e]
                    s32 = s32B[e % 2]
                    for hf in range(2):
                        w1_free = 0
                        w2_free = 0
                        fgrp = 0
                        for f in range(FH):
                            hb = fgrp % 2
                            reg = (0, 512) if hb == 0 else (512, 1024)
                            if f % 8 == 0:
                                p.wait("tensor", sW, w1_thr[(e, hf, f // 8)])
                            if f == 0:
                                p.wait("tensor", sW, xs_ld)
                                p.wait("tensor", sW, b_thr[e][0])
                            semref, cnt = h_free[hb]
                            p.wait("tensor", semref, cnt)
                            for h in range(HC):
                                p.emit("tensor", "matmul", p_h[:, reg[0]:reg[1]],
                                       w1b[:, h, f * 128:(f + 1) * 128], Xs[:, h, :],
                                       start=(h == 0), stop=(h == HC - 1), sem=(sT if h == HC - 1 else None))
                            g_n = sT.n
                            if f == FH - 1:
                                w1_free = g_n
                                if hf == 1:
                                    xs_free_by[e] = g_n
                            if hb == 0:
                                p.wait("scalar", sT, g_n)
                                p.emit("scalar", "activation", h1[:, f, :], p_h[:, reg[0]:reg[1]], AF.Relu,
                                       bias=b1c[:, hf * FH + f:hf * FH + f + 1], scale=1.0, sem=sS)
                                h_free[0] = (sS, sS.n)
                            else:
                                p.wait("vector", sT, g_n)
                                p.emit("vector", "tensor_scalar", h1[:, f, :], p_h[:, reg[0]:reg[1]],
                                       b1c[:, hf * FH + f:hf * FH + f + 1], 0.0, OP.add, OP.max, sem=sV)
                                h_free[1] = (sV, sV.n)
                            fgrp += 1
                        h1_s = sS.n
                        h1_v = sV.n
                        for st in range(CAP // 128):
                            yp = p_yA if (st % 2 == 0) else p_yB
                            key = (e, hf, st)
                            prevk = None
                            if st >= 2:
                                prevk = (e, hf, st - 2)
                            elif hf == 1:
                                prevk = (e, 0, st + 2)
                            elif e >= 1:
                                prevk = (e - 1, 1, st + 2)
                            if prevk is not None and prevk in ys_evac:
                                eng, cnt = ys_evac[prevk]
                                p.wait("tensor", eng, cnt)
                            p.wait("tensor", sS, h1_s)
                            p.wait("tensor", sV, h1_v)
                            if hf == 0:
                                p.wait("tensor", sW, b_thr[e][1])
                                p.emit("tensor", "matmul", yp[:, 0:512], ones_col[:, :], b2r[:, 0:512], start=True, stop=False)
                                p.emit("tensor", "matmul", yp[:, 512:1024], ones_col[:, :], b2r[:, 512:1024], start=True, stop=False)
                            for f in range(FH):
                                if st == 0 and f % 8 == 0:
                                    p.wait("tensor", sW, w2_thr[(e, hf, f // 8)])
                                st0 = (hf == 1 and f == 0)
                                lastf = (f == FH - 1)
                                p.emit("tensor", "matmul", yp[:, 0:512], h1[:, f, st * 128:(st + 1) * 128],
                                       w2b[:, f, 0:512], start=st0, stop=lastf)
                                p.emit("tensor", "matmul", yp[:, 512:1024], h1[:, f, st * 128:(st + 1) * 128],
                                       w2b[:, f, 512:1024], start=st0, stop=lastf,
                                       sem=(sT if lastf else None))
                            mm2_n = sT.n
                            if st == CAP // 128 - 1:
                                w2_free = mm2_n
                            if hf == 0:
                                p.wait("scalar", sT, mm2_n)
                                if e >= 1:
                                    p.wait("scalar", sW, ys_scat[(e - 1, st)])
                                p.emit("scalar", "activation", Ys[:, st, :], yp[:, 0:1024], AF.Identity,
                                       bias=c_zero[:, :], scale=1.0, sem=sS)
                                ys_evac[key] = (sS, sS.n)
                            else:
                                p.wait("vector", sT, mm2_n)
                                p.emit("vector", "tensor_tensor", Ys[:, st, :], yp[:, 0:1024], Ys[:, st, :], OP.add, sem=sV)
                                ys_evac[key] = (sV, sV.n)
                        b_free = (sS.n, sV.n, sT.n)
                        if hf == 0:
                            emit_weight_loads(e, 1, w1_free, w2_free, b_free)
                            if e < NEXP - 1:
                                emit_idx_sync(e + 1)
                                emit_gather(e + 1)
                        elif e < NEXP - 1:
                            emit_weight_loads(e + 1, 0, w1_free, w2_free, b_free)
                    p.wait("gpsimd", sW, s32_ld_by[e])
                    p.wait("gpsimd", sO, yz_done)
                    for st in range(CAP // 128):
                        eng, cnt = ys_evac[(e, 1, st)]
                        p.wait("gpsimd", eng, cnt)
                        ys_scat[(e, st)] = emit_indirect(yscr, s32[:, st:st + 1], Ys[:, st, :], "tok", sW)
                    scat_done[e] = sW.n
                scat_all = sW.n

                yld = {}
                for k in range(4):
                    p.wait("sync", sW, scat_all)
                    if k >= 1:
                        p.wait("sync", sV, add_done)
                    yld[k] = p.emit("sync", "dma_start", sem=sL2, k=16, out=Ys[:, :, :],
                                    in_=yscr.rearrange("(i p) h -> p i h", p=128)[:, 4 * k:4 * (k + 1), :])
                    for i in range(4):
                        tt = 4 * k + i
                        p.wait("scalar", sL2, yld[k])
                        p.emit("scalar", "activation", Ys[:, i, :], Ys[:, i, :], AF.Identity,
                               bias=c_zero[:, :], scale=rstat[:, GMX, tt:tt + 1], sem=sS)
                        p.wait("vector", sS, sS.n)
                        p.emit("vector", "tensor_tensor", x_tok[:, tt, :], Ys[:, i, :], x_tok[:, tt, :], OP.add, sem=sV)
                    add_done = sV.n
                    p.wait("sync", sV, add_done)
                    p.emit("sync", "dma_start", sem=sO, k=16,
                           out=y_d.rearrange("(i p) h -> p i h", p=128)[:, 4 * k:4 * (k + 1), :],
                           in_=x_tok[:, 4 * k:4 * (k + 1), :])
                p.wait("sync", sO, sO.n)
                p.flush(blkB)
    import bass_rust as _bass_rust
    from concourse.library_config import all_libraries, standard
    lm = {}
    for lib in all_libraries:
        for it in lib.instructions:
            lm[it] = lm.get(it, 0) | (1 << lib.index)
    _bass_rust.insert_library_loads(nc, lm, len(all_libraries), standard.index)
    mybir.codegen_inst_isa_subclasses(nc)
    return nc


def _consts():
    lam = np.float64(DECAY)
    jl = np.arange(128)
    tl = np.arange(128)
    mk = np.where(jl[:, None] < tl[None, :], lam ** (tl[None, :] - 1 - jl[:, None]), 0.0) / B
    mask = np.concatenate([mk] * B, axis=1).astype(np.float32)
    dq = np.concatenate([np.broadcast_to(lam ** tl, (128, 128))] * B, axis=1).astype(np.float32)
    lamk = (lam ** (127 - jl) / B).astype(np.float32).reshape(128, 1)
    ident = np.eye(128, dtype=ml_dtypes.bfloat16)
    lstrict = np.triu(np.ones((128, 128)), 1).astype(ml_dtypes.bfloat16)
    tt_i = np.arange(128) // NEXP
    e_i = np.arange(128) % NEXP
    ltile = ((e_i[:, None] == e_i[None, :]) & (tt_i[:, None] < tt_i[None, :])).astype(ml_dtypes.bfloat16)
    eoff = np.broadcast_to((e_i * CAP).astype(np.float32), (128, 128)).copy()
    identf = np.eye(128, dtype=np.float32)
    tok16 = (np.arange(128)[:, None] + 128 * np.arange(NT)[None, :]).astype(np.int16)
    tok32 = tok16.astype(np.uint32)
    return mask, np.ascontiguousarray(dq), lamk, ident, lstrict, ltile, eoff, identf, tok16, tok32


_NC_CACHE = {}
LAST_RES = None


def kernel(L0_attn_x, L0_attn_Wq, L0_attn_Wk, L0_attn_Wv, L0_attn_Wo, L0_attn_gate, L0_attn_M0,
           L0_ffn_Wr, L0_ffn_br, L0_ffn_W1, L0_ffn_b1, L0_ffn_W2, L0_ffn_b2):
    x = np.asarray(L0_attn_x, dtype=np.float32)
    wqt = np.ascontiguousarray(np.asarray(L0_attn_Wq, np.float32).T)
    wkt = np.ascontiguousarray(np.asarray(L0_attn_Wk, np.float32).T)
    wvt = np.ascontiguousarray(np.asarray(L0_attn_Wv, np.float32).T)
    wot = np.ascontiguousarray(np.asarray(L0_attn_Wo, np.float32).T)
    gate = np.asarray(L0_attn_gate, np.float32).reshape(1, 1)
    m0t = np.ascontiguousarray(np.asarray(L0_attn_M0, np.float32).T)
    wr = np.ascontiguousarray(np.asarray(L0_ffn_Wr, np.float32))
    br = np.asarray(L0_ffn_br, np.float32).reshape(1, NEXP)
    w1 = np.ascontiguousarray(np.asarray(L0_ffn_W1, np.float32))
    b1 = np.ascontiguousarray(np.asarray(L0_ffn_b1, np.float32).reshape(NEXP, FC, 128).transpose(0, 2, 1))
    w2 = np.ascontiguousarray(np.asarray(L0_ffn_W2, np.float32))
    b2 = np.ascontiguousarray(np.asarray(L0_ffn_b2, np.float32))
    mask, dq, lamk, ident, lstrict, ltile, eoff, identf, tok16, tok32 = _consts()

    if "nc" not in _NC_CACHE:
        _NC_CACHE["nc"] = build()
    nc = _NC_CACHE["nc"]

    lam = np.float64(DECAY)
    in_maps = []
    for c in range(NCORE):
        xs = np.ascontiguousarray(x[:, c * SC:(c + 1) * SC, :]).reshape(T, H)
        w = np.zeros(9, np.float64)
        for k in range(c):
            w[k] = lam ** (SC * (c - 1 - k))
        w[8] = lam ** (SC * c)
        prefw = np.ascontiguousarray(np.tile(w.astype(np.float32)[None, :], (128, 1)))
        in_maps.append({
            "x": xs, "wqt": wqt, "wkt": wkt, "wvt": wvt, "wot": wot, "gate": gate, "m0t": m0t,
            "wr": wr, "br": br, "w1": w1, "b1": b1, "w2": w2, "b2": b2,
            "mask": mask, "dq": dq, "lamk": lamk, "prefw": prefw, "ident": ident,
            "lstrict": lstrict, "ltile": ltile, "eoff": eoff, "identf": identf,
            "tok16": tok16, "tok32": tok32,
        })
    res = run_bass_kernel_spmd(nc, in_maps, core_ids=list(range(NCORE)))
    global LAST_RES
    LAST_RES = res
    outs = [np.asarray(res.results[c]["y"], dtype=np.float32).reshape(B, SC, H) for c in range(NCORE)]
    return np.concatenate(outs, axis=1)


# version 08
# speedup vs baseline: 2.7391x; 1.0145x over previous
import numpy as np
import ml_dtypes
from contextlib import ExitStack

import concourse.bass as bass
import concourse.mybir as mybir
from concourse.alu_op_type import AluOpType
from concourse.bass_utils import run_bass_kernel_spmd

F32 = mybir.dt.float32
BF16 = mybir.dt.bfloat16
AF = mybir.ActivationFunctionType
AX = mybir.AxisListType
OP = AluOpType
I16 = mybir.dt.int16
U32 = mybir.dt.uint32

NCORE = 8
B = 4
S = 4096
H = 1024
M = 256
NEXP = 8
DFF = 4096
DECAY = 0.99
SC = S // NCORE
T = B * SC
C = 128
NCH = SC // C
NT = T // 128
TB = 512
NTB = T // TB
HC = H // 128
FC = DFF // 128
FH = FC // 2
CAP = 512
NSLOT = NEXP * CAP
LAM128 = float(np.float32(np.float64(DECAY) ** 128))
EPS = 1e-5

ENGS = ("sync", "gpsimd", "tensor", "vector", "scalar")


class Ctr:
    def __init__(self, h):
        self.h = h
        self.n = 0


class Prog:
    def __init__(self):
        self.q = {k: [] for k in ENGS}

    def add(self, eng, f):
        self.q[eng].append(f)

    def wait(self, eng, sem, val):
        assert val <= sem.n, f"wait target {val} > planned {sem.n} ({eng})"
        if val <= 0:
            return
        self.add(eng, lambda e, s=sem.h, v=val: e.wait_ge(s, v))

    def emit(self, eng, meth, *a, sem=None, k=1, **kw):
        if sem is not None:
            sem.n += k
            self.add(eng, lambda e, m=meth, a=a, kw=kw, sh=sem.h, kk=k:
                     getattr(e, m)(*a, **kw).then_inc(sh, kk))
            return sem.n
        self.add(eng, lambda e, m=meth, a=a, kw=kw: getattr(e, m)(*a, **kw))
        return None

    def flush(self, blk):
        for eng in ENGS:
            fns = self.q[eng]
            self.q[eng] = []
            if fns:
                getattr(blk, eng)(lambda e, fns=fns: [f(e) for f in fns])


def build():
    nc = bass.Bass("TRN2", target_bir_lowering=False, detect_race_conditions=False)
    x_d = nc.dram_tensor("x", [T, H], F32, kind="ExternalInput").ap()
    wqt_d = nc.dram_tensor("wqt", [H, M], F32, kind="ExternalInput").ap()
    wkt_d = nc.dram_tensor("wkt", [H, M], F32, kind="ExternalInput").ap()
    wvt_d = nc.dram_tensor("wvt", [H, M], F32, kind="ExternalInput").ap()
    wot_d = nc.dram_tensor("wot", [M, H], F32, kind="ExternalInput").ap()
    gate_d = nc.dram_tensor("gate", [1, 1], F32, kind="ExternalInput").ap()
    m0t_d = nc.dram_tensor("m0t", [M, M], F32, kind="ExternalInput").ap()
    wr_d = nc.dram_tensor("wr", [H, NEXP], F32, kind="ExternalInput").ap()
    br_d = nc.dram_tensor("br", [1, NEXP], F32, kind="ExternalInput").ap()
    w1_d = nc.dram_tensor("w1", [NEXP, 2, 2, 128, HC, 1024], F32, kind="ExternalInput").ap()
    b1_d = nc.dram_tensor("b1", [NEXP, 128, FC], F32, kind="ExternalInput").ap()
    w2_d = nc.dram_tensor("w2", [NEXP, 2, 2, 128, HC, 1024], F32, kind="ExternalInput").ap()
    b2_d = nc.dram_tensor("b2", [NEXP, H], F32, kind="ExternalInput").ap()
    mask_d = nc.dram_tensor("mask", [128, 512], F32, kind="ExternalInput").ap()
    dq_d = nc.dram_tensor("dq", [128, 512], F32, kind="ExternalInput").ap()
    lamk_d = nc.dram_tensor("lamk", [128, 1], F32, kind="ExternalInput").ap()
    prefw_d = nc.dram_tensor("prefw", [128, 9], F32, kind="ExternalInput").ap()
    ident_d = nc.dram_tensor("ident", [128, 128], BF16, kind="ExternalInput").ap()
    y_d = nc.dram_tensor("y", [T, H], F32, kind="ExternalOutput").ap()
    lstrict_d = nc.dram_tensor("lstrict", [128, 128], BF16, kind="ExternalInput").ap()
    ltile_d = nc.dram_tensor("ltile", [128, 128], BF16, kind="ExternalInput").ap()
    eoff_d = nc.dram_tensor("eoff", [128, 128], F32, kind="ExternalInput").ap()
    identf_d = nc.dram_tensor("identf", [128, 128], F32, kind="ExternalInput").ap()
    tok16_d = nc.dram_tensor("tok16", [128, NT], mybir.dt.int16, kind="ExternalInput").ap()
    tok32_d = nc.dram_tensor("tok32", [128, NT], mybir.dt.uint32, kind="ExternalInput").ap()
    xlt_d = nc.dram_tensor("xlt", [T + 16, H], BF16).ap()
    yscr = nc.dram_tensor("yscr", [T, H], F32).ap()
    inv16 = nc.dram_tensor("inv16", [NSLOT, 1], mybir.dt.int16).ap()
    inv32 = nc.dram_tensor("inv32", [NSLOT, 1], mybir.dt.uint32).ap()
    agin = nc.dram_tensor("agin", [M, M], F32).ap()
    agout = nc.dram_tensor("agout", [NCORE * M, M], F32).ap()

    p = Prog()
    with ExitStack() as ctx:
        E = ctx.enter_context
        sLd = Ctr(E(nc.semaphore("sLd")))
        sW = Ctr(E(nc.semaphore("sW")))
        sT = Ctr(E(nc.semaphore("sT")))
        sV = Ctr(E(nc.semaphore("sV")))
        sS = Ctr(E(nc.semaphore("sS")))
        sC = Ctr(E(nc.semaphore("sC")))
        sG = Ctr(E(nc.semaphore("sG")))
        sL2 = Ctr(E(nc.semaphore("sL2")))
        sO = Ctr(E(nc.semaphore("sO")))
        x_tok = E(nc.sbuf_tensor("x_tok", [128, NT, H], F32))
        ident = E(nc.sbuf_tensor("ident_sb", [128, 128], BF16))
        ones_col = E(nc.sbuf_tensor("ones_col", [1, 128], BF16))
        c_eps = E(nc.sbuf_tensor("c_eps", [128, 1], F32))
        c_zero = E(nc.sbuf_tensor("c_zero", [128, 1], F32))
        c_one = E(nc.sbuf_tensor("c_one", [128, 1], F32))

        with ExitStack() as ctxA:
            EA = ctxA.enter_context
            xln1 = EA(nc.sbuf_tensor("xln1", [128, HC, T], BF16))
            xa = EA(nc.sbuf_tensor("xa", [128, 2, H], BF16))
            stats = EA(nc.sbuf_tensor("stats", [128, 9, NT], F32))
            wq_sb = EA(nc.sbuf_tensor("wq_sb", [128, HC, M], BF16))
            wk_sb = EA(nc.sbuf_tensor("wk_sb", [128, HC, M], BF16))
            wv_sb = EA(nc.sbuf_tensor("wv_sb", [128, HC, M], BF16))
            wot_sb = EA(nc.sbuf_tensor("wot_sb", [128, 2, H], BF16))
            qT = EA(nc.sbuf_tensor("qT", [128, 2, T], BF16))
            kT = EA(nc.sbuf_tensor("kT", [128, 2, T], BF16))
            ktok = EA(nc.sbuf_tensor("ktok", [128, NT, M], BF16))
            vtok = EA(nc.sbuf_tensor("vtok", [128, NT, M], BF16))
            rT = EA(nc.sbuf_tensor("rT", [128, 2, T], BF16))
            sTb = EA(nc.sbuf_tensor("sTb", [128, B, 512], BF16))
            qs = EA(nc.sbuf_tensor("qs", [128, 2, 512], BF16))
            Ast = EA(nc.sbuf_tensor("Ast", [128, 2, M], F32))
            AstBf = EA(nc.sbuf_tensor("AstBf", [128, 2, M], BF16))
            dA = EA(nc.sbuf_tensor("dA", [128, NCH, 2, M], F32))
            Zw = EA(nc.sbuf_tensor("Zw", [128, 2, M], F32))
            Zw2 = EA(nc.sbuf_tensor("Zw2", [128, 2, M], F32))
            m0t_sb = EA(nc.sbuf_tensor("m0t_sb", [128, 2, M], F32))
            Zall = EA(nc.sbuf_tensor("Zall", [128, 2 * NCORE, M], F32))
            mask_sb = EA(nc.sbuf_tensor("mask_sb", [128, 512], F32))
            dq_sb = EA(nc.sbuf_tensor("dq_sb", [128, 512], F32))
            lamk_sb = EA(nc.sbuf_tensor("lamk_sb", [128, 1], F32))
            prefw_sb = EA(nc.sbuf_tensor("prefw_sb", [128, 9], F32))
            gate_sb = EA(nc.sbuf_tensor("gate_sb", [1, 1], F32))
            sig_s = EA(nc.sbuf_tensor("sig_s", [1, 4], F32))
            sig_bf = EA(nc.sbuf_tensor("sig_bf", [1, 1], BF16))
            sig_bc = EA(nc.sbuf_tensor("sig_bc", [128, 1], F32))
            p_tr = EA(nc.psum_tensor("p_tr", [128, 2, 1024], BF16))
            p_pr = EA(nc.psum_tensor("p_pr", [128, 1024], F32))
            p_s = EA(nc.psum_tensor("p_s", [128, 1024], F32))
            p_r = EA(nc.psum_tensor("p_r", [128, 1024], F32))

            with nc.Block("A", no_gpsimd_drain=True) as blkA:
                x_ld = []
                for k in range(4):
                    x_ld.append(p.emit("sync", "dma_start", sem=sLd, k=16,
                                       out=x_tok[:, 4 * k:4 * (k + 1), :],
                                       in_=x_d.rearrange("(i p) h -> p i h", p=128)[:, 4 * k:4 * (k + 1), :]))
                m0_ld = p.emit("sync", "dma_start", sem=sLd, k=16, out=m0t_sb[:, :, :],
                               in_=m0t_d.rearrange("(c p) m -> p c m", p=128))
                msk_ld = p.emit("sync", "dma_start", sem=sLd, k=16, out=mask_sb[:, :], in_=mask_d)
                dq_ld = p.emit("sync", "dma_start", sem=sLd, k=16, out=dq_sb[:, :], in_=dq_d)
                lamk_ld = p.emit("sync", "dma_start", sem=sLd, k=16, out=lamk_sb[:, :], in_=lamk_d)
                prefw_ld = p.emit("sync", "dma_start", sem=sLd, k=16, out=prefw_sb[:, :], in_=prefw_d)
                gate_ld = p.emit("sync", "dma_start", sem=sLd, k=16, out=gate_sb[:, :], in_=gate_d)
                id_ld = p.emit("sync", "dma_start", sem=sLd, k=16, out=ident[:, :], in_=ident_d)
                a_total = sLd.n

                p.emit("gpsimd", "memset", c_eps[:, :], EPS, sem=sG)
                p.emit("gpsimd", "memset", c_zero[:, :], 0.0, sem=sG)
                p.emit("gpsimd", "memset", c_one[:, :], 1.0, sem=sG)
                p.emit("gpsimd", "memset", ones_col[:, :], 1.0, sem=sG)
                consts_done = sG.n
                wq_ld = p.emit("gpsimd", "dma_start", sem=sW, k=16, out=wq_sb[:, :, :],
                               in_=wqt_d.rearrange("(c p) m -> p c m", p=128))
                wk_ld = p.emit("gpsimd", "dma_start", sem=sW, k=16, out=wk_sb[:, :, :],
                               in_=wkt_d.rearrange("(c p) m -> p c m", p=128))
                wv_ld = p.emit("gpsimd", "dma_start", sem=sW, k=16, out=wv_sb[:, :, :],
                               in_=wvt_d.rearrange("(c p) m -> p c m", p=128))
                wo_ld = p.emit("gpsimd", "dma_start", sem=sW, k=16, out=wot_sb[:, :, :],
                               in_=wot_d.rearrange("(c p) h -> p c h", p=128))

                p.wait("scalar", sLd, a_total)
                p.wait("scalar", sG, consts_done)
                p.emit("scalar", "activation", sig_s[:, 0:1], gate_sb[:, :], AF.Exp, bias=c_zero[0:1, :], scale=-1.0)
                p.emit("scalar", "activation", sig_s[:, 1:2], sig_s[:, 0:1], AF.Identity, bias=c_one[0:1, :], scale=1.0)
                p.emit("scalar", "activation", sig_s[:, 2:3], sig_s[:, 1:2], AF.Ln, bias=c_zero[0:1, :], scale=1.0)
                p.emit("scalar", "activation", sig_s[:, 3:4], sig_s[:, 2:3], AF.Exp, bias=c_zero[0:1, :], scale=-1.0)
                p.emit("scalar", "activation", sig_bf[:, :], sig_s[:, 3:4], AF.Identity, bias=c_zero[0:1, :], scale=1.0, sem=sS)
                sigbf_done = sS.n
                p.wait("tensor", sS, sigbf_done)
                p.emit("tensor", "matmul", p_pr[:, 0:1], ones_col[:, :], sig_bf[:, :], start=True, stop=True, sem=sT)
                p.wait("vector", sT, sT.n)
                p.emit("vector", "tensor_copy", sig_bc[:, :], p_pr[:, 0:1], sem=sV)
                sigbc_done = sV.n
                p.wait("scalar", sW, wo_ld)
                p.wait("scalar", sV, sigbc_done)
                p.emit("scalar", "activation", wot_sb[:, 0, :], wot_sb[:, 0, :], AF.Identity, bias=c_zero[:, :], scale=sig_bc[:, 0:1])
                p.emit("scalar", "activation", wot_sb[:, 1, :], wot_sb[:, 1, :], AF.Identity, bias=c_zero[:, :], scale=sig_bc[:, 0:1], sem=sS)
                wotg_done = sS.n

                SUM, SQS, MEAN, EX2, VAR, MSQ, RSTD, MS, NEGMS = range(9)
                for tt in range(NT):
                    p.wait("vector", sLd, a_total)
                    p.emit("vector", "tensor_reduce", stats[:, SUM, tt:tt + 1], x_tok[:, tt, :], axis=AX.X, op=OP.add, sem=sV)
                    p.wait("scalar", sLd, a_total)
                    p.emit("scalar", "activation", qT[:, 0, 0:H], x_tok[:, tt, :], AF.Square,
                           bias=c_zero[:, :], scale=1.0, accum_out=stats[:, SQS, tt:tt + 1], sem=sS)
                sum_done = sV.n
                p.wait("scalar", sV, sum_done)
                p.emit("scalar", "activation", stats[:, MEAN, :], stats[:, SUM, :], AF.Identity, bias=c_zero[:, :], scale=1.0 / H)
                p.emit("scalar", "activation", stats[:, EX2, :], stats[:, SQS, :], AF.Identity, bias=c_zero[:, :], scale=1.0 / H)
                p.emit("scalar", "activation", stats[:, MSQ, :], stats[:, MEAN, :], AF.Square, bias=c_zero[:, :], scale=1.0, sem=sS)
                p.wait("vector", sS, sS.n)
                p.emit("vector", "tensor_tensor", stats[:, VAR, :], stats[:, EX2, :], stats[:, MSQ, :], OP.subtract, sem=sV)
                p.wait("scalar", sV, sV.n)
                p.emit("scalar", "activation", stats[:, RSTD, :], stats[:, VAR, :], AF.Ln, bias=c_eps[:, :], scale=1.0)
                p.emit("scalar", "activation", stats[:, RSTD, :], stats[:, RSTD, :], AF.Exp, bias=c_zero[:, :], scale=-0.5, sem=sS)
                p.wait("vector", sS, sS.n)
                p.emit("vector", "tensor_tensor", stats[:, MS, :], stats[:, MEAN, :], stats[:, RSTD, :], OP.mult, sem=sV)
                p.wait("scalar", sV, sV.n)
                p.emit("scalar", "activation", stats[:, NEGMS, :], stats[:, MS, :], AF.Identity, bias=c_zero[:, :], scale=-1.0, sem=sS)
                negms_done = sS.n

                ping_ev = [0, 0]
                aff_free = [0, 0]
                for tt in range(NT):
                    pi = tt % 2
                    p.wait("scalar", sS, negms_done)
                    p.wait("scalar", sT, aff_free[pi])
                    p.emit("scalar", "activation", xa[:, pi, :], x_tok[:, tt, :], AF.Identity,
                           bias=stats[:, NEGMS, tt:tt + 1], scale=stats[:, RSTD, tt:tt + 1], sem=sS)
                    a_n = sS.n
                    for h in range(HC):
                        bi = h % 2
                        p.wait("tensor", sS, a_n)
                        p.wait("tensor", sV, ping_ev[bi])
                        if tt == 0 and h == 0:
                            p.wait("tensor", sLd, a_total)
                        p.emit("tensor", "transpose", p_tr[:, bi, 0:128], xa[:, pi, h * 128:(h + 1) * 128], ident[:, :], sem=sT)
                        p.wait("vector", sT, sT.n)
                        p.emit("vector", "tensor_copy", xln1[:, h, tt * 128:(tt + 1) * 128], p_tr[:, bi, 0:128], sem=sV)
                        ping_ev[bi] = sV.n
                    aff_free[pi] = sT.n
                xln1_done_v = sV.n

                proj_ping_v = [sV.n, sV.n]
                pp = 0
                for (wsb, dst) in ((wq_sb, qT), (wk_sb, kT)):
                    for mt in range(2):
                        for tb4 in range(4):
                            half = pp % 2
                            p.wait("tensor", sV, proj_ping_v[half])
                            if pp == 0:
                                p.wait("tensor", sV, xln1_done_v)
                                p.wait("tensor", sW, wv_ld)
                            for h in range(HC):
                                p.emit("tensor", "matmul", p_pr[:, half * 512:half * 512 + 512],
                                       wsb[:, h, mt * 128:(mt + 1) * 128], xln1[:, h, tb4 * 512:(tb4 + 1) * 512],
                                       start=(h == 0), stop=(h == HC - 1), sem=(sT if h == HC - 1 else None))
                            p.wait("vector", sT, sT.n)
                            p.emit("vector", "tensor_copy", dst[:, mt, tb4 * 512:(tb4 + 1) * 512],
                                   p_pr[:, half * 512:half * 512 + 512], sem=sV)
                            proj_ping_v[half] = sV.n
                            pp += 1
                qk_done_v = sV.n
                proj_ping_s = [0, 0]
                for tt in range(NT):
                    for (wsb, dst, scale_ap) in ((wk_sb, ktok, lamk_sb[:, 0:1]), (wv_sb, vtok, None)):
                        half = pp % 2
                        p.wait("tensor", sS, proj_ping_s[half])
                        p.wait("tensor", sV, proj_ping_v[half])
                        for h in range(HC):
                            p.emit("tensor", "matmul", p_pr[:, half * 512:half * 512 + 256],
                                   xln1[:, h, tt * 128:(tt + 1) * 128], wsb[:, h, :],
                                   start=(h == 0), stop=(h == HC - 1), sem=(sT if h == HC - 1 else None))
                        p.wait("scalar", sT, sT.n)
                        if tt == 0 and scale_ap is not None:
                            p.wait("scalar", sLd, a_total)
                        p.emit("scalar", "activation", dst[:, tt, :], p_pr[:, half * 512:half * 512 + 256],
                               AF.Identity, bias=c_zero[:, :],
                               scale=(scale_ap if scale_ap is not None else 1.0), sem=sS)
                        proj_ping_s[half] = sS.n
                        pp += 1
                kv_done_s = sS.n

                p.wait("tensor", sS, kv_done_s)
                da_ping = [0, 0]
                for cl in range(NCH):
                    for mk in range(2):
                        reg = (0, 256) if mk == 0 else (512, 768)
                        p.wait("tensor", sV, da_ping[mk])
                        for bp in range(B):
                            tt = bp * NCH + cl
                            p.emit("tensor", "matmul", p_pr[:, reg[0]:reg[1]],
                                   ktok[:, tt, mk * 128:(mk + 1) * 128], vtok[:, tt, :],
                                   start=(bp == 0), stop=(bp == B - 1), sem=(sT if bp == B - 1 else None))
                        p.wait("vector", sT, sT.n)
                        p.emit("vector", "tensor_copy", dA[:, cl, mk, :], p_pr[:, reg[0]:reg[1]], sem=sV)
                        da_ping[mk] = sV.n
                p.emit("vector", "tensor_copy", Zw[:, :, :], dA[:, 0, :, :], sem=sV)
                for cl in range(1, NCH):
                    p.wait("scalar", sV, sV.n)
                    p.emit("scalar", "activation", Zw2[:, :, :], Zw[:, :, :], AF.Identity, bias=c_zero[:, :], scale=LAM128, sem=sS)
                    p.wait("vector", sS, sS.n)
                    p.emit("vector", "tensor_tensor", Zw[:, :, :], Zw2[:, :, :], dA[:, cl, :, :], OP.add, sem=sV)
                z_done_v = sV.n

                p.wait("gpsimd", sV, z_done_v)
                p.emit("gpsimd", "dma_start", sem=sW, k=16, out=agin.rearrange("(c p) m -> p c m", p=128), in_=Zw[:, :, :])
                p.wait("gpsimd", sW, sW.n)
                p.emit("gpsimd", "collective_compute", "AllGather", OP.bypass,
                       replica_groups=[list(range(NCORE))], ins=[agin.opt()], outs=[agout.opt()], sem=sC)
                p.wait("gpsimd", sC, sC.n)
                zall_ld = p.emit("gpsimd", "dma_start", sem=sW, k=16, out=Zall[:, :, :],
                                 in_=agout.rearrange("(c p) m -> p c m", p=128))

                p.wait("scalar", sW, zall_ld)
                p.wait("scalar", sLd, a_total)
                p.emit("scalar", "activation", Ast[:, :, :], m0t_sb[:, :, :], AF.Identity, bias=c_zero[:, :],
                       scale=prefw_sb[:, 8:9], sem=sS)
                for k in range(NCORE):
                    p.emit("scalar", "activation", Zw2[:, :, :], Zall[:, 2 * k:2 * k + 2, :], AF.Identity,
                           bias=c_zero[:, :], scale=prefw_sb[:, k:k + 1], sem=sS)
                    p.wait("vector", sS, sS.n)
                    p.emit("vector", "tensor_tensor", Ast[:, :, :], Zw2[:, :, :], Ast[:, :, :], OP.add, sem=sV)
                    p.wait("scalar", sV, sV.n)
                p.emit("vector", "tensor_copy", AstBf[:, :, :], Ast[:, :, :], sem=sV)
                astbf_done = sV.n

                s_ping = [sV.n, sV.n]
                for cl in range(NCH):
                    stb_ev = {}
                    for bp in range(B):
                        half = bp % 2
                        p.wait("tensor", sV, s_ping[half])
                        if cl == 0 and bp == 0:
                            p.wait("tensor", sV, qk_done_v)
                        for mk in range(2):
                            p.emit("tensor", "matmul", p_s[:, half * 512:half * 512 + 512],
                                   kT[:, mk, bp * SC + cl * 128: bp * SC + cl * 128 + 128],
                                   qT[:, mk, :].rearrange("p (b s) -> p b s", b=B)[:, :, cl * 128:(cl + 1) * 128],
                                   start=(mk == 0), stop=(mk == 1), sem=(sT if mk == 1 else None))
                        p.wait("vector", sT, sT.n)
                        if cl == 0 and bp == 0:
                            p.wait("vector", sLd, a_total)
                            p.wait("vector", sLd, a_total)
                        p.emit("vector", "tensor_tensor", sTb[:, bp, :],
                               p_s[:, half * 512:half * 512 + 512], mask_sb[:, :], OP.mult, sem=sV)
                        stb_ev[bp] = sV.n
                        s_ping[half] = sV.n
                    for mk in range(2):
                        p.emit("vector", "tensor_tensor",
                               qs[:, mk, :].rearrange("p (b s) -> p b s", b=B),
                               qT[:, mk, :].rearrange("p (b s) -> p b s", b=B)[:, :, cl * 128:(cl + 1) * 128],
                               dq_sb.rearrange("p (b s) -> p b s", b=B), OP.mult, sem=sV)
                    qs_done = sV.n
                    for mv in range(2):
                        p.wait("tensor", sV, qs_done)
                        p.wait("tensor", sV, astbf_done)
                        p.emit("tensor", "matmul", p_r[:, mv * 512:mv * 512 + 512],
                               AstBf[:, 0, mv * 128:(mv + 1) * 128], qs[:, 0, :], start=True, stop=False)
                        p.emit("tensor", "matmul", p_r[:, mv * 512:mv * 512 + 512],
                               AstBf[:, 1, mv * 128:(mv + 1) * 128], qs[:, 1, :], start=False, stop=False)
                        for bp in range(B):
                            tt = bp * NCH + cl
                            p.wait("tensor", sV, stb_ev[bp])
                            p.emit("tensor", "matmul", p_r[:, mv * 512:mv * 512 + 512],
                                   vtok[:, tt, mv * 128:(mv + 1) * 128], sTb[:, bp, :],
                                   start=False, stop=(bp == B - 1), sem=(sT if bp == B - 1 else None))
                        p.wait("vector", sT, sT.n)
                        p.emit("vector", "tensor_copy",
                               rT[:, mv, :].rearrange("p (b s) -> p b s", b=B)[:, :, cl * 128:(cl + 1) * 128],
                               p_r[:, mv * 512:mv * 512 + 512].rearrange("p (b s) -> p b s", b=B), sem=sV)
                    if cl < NCH - 1:
                        p.wait("scalar", sT, sT.n)
                        p.emit("scalar", "activation", Zw2[:, :, :], Ast[:, :, :], AF.Identity, bias=c_zero[:, :],
                               scale=LAM128, sem=sS)
                        p.wait("vector", sS, sS.n)
                        p.emit("vector", "tensor_tensor", Ast[:, :, :], Zw2[:, :, :], dA[:, cl, :, :], OP.add, sem=sV)
                        p.emit("vector", "tensor_copy", AstBf[:, :, :], Ast[:, :, :], sem=sV)
                        astbf_done = sV.n
                r_done_v = sV.n

                x1_add = {}
                for tt in range(NT):
                    if tt == 0:
                        p.wait("tensor", sV, r_done_v)
                        p.wait("tensor", sS, wotg_done)
                    else:
                        p.wait("tensor", sV, x1_add[tt - 1])
                    for half in range(2):
                        for mt in range(2):
                            p.emit("tensor", "matmul", p_pr[:, half * 512:half * 512 + 512],
                                   rT[:, mt, tt * 128:(tt + 1) * 128], wot_sb[:, mt, half * 512:half * 512 + 512],
                                   start=(mt == 0), stop=(mt == 1),
                                   sem=(sT if (half == 1 and mt == 1) else None))
                    p.wait("vector", sT, sT.n)
                    p.emit("vector", "tensor_tensor", x_tok[:, tt, :], p_pr[:, 0:1024], x_tok[:, tt, :], OP.add, sem=sV)
                    x1_add[tt] = sV.n
                x1_done = sV.n
                p.flush(blkA)

        with ExitStack() as ctxB:
            EB = ctxB.enter_context
            stats2 = EB(nc.sbuf_tensor("stats2", [128, 9, NT], F32))
            wr_sb = EB(nc.sbuf_tensor("wr_sb", [128, HC, NEXP], BF16))
            br_r = EB(nc.sbuf_tensor("br_r", [1, NEXP], BF16))
            probs = EB(nc.sbuf_tensor("probs", [128, NT, NEXP], F32))
            expt = EB(nc.sbuf_tensor("expt", [128, NEXP], F32))
            rstat = EB(nc.sbuf_tensor("rstat", [128, 5, NT], F32))
            selm = EB(nc.sbuf_tensor("selm", [128, NT, NEXP], BF16))
            tmp_epi = EB(nc.sbuf_tensor("tmp_epi", [128, H], F32))
            tok16_sb = EB(nc.sbuf_tensor("tok16_sb", [128, NT], I16))
            tok32_sb = EB(nc.sbuf_tensor("tok32_sb", [128, NT], U32))
            ones128 = EB(nc.sbuf_tensor("ones128", [128, 1], BF16))
            b1c = EB(nc.sbuf_tensor("b1c", [128, FC], F32))
            b2r = EB(nc.sbuf_tensor("b2r", [1, H], BF16))
            w1b = EB(nc.sbuf_tensor("w1b", [128, HC, 2048], BF16))
            w2b = EB(nc.sbuf_tensor("w2b", [128, FH, H], BF16))
            p_yA = EB(nc.psum_tensor("p_yA", [128, 1024], F32))
            p_yB = EB(nc.psum_tensor("p_yB", [128, 1024], F32))
            p_h = EB(nc.psum_tensor("p_h", [128, 1024], F32))
            p_rt = EB(nc.psum_tensor("p_rt", [128, 512], F32))
            p_tr2 = EB(nc.psum_tensor("p_tr2", [128, 1024], BF16))

            with nc.Block("Bf") as blkB:
                SUM, SQS, MEAN, EX2, VAR, MSQ, RSTD, MS, NEGMS = range(9)
                MX, SE, LNSE, RINV, GMX = range(5)

                regbox = {}

                def _mkregs(g):
                    r1 = g.register("bnd_slot").__enter__()
                    g.reg_mov(r1, NSLOT - 1)
                    r2 = g.register("bnd_tok").__enter__()
                    g.reg_mov(r2, T - 1)
                    regbox["slot"] = r1
                    regbox["tok"] = r2
                p.add("gpsimd", _mkregs)

                def emit_indirect(out_ap, idx_ap, in_ap, bkey, sem, k=16):
                    sem.n += k
                    def f(g, out_ap=out_ap, idx_ap=idx_ap, in_ap=in_ap, bkey=bkey, sh=sem.h, kk=k):
                        g.indirect_dma_start(out_ap, bass.IndirectOffsetOnAxis(ap=idx_ap, axis=0),
                                             in_ap, None, bounds_check=regbox[bkey],
                                             oob_is_err=False).then_inc(sh, kk)
                    p.add("gpsimd", f)
                    return sem.n

                wr_ld = p.emit("gpsimd", "dma_start", sem=sW, k=16, out=wr_sb[:, :, :],
                               in_=wr_d.rearrange("(c p) e -> p c e", p=128))
                br_ld = p.emit("gpsimd", "dma_start", sem=sW, k=16, out=br_r[:, :], in_=br_d)
                tk16_ld = p.emit("gpsimd", "dma_start", sem=sW, k=16, out=tok16_sb[:, :], in_=tok16_d)
                tk32_ld = p.emit("gpsimd", "dma_start", sem=sW, k=16, out=tok32_sb[:, :], in_=tok32_d)
                p.emit("gpsimd", "memset", ones128[:, :], 1.0, sem=sG)

                w1_thr = {}
                w2_thr = {}
                b_thr = {}

                def emit_weight_loads(e, hf, w1_free, w2_free, b_free):
                    for qp in range(2):
                        if w1_free:
                            p.wait("gpsimd", sT, w1_free)
                        w1_thr[(e, hf, qp)] = p.emit(
                            "gpsimd", "dma_start", sem=sW, k=16,
                            out=w1b[:, :, qp * 1024:(qp + 1) * 1024],
                            in_=w1_d[e, hf, qp])
                    for qp in range(2):
                        if w2_free:
                            p.wait("gpsimd", sT, w2_free)
                        w2_thr[(e, hf, qp)] = p.emit(
                            "gpsimd", "dma_start", sem=sW, k=16,
                            out=w2b[:, 8 * qp:8 * (qp + 1), :],
                            in_=w2_d[e, hf, qp])
                    if hf == 0:
                        if b_free is not None:
                            p.wait("gpsimd", sS, b_free[0])
                            p.wait("gpsimd", sV, b_free[1])
                            p.wait("gpsimd", sT, b_free[2])
                        b_thr[e] = (
                            p.emit("gpsimd", "dma_start", sem=sW, k=16, out=b1c[:, :], in_=b1_d[e]),
                            p.emit("gpsimd", "dma_start", sem=sW, k=16, out=b2r[:, :], in_=b2_d[e:e + 1, :]),
                        )

                emit_weight_loads(0, 0, 0, 0, None)

                p.emit("vector", "memset", tmp_epi[:, :], 0.0, sem=sV)
                zv = sV.n
                p.wait("sync", sV, zv)
                ysz = []
                for k in range(NT):
                    ysz.append(p.emit("sync", "dma_start", sem=sO, k=16,
                                      out=yscr.rearrange("(i p) h -> p i h", p=128)[:, k:k + 1, :],
                                      in_=tmp_epi[:, :].rearrange("p (i h) -> p i h", i=1)))
                p.wait("gpsimd", sV, zv)
                xpz = p.emit("gpsimd", "dma_start", sem=sW, k=16,
                             out=xlt_d[T:T + 16, :], in_=tmp_epi[0:16, 0:H])
                yz_done = sO.n

                with ExitStack() as ctxR:
                    ER = ctxR.enter_context
                    xln2 = ER(nc.sbuf_tensor("xln2", [128, HC, T], BF16))
                    xa2 = ER(nc.sbuf_tensor("xa2", [128, 2, H], BF16))
                    sqs_scr = ER(nc.sbuf_tensor("sqs_scr", [128, H], F32))
                    Lst_sb = ER(nc.sbuf_tensor("Lst_sb", [128, 128], BF16))
                    Lti_sb = ER(nc.sbuf_tensor("Lti_sb", [128, 128], BF16))
                    eoff_sb = ER(nc.sbuf_tensor("eoff_sb", [128, 128], F32))
                    rnk = ER(nc.sbuf_tensor("rnk", [128, 128], F32))
                    cnt_bf = ER(nc.sbuf_tensor("cnt_bf", [128, 1], BF16))
                    orow16 = ER(nc.sbuf_tensor("orow16", [1, 128], mybir.dt.float16))
                    ocol = ER(nc.sbuf_tensor("ocol", [1, 128], mybir.dt.float16))
                    obc = ER(nc.sbuf_tensor("obc", [128, 128], F32))
                    slotf = ER(nc.sbuf_tensor("slotf", [128, 128], F32))
                    slotg = ER(nc.sbuf_tensor("slotg", [128, NT], F32))
                    slotu = ER(nc.sbuf_tensor("slotu", [128, NT], U32))
                    pr16 = ER(nc.sbuf_tensor("pr16", [128, 32], I16))
                    pr32 = ER(nc.sbuf_tensor("pr32", [128, 32], U32))
                    identf_sb = ER(nc.sbuf_tensor("identf_sb", [128, 128], F32))
                    cntrow = ER(nc.sbuf_tensor("cntrow", [1, 128], F32))
                    ovf = ER(nc.sbuf_tensor("ovf", [128, 128], F32))
                    m8 = ER(nc.sbuf_tensor("m8", [128, 128], F32))
                    ri32 = ER(nc.sbuf_tensor("ri32", [128, 128], mybir.dt.int32))
                    pi2 = ER(nc.sbuf_tensor("pi2", [128, 128], F32))
                    slotg2 = ER(nc.sbuf_tensor("slotg2", [128, NT], F32))
                    slotu2 = ER(nc.sbuf_tensor("slotu2", [128, NT], U32))

                    idf_ld = p.emit("sync", "dma_start", sem=sL2, k=16, out=identf_sb[:, :], in_=identf_d)
                    lst_ld = p.emit("sync", "dma_start", sem=sL2, k=16, out=Lst_sb[:, :], in_=lstrict_d)
                    lti_ld = p.emit("sync", "dma_start", sem=sL2, k=16, out=Lti_sb[:, :], in_=ltile_d)
                    eo_ld = p.emit("sync", "dma_start", sem=sL2, k=16, out=eoff_sb[:, :], in_=eoff_d)
                    b_total = sL2.n

                    for tt in range(NT):
                        if tt == 0:
                            p.wait("vector", sV, x1_done)
                        p.emit("vector", "tensor_reduce", stats2[:, SUM, tt:tt + 1], x_tok[:, tt, :], axis=AX.X, op=OP.add, sem=sV)
                        p.emit("scalar", "activation", sqs_scr[:, :], x_tok[:, tt, :], AF.Square,
                               bias=c_zero[:, :], scale=1.0, accum_out=stats2[:, SQS, tt:tt + 1], sem=sS)
                    p.wait("scalar", sV, sV.n)
                    p.emit("scalar", "activation", stats2[:, MEAN, :], stats2[:, SUM, :], AF.Identity, bias=c_zero[:, :], scale=1.0 / H)
                    p.emit("scalar", "activation", stats2[:, EX2, :], stats2[:, SQS, :], AF.Identity, bias=c_zero[:, :], scale=1.0 / H)
                    p.emit("scalar", "activation", stats2[:, MSQ, :], stats2[:, MEAN, :], AF.Square, bias=c_zero[:, :], scale=1.0, sem=sS)
                    p.wait("vector", sS, sS.n)
                    p.emit("vector", "tensor_tensor", stats2[:, VAR, :], stats2[:, EX2, :], stats2[:, MSQ, :], OP.subtract, sem=sV)
                    p.wait("scalar", sV, sV.n)
                    p.emit("scalar", "activation", stats2[:, RSTD, :], stats2[:, VAR, :], AF.Ln, bias=c_eps[:, :], scale=1.0)
                    p.emit("scalar", "activation", stats2[:, RSTD, :], stats2[:, RSTD, :], AF.Exp, bias=c_zero[:, :], scale=-0.5, sem=sS)
                    p.wait("vector", sS, sS.n)
                    p.emit("vector", "tensor_tensor", stats2[:, MS, :], stats2[:, MEAN, :], stats2[:, RSTD, :], OP.mult, sem=sV)
                    p.wait("scalar", sV, sV.n)
                    p.emit("scalar", "activation", stats2[:, NEGMS, :], stats2[:, MS, :], AF.Identity, bias=c_zero[:, :], scale=-1.0, sem=sS)
                    negms2 = sS.n

                    tr_free = 0
                    aff2_free = {0: 0, 1: 0}
                    xs_done = []
                    for tt in range(NT):
                        pi = tt % 2
                        p.wait("scalar", sS, negms2)
                        p.wait("scalar", sT, aff2_free[pi])
                        if tt >= 2:
                            p.wait("scalar", sO, xst[tt - 2])
                        p.emit("scalar", "activation", xa2[:, pi, :], x_tok[:, tt, :], AF.Identity,
                               bias=stats2[:, NEGMS, tt:tt + 1], scale=stats2[:, RSTD, tt:tt + 1], sem=sS)
                        a_n = sS.n
                        if tt == 0:
                            xst = {}
                        p.wait("sync", sS, a_n)
                        xst[tt] = p.emit("sync", "dma_start", sem=sO, k=16,
                                         out=xlt_d[0:T, :].rearrange("(i p) h -> p i h", p=128)[:, tt:tt + 1, :],
                                         in_=xa2[:, pi, :].rearrange("p (i h) -> p i h", i=1))
                        for h in range(HC):
                            p.wait("tensor", sS, a_n)
                            p.wait("tensor", sV, tr_free)
                            p.emit("tensor", "transpose", p_tr2[:, 0:128], xa2[:, pi, h * 128:(h + 1) * 128], ident[:, :], sem=sT)
                            p.wait("vector", sT, sT.n)
                            p.emit("vector", "tensor_copy", xln2[:, h, tt * 128:(tt + 1) * 128], p_tr2[:, 0:128], sem=sV)
                            tr_free = sV.n
                        aff2_free[pi] = sT.n
                    xln2_done = sV.n
                    xst_all = sO.n

                    rt_free = 0
                    for tt in range(NT):
                        if tt == 0:
                            p.wait("tensor", sV, xln2_done)
                            p.wait("tensor", sW, br_ld)
                        p.wait("tensor", sS, rt_free)
                        p.emit("tensor", "matmul", p_rt[:, 0:NEXP], ones_col[:, :], br_r[:, :], start=True, stop=False)
                        for h in range(HC):
                            p.emit("tensor", "matmul", p_rt[:, 0:NEXP], xln2[:, h, tt * 128:(tt + 1) * 128],
                                   wr_sb[:, h, :], start=False, stop=(h == HC - 1), sem=(sT if h == HC - 1 else None))
                        p.wait("vector", sT, sT.n)
                        p.emit("vector", "tensor_reduce", rstat[:, MX, tt:tt + 1], p_rt[:, 0:NEXP], axis=AX.X, op=OP.max, negate=True, sem=sV)
                        p.wait("scalar", sV, sV.n)
                        p.emit("scalar", "activation", expt[:, :], p_rt[:, 0:NEXP], AF.Exp,
                               bias=rstat[:, MX, tt:tt + 1], scale=1.0, accum_out=rstat[:, SE, tt:tt + 1], sem=sS)
                        rt_free = sS.n
                        p.emit("scalar", "activation", rstat[:, LNSE, tt:tt + 1], rstat[:, SE, tt:tt + 1], AF.Ln, bias=c_zero[:, :], scale=1.0)
                        p.emit("scalar", "activation", rstat[:, RINV, tt:tt + 1], rstat[:, LNSE, tt:tt + 1], AF.Exp, bias=c_zero[:, :], scale=-1.0)
                        p.emit("scalar", "activation", probs[:, tt, :], expt[:, :], AF.Identity, bias=c_zero[:, :],
                               scale=rstat[:, RINV, tt:tt + 1], sem=sS)
                        p.wait("vector", sS, sS.n)
                        p.emit("vector", "tensor_reduce", rstat[:, GMX, tt:tt + 1], probs[:, tt, :], axis=AX.X, op=OP.max, sem=sV)
                    for tt in range(NT):
                        for e in range(NEXP):
                            p.emit("vector", "tensor_tensor", selm[:, tt, e:e + 1], probs[:, tt, e:e + 1],
                                   rstat[:, GMX, tt:tt + 1], OP.is_ge, sem=(sV if e == NEXP - 1 else None))
                    sel_done = sV.n

                    p.wait("tensor", sV, sel_done)
                    p.wait("tensor", sS, rt_free)
                    p.wait("tensor", sL2, b_total)
                    p.emit("tensor", "matmul", p_rt[:, 0:128], Lst_sb[:, :],
                           selm[:, :, :].rearrange("p t e -> p (t e)"), start=True, stop=True, sem=sT)
                    p.wait("vector", sT, sT.n)
                    p.emit("vector", "tensor_copy", rnk[:, :], p_rt[:, 0:128], sem=sV)
                    p.emit("tensor", "matmul", p_rt[0:1, 128:256], ones128[:, :],
                           selm[:, :, :].rearrange("p t e -> p (t e)"), start=True, stop=True, sem=sT)
                    cnt_mm = sT.n
                    p.wait("vector", sT, cnt_mm)
                    p.emit("vector", "tensor_copy", cntrow[:, :], p_rt[0:1, 128:256], sem=sV)
                    p.wait("tensor", sV, sV.n)
                    p.wait("tensor", sL2, b_total)
                    p.emit("tensor", "transpose", p_h[:, 0:1], cntrow[:, :], identf_sb[0:1, 0:1], sem=sT)
                    ct_mm = sT.n
                    p.wait("vector", sT, ct_mm)
                    p.emit("vector", "tensor_copy", cnt_bf[:, :], p_h[:, 0:1], sem=sV)
                    p.wait("tensor", sV, sV.n)
                    p.emit("tensor", "matmul", p_rt[0:1, 384:512], cnt_bf[:, :], Lti_sb[:, :], start=True, stop=True, sem=sT)
                    p.wait("vector", sT, sT.n)
                    p.emit("vector", "tensor_copy", orow16[:, :], p_rt[0:1, 384:512], sem=sV)
                    p.wait("tensor", sV, sV.n)
                    p.emit("tensor", "matmul", p_h[:, 128:256], ones_col[:, :], orow16[:, :], start=True, stop=True, sem=sT)
                    p.wait("vector", sT, sT.n)
                    p.emit("vector", "tensor_copy", obc[:, :], p_h[:, 128:256], sem=sV)
                    p.emit("vector", "tensor_tensor", slotf[:, :], rnk[:, :], obc[:, :], OP.add)
                    p.emit("vector", "tensor_scalar", ovf[:, :], slotf[:, :], float(CAP) - 0.5,
                           65536.0, OP.is_ge, OP.mult)
                    p.emit("vector", "tensor_scalar", pi2[:, :], slotf[:, :], -3.5, 0.125, OP.add, OP.mult)
                    p.emit("vector", "tensor_copy", ri32[:, :], pi2[:, :])
                    p.emit("vector", "tensor_copy", m8[:, :], ri32[:, :])
                    p.emit("vector", "tensor_scalar", pi2[:, :], m8[:, :], -8.0, 0.0, OP.mult, OP.add)
                    p.emit("vector", "tensor_tensor", pi2[:, :], pi2[:, :], slotf[:, :], OP.add)
                    p.emit("vector", "tensor_scalar", pi2[:, :], pi2[:, :], float(16 * (CAP // 128)), 0.0, OP.mult, OP.add)
                    p.emit("vector", "tensor_tensor", pi2[:, :], pi2[:, :], m8[:, :], OP.add)
                    p.emit("vector", "tensor_tensor", pi2[:, :], pi2[:, :], eoff_sb[:, :], OP.add)
                    p.emit("vector", "tensor_tensor", pi2[:, :], pi2[:, :], ovf[:, :], OP.add)
                    p.emit("vector", "tensor_tensor", pi2[:, :], pi2[:, :], selm[:, :, :].rearrange("p t e -> p (t e)"), OP.mult)
                    p.emit("vector", "tensor_reduce", slotg2[:, :],
                           pi2[:, :].rearrange("p (t e) -> p t e", e=NEXP), axis=AX.X, op=OP.add, sem=sV)
                    p.emit("vector", "tensor_tensor", slotf[:, :], slotf[:, :], eoff_sb[:, :], OP.add)
                    p.emit("vector", "tensor_tensor", slotf[:, :], slotf[:, :], ovf[:, :], OP.add)
                    p.emit("vector", "tensor_tensor", slotf[:, :], slotf[:, :], selm[:, :, :].rearrange("p t e -> p (t e)"), OP.mult)
                    p.emit("vector", "tensor_reduce", slotg[:, :],
                           slotf[:, :].rearrange("p (t e) -> p t e", e=NEXP), axis=AX.X, op=OP.add, sem=sV)
                    p.wait("vector", sV, sV.n)
                    p.emit("vector", "tensor_copy", slotu2[:, :], slotg2[:, :])
                    p.emit("vector", "tensor_copy", slotu[:, :], slotg[:, :], sem=sV)
                    slot_done = sV.n

                    p.emit("gpsimd", "memset", pr16[:, :], float(T), sem=sG)
                    p.emit("gpsimd", "memset", pr32[:, :], float(T), sem=sG)
                    p.emit("gpsimd", "dma_start", sem=sW, k=16,
                           out=inv16.rearrange("(p f) one -> p (f one)", p=128), in_=pr16[:, :])
                    p.emit("gpsimd", "dma_start", sem=sW, k=16,
                           out=inv32.rearrange("(p f) one -> p (f one)", p=128), in_=pr32[:, :])
                    p.wait("gpsimd", sW, sW.n)
                    p.wait("gpsimd", sV, slot_done)
                    for tt in range(NT):
                        emit_indirect(inv16, slotu[:, tt:tt + 1], tok16_sb[:, tt:tt + 1], "slot", sW)
                        emit_indirect(inv32, slotu2[:, tt:tt + 1], tok32_sb[:, tt:tt + 1], "slot", sW)
                    inv_done = sW.n
                    rank_T = sT.n
                    rank_V = sV.n
                    rank_S = sS.n

                C32 = CAP // 16
                NST = CAP // 128
                h1 = EB(nc.sbuf_tensor("h1", [128, FH, CAP], BF16))
                Xs0 = EB(nc.sbuf_tensor("Xs0", [128, HC, CAP], BF16))
                Xs1 = EB(nc.sbuf_tensor("Xs1", [128, HC, CAP], BF16))
                XsB = [Xs0, Xs1]
                Ys = EB(nc.sbuf_tensor("Ys", [128, NST, H], F32))
                g16 = EB(nc.sbuf_tensor("g16", [128, C32], I16))
                s32a = EB(nc.sbuf_tensor("s32a", [128, NST], U32))
                s32b = EB(nc.sbuf_tensor("s32b", [128, NST], U32))
                s32B = [s32a, s32b]

                idx_thr = {}
                xs_ld_by = {}
                s32_ld_by = {}
                xs_free_by = {}
                scat_done = {}

                def emit_idx_sync(e):
                    p.wait("gpsimd", sW, inv_done)
                    if e >= 1:
                        p.wait("gpsimd", sW, xs_ld_by[e - 1])
                    t = p.emit("gpsimd", "dma_start", sem=sW, k=16, out=g16[0:16, :],
                               in_=bass.AP(inv16.tensor, e * CAP, [[C32, 16], [1, C32]]))
                    p.wait("gpsimd", sW, t)
                    t = p.emit("gpsimd", "dma_start", sem=sW, k=16, out=g16[16:32, :], in_=g16[0:16, :])
                    p.wait("gpsimd", sW, t)
                    t = p.emit("gpsimd", "dma_start", sem=sW, k=16, out=g16[32:64, :], in_=g16[0:32, :])
                    p.wait("gpsimd", sW, t)
                    g16rep = p.emit("gpsimd", "dma_start", sem=sW, k=16, out=g16[64:128, :], in_=g16[0:64, :])
                    idx_thr[e] = g16rep

                def emit_gather(e):
                    p.wait("gpsimd", sW, idx_thr[e])
                    if e == 0:
                        p.wait("gpsimd", sT, rank_T)
                        p.wait("gpsimd", sV, rank_V)
                        p.wait("gpsimd", sS, rank_S)
                        p.wait("gpsimd", sO, xst_all)
                        p.wait("gpsimd", sW, inv_done)
                    elif e >= 2:
                        p.wait("gpsimd", sT, xs_free_by[e - 2])
                        p.wait("gpsimd", sW, scat_done[e - 2])
                    xs_ld_by[e] = p.emit("gpsimd", "dma_gather", XsB[e % 2][:, :, :], xlt_d,
                                         g16[:, :], CAP, CAP, H, transpose=True, sem=sW, k=16)
                    s32_ld_by[e] = p.emit("gpsimd", "dma_start", sem=sW, k=16, out=s32B[e % 2][:, :],
                                          in_=bass.AP(inv32.tensor, e * CAP, [[NST, 128], [1, NST]]))

                emit_idx_sync(0)
                emit_gather(0)

                h_free = [(sS, 0), (sV, 0)]
                ys_scat = {}
                ys_evac = {}
                for e in range(NEXP):
                    Xs = XsB[e % 2]
                    xs_ld = xs_ld_by[e]
                    s32 = s32B[e % 2]
                    for hf in range(2):
                        w1_free = 0
                        w2_free = 0
                        fgrp = 0
                        for f in range(FH):
                            hb = fgrp % 2
                            reg = (0, 512) if hb == 0 else (512, 1024)
                            if f % 8 == 0:
                                p.wait("tensor", sW, w1_thr[(e, hf, f // 8)])
                            if f == 0:
                                p.wait("tensor", sW, xs_ld)
                                p.wait("tensor", sW, b_thr[e][0])
                            semref, cnt = h_free[hb]
                            p.wait("tensor", semref, cnt)
                            for h in range(HC):
                                p.emit("tensor", "matmul", p_h[:, reg[0]:reg[1]],
                                       w1b[:, h, f * 128:(f + 1) * 128], Xs[:, h, :],
                                       start=(h == 0), stop=(h == HC - 1), sem=(sT if h == HC - 1 else None))
                            g_n = sT.n
                            if f == FH - 1:
                                w1_free = g_n
                                if hf == 1:
                                    xs_free_by[e] = g_n
                            if hb == 0:
                                p.wait("scalar", sT, g_n)
                                p.emit("scalar", "activation", h1[:, f, :], p_h[:, reg[0]:reg[1]], AF.Relu,
                                       bias=b1c[:, hf * FH + f:hf * FH + f + 1], scale=1.0, sem=sS)
                                h_free[0] = (sS, sS.n)
                            else:
                                p.wait("vector", sT, g_n)
                                p.emit("vector", "tensor_scalar", h1[:, f, :], p_h[:, reg[0]:reg[1]],
                                       b1c[:, hf * FH + f:hf * FH + f + 1], 0.0, OP.add, OP.max, sem=sV)
                                h_free[1] = (sV, sV.n)
                            fgrp += 1
                        h1_s = sS.n
                        h1_v = sV.n
                        for st in range(CAP // 128):
                            yp = p_yA if (st % 2 == 0) else p_yB
                            key = (e, hf, st)
                            prevk = None
                            if st >= 2:
                                prevk = (e, hf, st - 2)
                            elif hf == 1:
                                prevk = (e, 0, st + 2)
                            elif e >= 1:
                                prevk = (e - 1, 1, st + 2)
                            if prevk is not None and prevk in ys_evac:
                                eng, cnt = ys_evac[prevk]
                                p.wait("tensor", eng, cnt)
                            p.wait("tensor", sS, h1_s)
                            p.wait("tensor", sV, h1_v)
                            if hf == 0:
                                p.wait("tensor", sW, b_thr[e][1])
                                p.emit("tensor", "matmul", yp[:, 0:512], ones_col[:, :], b2r[:, 0:512], start=True, stop=False)
                                p.emit("tensor", "matmul", yp[:, 512:1024], ones_col[:, :], b2r[:, 512:1024], start=True, stop=False)
                            for f in range(FH):
                                if st == 0 and f % 8 == 0:
                                    p.wait("tensor", sW, w2_thr[(e, hf, f // 8)])
                                st0 = (hf == 1 and f == 0)
                                lastf = (f == FH - 1)
                                p.emit("tensor", "matmul", yp[:, 0:512], h1[:, f, st * 128:(st + 1) * 128],
                                       w2b[:, f, 0:512], start=st0, stop=lastf)
                                p.emit("tensor", "matmul", yp[:, 512:1024], h1[:, f, st * 128:(st + 1) * 128],
                                       w2b[:, f, 512:1024], start=st0, stop=lastf,
                                       sem=(sT if lastf else None))
                            mm2_n = sT.n
                            if st == CAP // 128 - 1:
                                w2_free = mm2_n
                            if hf == 0:
                                p.wait("scalar", sT, mm2_n)
                                if e >= 1:
                                    p.wait("scalar", sW, ys_scat[(e - 1, st)])
                                p.emit("scalar", "activation", Ys[:, st, :], yp[:, 0:1024], AF.Identity,
                                       bias=c_zero[:, :], scale=1.0, sem=sS)
                                ys_evac[key] = (sS, sS.n)
                            else:
                                p.wait("vector", sT, mm2_n)
                                p.emit("vector", "tensor_tensor", Ys[:, st, :], yp[:, 0:1024], Ys[:, st, :], OP.add, sem=sV)
                                ys_evac[key] = (sV, sV.n)
                        b_free = (sS.n, sV.n, sT.n)
                        if hf == 0:
                            emit_weight_loads(e, 1, w1_free, w2_free, b_free)
                            if e < NEXP - 1:
                                emit_idx_sync(e + 1)
                                emit_gather(e + 1)
                        elif e < NEXP - 1:
                            emit_weight_loads(e + 1, 0, w1_free, w2_free, b_free)
                    p.wait("gpsimd", sW, s32_ld_by[e])
                    p.wait("gpsimd", sO, yz_done)
                    for st in range(CAP // 128):
                        eng, cnt = ys_evac[(e, 1, st)]
                        p.wait("gpsimd", eng, cnt)
                        ys_scat[(e, st)] = emit_indirect(yscr, s32[:, st:st + 1], Ys[:, st, :], "tok", sW)
                    scat_done[e] = sW.n
                scat_all = sW.n

                yld = {}
                for k in range(4):
                    p.wait("sync", sW, scat_all)
                    if k >= 1:
                        p.wait("sync", sV, add_done)
                    yld[k] = p.emit("sync", "dma_start", sem=sL2, k=16, out=Ys[:, :, :],
                                    in_=yscr.rearrange("(i p) h -> p i h", p=128)[:, 4 * k:4 * (k + 1), :])
                    for i in range(4):
                        tt = 4 * k + i
                        p.wait("scalar", sL2, yld[k])
                        p.emit("scalar", "activation", Ys[:, i, :], Ys[:, i, :], AF.Identity,
                               bias=c_zero[:, :], scale=rstat[:, GMX, tt:tt + 1], sem=sS)
                        p.wait("vector", sS, sS.n)
                        p.emit("vector", "tensor_tensor", x_tok[:, tt, :], Ys[:, i, :], x_tok[:, tt, :], OP.add, sem=sV)
                    add_done = sV.n
                    p.wait("sync", sV, add_done)
                    p.emit("sync", "dma_start", sem=sO, k=16,
                           out=y_d.rearrange("(i p) h -> p i h", p=128)[:, 4 * k:4 * (k + 1), :],
                           in_=x_tok[:, 4 * k:4 * (k + 1), :])
                p.wait("sync", sO, sO.n)
                p.flush(blkB)
    import bass_rust as _bass_rust
    from concourse.library_config import all_libraries, standard
    lm = {}
    for lib in all_libraries:
        for it in lib.instructions:
            lm[it] = lm.get(it, 0) | (1 << lib.index)
    _bass_rust.insert_library_loads(nc, lm, len(all_libraries), standard.index)
    mybir.codegen_inst_isa_subclasses(nc)
    return nc


def _consts():
    lam = np.float64(DECAY)
    jl = np.arange(128)
    tl = np.arange(128)
    mk = np.where(jl[:, None] < tl[None, :], lam ** (tl[None, :] - 1 - jl[:, None]), 0.0) / B
    mask = np.concatenate([mk] * B, axis=1).astype(np.float32)
    dq = np.concatenate([np.broadcast_to(lam ** tl, (128, 128))] * B, axis=1).astype(np.float32)
    lamk = (lam ** (127 - jl) / B).astype(np.float32).reshape(128, 1)
    ident = np.eye(128, dtype=ml_dtypes.bfloat16)
    lstrict = np.triu(np.ones((128, 128)), 1).astype(ml_dtypes.bfloat16)
    tt_i = np.arange(128) // NEXP
    e_i = np.arange(128) % NEXP
    ltile = ((e_i[:, None] == e_i[None, :]) & (tt_i[:, None] < tt_i[None, :])).astype(ml_dtypes.bfloat16)
    eoff = np.broadcast_to((e_i * CAP).astype(np.float32), (128, 128)).copy()
    identf = np.eye(128, dtype=np.float32)
    tok16 = (np.arange(128)[:, None] + 128 * np.arange(NT)[None, :]).astype(np.int16)
    tok32 = tok16.astype(np.uint32)
    return mask, np.ascontiguousarray(dq), lamk, ident, lstrict, ltile, eoff, identf, tok16, tok32


_NC_CACHE = {}
LAST_RES = None


def kernel(L0_attn_x, L0_attn_Wq, L0_attn_Wk, L0_attn_Wv, L0_attn_Wo, L0_attn_gate, L0_attn_M0,
           L0_ffn_Wr, L0_ffn_br, L0_ffn_W1, L0_ffn_b1, L0_ffn_W2, L0_ffn_b2):
    x = np.asarray(L0_attn_x, dtype=np.float32)
    wqt = np.ascontiguousarray(np.asarray(L0_attn_Wq, np.float32).T)
    wkt = np.ascontiguousarray(np.asarray(L0_attn_Wk, np.float32).T)
    wvt = np.ascontiguousarray(np.asarray(L0_attn_Wv, np.float32).T)
    wot = np.ascontiguousarray(np.asarray(L0_attn_Wo, np.float32).T)
    gate = np.asarray(L0_attn_gate, np.float32).reshape(1, 1)
    m0t = np.ascontiguousarray(np.asarray(L0_attn_M0, np.float32).T)
    wr = np.ascontiguousarray(np.asarray(L0_ffn_Wr, np.float32))
    br = np.asarray(L0_ffn_br, np.float32).reshape(1, NEXP)
    w1 = np.ascontiguousarray(
        np.asarray(L0_ffn_W1, np.float32).reshape(NEXP, HC, 128, 2, 2, 1024)
        .transpose(0, 3, 4, 2, 1, 5))
    b1 = np.ascontiguousarray(np.asarray(L0_ffn_b1, np.float32).reshape(NEXP, FC, 128).transpose(0, 2, 1))
    w2 = np.ascontiguousarray(
        np.asarray(L0_ffn_W2, np.float32).reshape(NEXP, 2, 2, HC, 128, 1024)
        .transpose(0, 1, 2, 4, 3, 5))
    b2 = np.ascontiguousarray(np.asarray(L0_ffn_b2, np.float32))
    mask, dq, lamk, ident, lstrict, ltile, eoff, identf, tok16, tok32 = _consts()

    if "nc" not in _NC_CACHE:
        _NC_CACHE["nc"] = build()
    nc = _NC_CACHE["nc"]

    lam = np.float64(DECAY)
    in_maps = []
    for c in range(NCORE):
        xs = np.ascontiguousarray(x[:, c * SC:(c + 1) * SC, :]).reshape(T, H)
        w = np.zeros(9, np.float64)
        for k in range(c):
            w[k] = lam ** (SC * (c - 1 - k))
        w[8] = lam ** (SC * c)
        prefw = np.ascontiguousarray(np.tile(w.astype(np.float32)[None, :], (128, 1)))
        in_maps.append({
            "x": xs, "wqt": wqt, "wkt": wkt, "wvt": wvt, "wot": wot, "gate": gate, "m0t": m0t,
            "wr": wr, "br": br, "w1": w1, "b1": b1, "w2": w2, "b2": b2,
            "mask": mask, "dq": dq, "lamk": lamk, "prefw": prefw, "ident": ident,
            "lstrict": lstrict, "ltile": ltile, "eoff": eoff, "identf": identf,
            "tok16": tok16, "tok32": tok32,
        })
    res = run_bass_kernel_spmd(nc, in_maps, core_ids=list(range(NCORE)))
    global LAST_RES
    LAST_RES = res
    outs = [np.asarray(res.results[c]["y"], dtype=np.float32).reshape(B, SC, H) for c in range(NCORE)]
    return np.concatenate(outs, axis=1)


# version 09
# speedup vs baseline: 2.8438x; 1.0382x over previous
import numpy as np
import ml_dtypes
from contextlib import ExitStack

import concourse.bass as bass
import concourse.mybir as mybir
from concourse.alu_op_type import AluOpType
from concourse.bass_utils import run_bass_kernel_spmd

F32 = mybir.dt.float32
BF16 = mybir.dt.bfloat16
AF = mybir.ActivationFunctionType
AX = mybir.AxisListType
OP = AluOpType
I16 = mybir.dt.int16
U32 = mybir.dt.uint32

NCORE = 8
B = 4
S = 4096
H = 1024
M = 256
NEXP = 8
DFF = 4096
DECAY = 0.99
SC = S // NCORE
T = B * SC
C = 128
NCH = SC // C
NT = T // 128
TB = 512
NTB = T // TB
HC = H // 128
FC = DFF // 128
FH = FC // 2
CAP = 512
NSLOT = NEXP * CAP
LAM128 = float(np.float32(np.float64(DECAY) ** 128))
EPS = 1e-5

ENGS = ("sync", "gpsimd", "tensor", "vector", "scalar")


class Ctr:
    def __init__(self, h):
        self.h = h
        self.n = 0


class Prog:
    def __init__(self):
        self.q = {k: [] for k in ENGS}

    def add(self, eng, f):
        self.q[eng].append(f)

    def wait(self, eng, sem, val):
        assert val <= sem.n, f"wait target {val} > planned {sem.n} ({eng})"
        if val <= 0:
            return
        self.add(eng, lambda e, s=sem.h, v=val: e.wait_ge(s, v))

    def emit(self, eng, meth, *a, sem=None, k=1, **kw):
        if sem is not None:
            sem.n += k
            self.add(eng, lambda e, m=meth, a=a, kw=kw, sh=sem.h, kk=k:
                     getattr(e, m)(*a, **kw).then_inc(sh, kk))
            return sem.n
        self.add(eng, lambda e, m=meth, a=a, kw=kw: getattr(e, m)(*a, **kw))
        return None

    def flush(self, blk):
        for eng in ENGS:
            fns = self.q[eng]
            self.q[eng] = []
            if fns:
                getattr(blk, eng)(lambda e, fns=fns: [f(e) for f in fns])


def build():
    nc = bass.Bass("TRN2", target_bir_lowering=False, detect_race_conditions=False)
    x_d = nc.dram_tensor("x", [T, H], F32, kind="ExternalInput").ap()
    wqt_d = nc.dram_tensor("wqt", [H, M], F32, kind="ExternalInput").ap()
    wkt_d = nc.dram_tensor("wkt", [H, M], F32, kind="ExternalInput").ap()
    wvt_d = nc.dram_tensor("wvt", [H, M], F32, kind="ExternalInput").ap()
    wot_d = nc.dram_tensor("wot", [M, H], F32, kind="ExternalInput").ap()
    gate_d = nc.dram_tensor("gate", [1, 1], F32, kind="ExternalInput").ap()
    m0t_d = nc.dram_tensor("m0t", [M, M], F32, kind="ExternalInput").ap()
    wr_d = nc.dram_tensor("wr", [H, NEXP], F32, kind="ExternalInput").ap()
    br_d = nc.dram_tensor("br", [1, NEXP], F32, kind="ExternalInput").ap()
    w1_d = nc.dram_tensor("w1", [NEXP, 2, 2, 128, HC, 1024], F32, kind="ExternalInput").ap()
    b1_d = nc.dram_tensor("b1", [NEXP, 128, FC], F32, kind="ExternalInput").ap()
    w2_d = nc.dram_tensor("w2", [NEXP, 2, 2, 128, HC, 1024], F32, kind="ExternalInput").ap()
    b2_d = nc.dram_tensor("b2", [NEXP, H], F32, kind="ExternalInput").ap()
    mask_d = nc.dram_tensor("mask", [128, 512], F32, kind="ExternalInput").ap()
    dq_d = nc.dram_tensor("dq", [128, 512], F32, kind="ExternalInput").ap()
    lamk_d = nc.dram_tensor("lamk", [128, 1], F32, kind="ExternalInput").ap()
    prefw_d = nc.dram_tensor("prefw", [128, 9], F32, kind="ExternalInput").ap()
    ident_d = nc.dram_tensor("ident", [128, 128], BF16, kind="ExternalInput").ap()
    y_d = nc.dram_tensor("y", [T, H], F32, kind="ExternalOutput").ap()
    lstrict_d = nc.dram_tensor("lstrict", [128, 128], BF16, kind="ExternalInput").ap()
    ltile_d = nc.dram_tensor("ltile", [128, 128], BF16, kind="ExternalInput").ap()
    eoff_d = nc.dram_tensor("eoff", [128, 128], F32, kind="ExternalInput").ap()
    identf_d = nc.dram_tensor("identf", [128, 128], F32, kind="ExternalInput").ap()
    tok16_d = nc.dram_tensor("tok16", [128, NT], mybir.dt.int16, kind="ExternalInput").ap()
    tok32_d = nc.dram_tensor("tok32", [128, NT], mybir.dt.uint32, kind="ExternalInput").ap()
    xlt_d = nc.dram_tensor("xlt", [T + 16, H], BF16).ap()
    yscr = nc.dram_tensor("yscr", [T, H], F32).ap()
    inv16 = nc.dram_tensor("inv16", [NSLOT, 1], mybir.dt.int16).ap()
    inv32 = nc.dram_tensor("inv32", [NSLOT, 1], mybir.dt.uint32).ap()
    agin = nc.dram_tensor("agin", [M, M], F32).ap()
    agout = nc.dram_tensor("agout", [NCORE * M, M], F32).ap()

    p = Prog()
    with ExitStack() as ctx:
        E = ctx.enter_context
        sLd = Ctr(E(nc.semaphore("sLd")))
        sW = Ctr(E(nc.semaphore("sW")))
        sT = Ctr(E(nc.semaphore("sT")))
        sV = Ctr(E(nc.semaphore("sV")))
        sS = Ctr(E(nc.semaphore("sS")))
        sC = Ctr(E(nc.semaphore("sC")))
        sG = Ctr(E(nc.semaphore("sG")))
        sL2 = Ctr(E(nc.semaphore("sL2")))
        sO = Ctr(E(nc.semaphore("sO")))
        x_tok = E(nc.sbuf_tensor("x_tok", [128, NT, H], F32))
        ident = E(nc.sbuf_tensor("ident_sb", [128, 128], BF16))
        ones_col = E(nc.sbuf_tensor("ones_col", [1, 128], BF16))
        c_eps = E(nc.sbuf_tensor("c_eps", [128, 1], F32))
        c_zero = E(nc.sbuf_tensor("c_zero", [128, 1], F32))
        c_one = E(nc.sbuf_tensor("c_one", [128, 1], F32))

        with ExitStack() as ctxA:
            EA = ctxA.enter_context
            xln1 = EA(nc.sbuf_tensor("xln1", [128, HC, T], BF16))
            xa = EA(nc.sbuf_tensor("xa", [128, 2, H], BF16))
            stats = EA(nc.sbuf_tensor("stats", [128, 9, NT], F32))
            wq_sb = EA(nc.sbuf_tensor("wq_sb", [128, HC, M], BF16))
            wk_sb = EA(nc.sbuf_tensor("wk_sb", [128, HC, M], BF16))
            wv_sb = EA(nc.sbuf_tensor("wv_sb", [128, HC, M], BF16))
            wot_sb = EA(nc.sbuf_tensor("wot_sb", [128, 2, H], BF16))
            qT = EA(nc.sbuf_tensor("qT", [128, 2, T], BF16))
            kT = EA(nc.sbuf_tensor("kT", [128, 2, T], BF16))
            ktok = EA(nc.sbuf_tensor("ktok", [128, NT, M], BF16))
            vtok = EA(nc.sbuf_tensor("vtok", [128, NT, M], BF16))
            rT = EA(nc.sbuf_tensor("rT", [128, 2, T], BF16))
            sTb = EA(nc.sbuf_tensor("sTb", [128, B, 512], BF16))
            qs = EA(nc.sbuf_tensor("qs", [128, 2, 512], BF16))
            Ast = EA(nc.sbuf_tensor("Ast", [128, 2, M], F32))
            AstBf = EA(nc.sbuf_tensor("AstBf", [128, 2, M], BF16))
            dA = EA(nc.sbuf_tensor("dA", [128, NCH, 2, M], F32))
            Zw = EA(nc.sbuf_tensor("Zw", [128, 2, M], F32))
            Zw2 = EA(nc.sbuf_tensor("Zw2", [128, 2, M], F32))
            m0t_sb = EA(nc.sbuf_tensor("m0t_sb", [128, 2, M], F32))
            Zall = EA(nc.sbuf_tensor("Zall", [128, 2 * NCORE, M], F32))
            mask_sb = EA(nc.sbuf_tensor("mask_sb", [128, 512], F32))
            dq_sb = EA(nc.sbuf_tensor("dq_sb", [128, 512], F32))
            lamk_sb = EA(nc.sbuf_tensor("lamk_sb", [128, 1], F32))
            prefw_sb = EA(nc.sbuf_tensor("prefw_sb", [128, 9], F32))
            gate_sb = EA(nc.sbuf_tensor("gate_sb", [1, 1], F32))
            sig_s = EA(nc.sbuf_tensor("sig_s", [1, 4], F32))
            sig_bf = EA(nc.sbuf_tensor("sig_bf", [1, 1], BF16))
            sig_bc = EA(nc.sbuf_tensor("sig_bc", [128, 1], F32))
            p_tr = EA(nc.psum_tensor("p_tr", [128, 2, 1024], BF16))
            p_pr = EA(nc.psum_tensor("p_pr", [128, 1024], F32))
            p_s = EA(nc.psum_tensor("p_s", [128, 1024], F32))
            p_r = EA(nc.psum_tensor("p_r", [128, 1024], F32))

            with nc.Block("A", no_gpsimd_drain=True) as blkA:
                x_ld = []
                for k in range(4):
                    x_ld.append(p.emit("sync", "dma_start", sem=sLd, k=16,
                                       out=x_tok[:, 4 * k:4 * (k + 1), :],
                                       in_=x_d.rearrange("(i p) h -> p i h", p=128)[:, 4 * k:4 * (k + 1), :]))
                m0_ld = p.emit("sync", "dma_start", sem=sLd, k=16, out=m0t_sb[:, :, :],
                               in_=m0t_d.rearrange("(c p) m -> p c m", p=128))
                msk_ld = p.emit("sync", "dma_start", sem=sLd, k=16, out=mask_sb[:, :], in_=mask_d)
                dq_ld = p.emit("sync", "dma_start", sem=sLd, k=16, out=dq_sb[:, :], in_=dq_d)
                lamk_ld = p.emit("sync", "dma_start", sem=sLd, k=16, out=lamk_sb[:, :], in_=lamk_d)
                prefw_ld = p.emit("sync", "dma_start", sem=sLd, k=16, out=prefw_sb[:, :], in_=prefw_d)
                gate_ld = p.emit("sync", "dma_start", sem=sLd, k=16, out=gate_sb[:, :], in_=gate_d)
                id_ld = p.emit("sync", "dma_start", sem=sLd, k=16, out=ident[:, :], in_=ident_d)
                a_total = sLd.n

                p.emit("gpsimd", "memset", c_eps[:, :], EPS, sem=sG)
                p.emit("gpsimd", "memset", c_zero[:, :], 0.0, sem=sG)
                p.emit("gpsimd", "memset", c_one[:, :], 1.0, sem=sG)
                p.emit("gpsimd", "memset", ones_col[:, :], 1.0, sem=sG)
                consts_done = sG.n
                wq_ld = p.emit("gpsimd", "dma_start", sem=sW, k=16, out=wq_sb[:, :, :],
                               in_=wqt_d.rearrange("(c p) m -> p c m", p=128))
                wk_ld = p.emit("gpsimd", "dma_start", sem=sW, k=16, out=wk_sb[:, :, :],
                               in_=wkt_d.rearrange("(c p) m -> p c m", p=128))
                wv_ld = p.emit("gpsimd", "dma_start", sem=sW, k=16, out=wv_sb[:, :, :],
                               in_=wvt_d.rearrange("(c p) m -> p c m", p=128))
                wo_ld = p.emit("gpsimd", "dma_start", sem=sW, k=16, out=wot_sb[:, :, :],
                               in_=wot_d.rearrange("(c p) h -> p c h", p=128))

                p.wait("scalar", sLd, a_total)
                p.wait("scalar", sG, consts_done)
                p.emit("scalar", "activation", sig_s[:, 0:1], gate_sb[:, :], AF.Exp, bias=c_zero[0:1, :], scale=-1.0)
                p.emit("scalar", "activation", sig_s[:, 1:2], sig_s[:, 0:1], AF.Identity, bias=c_one[0:1, :], scale=1.0)
                p.emit("scalar", "activation", sig_s[:, 2:3], sig_s[:, 1:2], AF.Ln, bias=c_zero[0:1, :], scale=1.0)
                p.emit("scalar", "activation", sig_s[:, 3:4], sig_s[:, 2:3], AF.Exp, bias=c_zero[0:1, :], scale=-1.0)
                p.emit("scalar", "activation", sig_bf[:, :], sig_s[:, 3:4], AF.Identity, bias=c_zero[0:1, :], scale=1.0, sem=sS)
                sigbf_done = sS.n
                p.wait("tensor", sS, sigbf_done)
                p.emit("tensor", "matmul", p_pr[:, 0:1], ones_col[:, :], sig_bf[:, :], start=True, stop=True, sem=sT)
                p.wait("vector", sT, sT.n)
                p.emit("vector", "tensor_copy", sig_bc[:, :], p_pr[:, 0:1], sem=sV)
                sigbc_done = sV.n
                p.wait("scalar", sW, wo_ld)
                p.wait("scalar", sV, sigbc_done)
                p.emit("scalar", "activation", wot_sb[:, 0, :], wot_sb[:, 0, :], AF.Identity, bias=c_zero[:, :], scale=sig_bc[:, 0:1])
                p.emit("scalar", "activation", wot_sb[:, 1, :], wot_sb[:, 1, :], AF.Identity, bias=c_zero[:, :], scale=sig_bc[:, 0:1], sem=sS)
                wotg_done = sS.n

                SUM, SQS, MEAN, EX2, VAR, MSQ, RSTD, MS, NEGMS = range(9)
                for tt in range(NT):
                    p.wait("vector", sLd, a_total)
                    p.emit("vector", "tensor_reduce", stats[:, SUM, tt:tt + 1], x_tok[:, tt, :], axis=AX.X, op=OP.add, sem=sV)
                    p.wait("scalar", sLd, a_total)
                    p.emit("scalar", "activation", qT[:, 0, 0:H], x_tok[:, tt, :], AF.Square,
                           bias=c_zero[:, :], scale=1.0, accum_out=stats[:, SQS, tt:tt + 1], sem=sS)
                sum_done = sV.n
                p.wait("scalar", sV, sum_done)
                p.emit("scalar", "activation", stats[:, MEAN, :], stats[:, SUM, :], AF.Identity, bias=c_zero[:, :], scale=1.0 / H)
                p.emit("scalar", "activation", stats[:, EX2, :], stats[:, SQS, :], AF.Identity, bias=c_zero[:, :], scale=1.0 / H)
                p.emit("scalar", "activation", stats[:, MSQ, :], stats[:, MEAN, :], AF.Square, bias=c_zero[:, :], scale=1.0, sem=sS)
                p.wait("vector", sS, sS.n)
                p.emit("vector", "tensor_tensor", stats[:, VAR, :], stats[:, EX2, :], stats[:, MSQ, :], OP.subtract, sem=sV)
                p.wait("scalar", sV, sV.n)
                p.emit("scalar", "activation", stats[:, RSTD, :], stats[:, VAR, :], AF.Ln, bias=c_eps[:, :], scale=1.0)
                p.emit("scalar", "activation", stats[:, RSTD, :], stats[:, RSTD, :], AF.Exp, bias=c_zero[:, :], scale=-0.5, sem=sS)
                p.wait("vector", sS, sS.n)
                p.emit("vector", "tensor_tensor", stats[:, MS, :], stats[:, MEAN, :], stats[:, RSTD, :], OP.mult, sem=sV)
                p.wait("scalar", sV, sV.n)
                p.emit("scalar", "activation", stats[:, NEGMS, :], stats[:, MS, :], AF.Identity, bias=c_zero[:, :], scale=-1.0, sem=sS)
                negms_done = sS.n

                ping_ev = [0, 0]
                aff_free = [0, 0]
                for tt in range(NT):
                    pi = tt % 2
                    p.wait("scalar", sS, negms_done)
                    p.wait("scalar", sT, aff_free[pi])
                    p.emit("scalar", "activation", xa[:, pi, :], x_tok[:, tt, :], AF.Identity,
                           bias=stats[:, NEGMS, tt:tt + 1], scale=stats[:, RSTD, tt:tt + 1], sem=sS)
                    a_n = sS.n
                    for h in range(HC):
                        bi = h % 2
                        p.wait("tensor", sS, a_n)
                        p.wait("tensor", sV, ping_ev[bi])
                        if tt == 0 and h == 0:
                            p.wait("tensor", sLd, a_total)
                        p.emit("tensor", "transpose", p_tr[:, bi, 0:128], xa[:, pi, h * 128:(h + 1) * 128], ident[:, :], sem=sT)
                        p.wait("vector", sT, sT.n)
                        p.emit("vector", "tensor_copy", xln1[:, h, tt * 128:(tt + 1) * 128], p_tr[:, bi, 0:128], sem=sV)
                        ping_ev[bi] = sV.n
                    aff_free[pi] = sT.n
                xln1_done_v = sV.n

                proj_ping_v = [sV.n, sV.n]
                pp = 0
                for (wsb, dst) in ((wq_sb, qT), (wk_sb, kT)):
                    for mt in range(2):
                        for tb4 in range(4):
                            half = pp % 2
                            p.wait("tensor", sV, proj_ping_v[half])
                            if pp == 0:
                                p.wait("tensor", sV, xln1_done_v)
                                p.wait("tensor", sW, wv_ld)
                            for h in range(HC):
                                p.emit("tensor", "matmul", p_pr[:, half * 512:half * 512 + 512],
                                       wsb[:, h, mt * 128:(mt + 1) * 128], xln1[:, h, tb4 * 512:(tb4 + 1) * 512],
                                       start=(h == 0), stop=(h == HC - 1), sem=(sT if h == HC - 1 else None))
                            p.wait("vector", sT, sT.n)
                            p.emit("vector", "tensor_copy", dst[:, mt, tb4 * 512:(tb4 + 1) * 512],
                                   p_pr[:, half * 512:half * 512 + 512], sem=sV)
                            proj_ping_v[half] = sV.n
                            pp += 1
                qk_done_v = sV.n
                proj_ping_s = [0, 0]
                for tt in range(NT):
                    for (wsb, dst, scale_ap) in ((wk_sb, ktok, lamk_sb[:, 0:1]), (wv_sb, vtok, None)):
                        half = pp % 2
                        p.wait("tensor", sS, proj_ping_s[half])
                        p.wait("tensor", sV, proj_ping_v[half])
                        for h in range(HC):
                            p.emit("tensor", "matmul", p_pr[:, half * 512:half * 512 + 256],
                                   xln1[:, h, tt * 128:(tt + 1) * 128], wsb[:, h, :],
                                   start=(h == 0), stop=(h == HC - 1), sem=(sT if h == HC - 1 else None))
                        p.wait("scalar", sT, sT.n)
                        if tt == 0 and scale_ap is not None:
                            p.wait("scalar", sLd, a_total)
                        p.emit("scalar", "activation", dst[:, tt, :], p_pr[:, half * 512:half * 512 + 256],
                               AF.Identity, bias=c_zero[:, :],
                               scale=(scale_ap if scale_ap is not None else 1.0), sem=sS)
                        proj_ping_s[half] = sS.n
                        pp += 1
                kv_done_s = sS.n

                p.wait("tensor", sS, kv_done_s)
                da_ping = [0, 0]
                for cl in range(NCH):
                    for mk in range(2):
                        reg = (0, 256) if mk == 0 else (512, 768)
                        p.wait("tensor", sV, da_ping[mk])
                        for bp in range(B):
                            tt = bp * NCH + cl
                            p.emit("tensor", "matmul", p_pr[:, reg[0]:reg[1]],
                                   ktok[:, tt, mk * 128:(mk + 1) * 128], vtok[:, tt, :],
                                   start=(bp == 0), stop=(bp == B - 1), sem=(sT if bp == B - 1 else None))
                        p.wait("vector", sT, sT.n)
                        p.emit("vector", "tensor_copy", dA[:, cl, mk, :], p_pr[:, reg[0]:reg[1]], sem=sV)
                        da_ping[mk] = sV.n
                p.emit("vector", "tensor_copy", Zw[:, :, :], dA[:, 0, :, :], sem=sV)
                for cl in range(1, NCH):
                    p.wait("scalar", sV, sV.n)
                    p.emit("scalar", "activation", Zw2[:, :, :], Zw[:, :, :], AF.Identity, bias=c_zero[:, :], scale=LAM128, sem=sS)
                    p.wait("vector", sS, sS.n)
                    p.emit("vector", "tensor_tensor", Zw[:, :, :], Zw2[:, :, :], dA[:, cl, :, :], OP.add, sem=sV)
                z_done_v = sV.n

                p.wait("gpsimd", sV, z_done_v)
                p.emit("gpsimd", "dma_start", sem=sW, k=16, out=agin.rearrange("(c p) m -> p c m", p=128), in_=Zw[:, :, :])
                p.wait("gpsimd", sW, sW.n)
                p.emit("gpsimd", "collective_compute", "AllGather", OP.bypass,
                       replica_groups=[list(range(NCORE))], ins=[agin.opt()], outs=[agout.opt()], sem=sC)
                p.wait("gpsimd", sC, sC.n)
                zall_ld = p.emit("gpsimd", "dma_start", sem=sW, k=16, out=Zall[:, :, :],
                                 in_=agout.rearrange("(c p) m -> p c m", p=128))

                p.wait("scalar", sW, zall_ld)
                p.wait("scalar", sLd, a_total)
                p.emit("scalar", "activation", Ast[:, :, :], m0t_sb[:, :, :], AF.Identity, bias=c_zero[:, :],
                       scale=prefw_sb[:, 8:9], sem=sS)
                for k in range(NCORE):
                    p.emit("scalar", "activation", Zw2[:, :, :], Zall[:, 2 * k:2 * k + 2, :], AF.Identity,
                           bias=c_zero[:, :], scale=prefw_sb[:, k:k + 1], sem=sS)
                    p.wait("vector", sS, sS.n)
                    p.emit("vector", "tensor_tensor", Ast[:, :, :], Zw2[:, :, :], Ast[:, :, :], OP.add, sem=sV)
                    p.wait("scalar", sV, sV.n)
                p.emit("vector", "tensor_copy", AstBf[:, :, :], Ast[:, :, :], sem=sV)
                astbf_done = sV.n

                s_ping = [sV.n, sV.n]
                for cl in range(NCH):
                    stb_ev = {}
                    for bp in range(B):
                        half = bp % 2
                        p.wait("tensor", sV, s_ping[half])
                        if cl == 0 and bp == 0:
                            p.wait("tensor", sV, qk_done_v)
                        for mk in range(2):
                            p.emit("tensor", "matmul", p_s[:, half * 512:half * 512 + 512],
                                   kT[:, mk, bp * SC + cl * 128: bp * SC + cl * 128 + 128],
                                   qT[:, mk, :].rearrange("p (b s) -> p b s", b=B)[:, :, cl * 128:(cl + 1) * 128],
                                   start=(mk == 0), stop=(mk == 1), sem=(sT if mk == 1 else None))
                        p.wait("vector", sT, sT.n)
                        if cl == 0 and bp == 0:
                            p.wait("vector", sLd, a_total)
                            p.wait("vector", sLd, a_total)
                        p.emit("vector", "tensor_tensor", sTb[:, bp, :],
                               p_s[:, half * 512:half * 512 + 512], mask_sb[:, :], OP.mult, sem=sV)
                        stb_ev[bp] = sV.n
                        s_ping[half] = sV.n
                    for mk in range(2):
                        p.emit("vector", "tensor_tensor",
                               qs[:, mk, :].rearrange("p (b s) -> p b s", b=B),
                               qT[:, mk, :].rearrange("p (b s) -> p b s", b=B)[:, :, cl * 128:(cl + 1) * 128],
                               dq_sb.rearrange("p (b s) -> p b s", b=B), OP.mult, sem=sV)
                    qs_done = sV.n
                    for mv in range(2):
                        p.wait("tensor", sV, qs_done)
                        p.wait("tensor", sV, astbf_done)
                        p.emit("tensor", "matmul", p_r[:, mv * 512:mv * 512 + 512],
                               AstBf[:, 0, mv * 128:(mv + 1) * 128], qs[:, 0, :], start=True, stop=False)
                        p.emit("tensor", "matmul", p_r[:, mv * 512:mv * 512 + 512],
                               AstBf[:, 1, mv * 128:(mv + 1) * 128], qs[:, 1, :], start=False, stop=False)
                        for bp in range(B):
                            tt = bp * NCH + cl
                            p.wait("tensor", sV, stb_ev[bp])
                            p.emit("tensor", "matmul", p_r[:, mv * 512:mv * 512 + 512],
                                   vtok[:, tt, mv * 128:(mv + 1) * 128], sTb[:, bp, :],
                                   start=False, stop=(bp == B - 1), sem=(sT if bp == B - 1 else None))
                        p.wait("vector", sT, sT.n)
                        p.emit("vector", "tensor_copy",
                               rT[:, mv, :].rearrange("p (b s) -> p b s", b=B)[:, :, cl * 128:(cl + 1) * 128],
                               p_r[:, mv * 512:mv * 512 + 512].rearrange("p (b s) -> p b s", b=B), sem=sV)
                    if cl < NCH - 1:
                        p.wait("scalar", sT, sT.n)
                        p.emit("scalar", "activation", Zw2[:, :, :], Ast[:, :, :], AF.Identity, bias=c_zero[:, :],
                               scale=LAM128, sem=sS)
                        p.wait("vector", sS, sS.n)
                        p.emit("vector", "tensor_tensor", Ast[:, :, :], Zw2[:, :, :], dA[:, cl, :, :], OP.add, sem=sV)
                        p.emit("vector", "tensor_copy", AstBf[:, :, :], Ast[:, :, :], sem=sV)
                        astbf_done = sV.n
                r_done_v = sV.n

                x1_add = {}
                for tt in range(NT):
                    if tt == 0:
                        p.wait("tensor", sV, r_done_v)
                        p.wait("tensor", sS, wotg_done)
                    else:
                        p.wait("tensor", sV, x1_add[tt - 1])
                    for half in range(2):
                        for mt in range(2):
                            p.emit("tensor", "matmul", p_pr[:, half * 512:half * 512 + 512],
                                   rT[:, mt, tt * 128:(tt + 1) * 128], wot_sb[:, mt, half * 512:half * 512 + 512],
                                   start=(mt == 0), stop=(mt == 1),
                                   sem=(sT if (half == 1 and mt == 1) else None))
                    p.wait("vector", sT, sT.n)
                    p.emit("vector", "tensor_tensor", x_tok[:, tt, :], p_pr[:, 0:1024], x_tok[:, tt, :], OP.add, sem=sV)
                    x1_add[tt] = sV.n
                x1_done = sV.n
                p.flush(blkA)

        with ExitStack() as ctxB:
            EB = ctxB.enter_context
            stats2 = EB(nc.sbuf_tensor("stats2", [128, 9, NT], F32))
            wr_sb = EB(nc.sbuf_tensor("wr_sb", [128, HC, NEXP], BF16))
            br_r = EB(nc.sbuf_tensor("br_r", [1, NEXP], BF16))
            probs = EB(nc.sbuf_tensor("probs", [128, NT, NEXP], F32))
            expt = EB(nc.sbuf_tensor("expt", [128, NEXP], F32))
            rstat = EB(nc.sbuf_tensor("rstat", [128, 5, NT], F32))
            selm = EB(nc.sbuf_tensor("selm", [128, NT, NEXP], BF16))
            tmp_epi = EB(nc.sbuf_tensor("tmp_epi", [128, H], F32))
            tok16_sb = EB(nc.sbuf_tensor("tok16_sb", [128, NT], I16))
            tok32_sb = EB(nc.sbuf_tensor("tok32_sb", [128, NT], U32))
            ones128 = EB(nc.sbuf_tensor("ones128", [128, 1], BF16))
            b1c = EB(nc.sbuf_tensor("b1c", [128, FC], F32))
            b2r = EB(nc.sbuf_tensor("b2r", [1, H], BF16))
            w1b = EB(nc.sbuf_tensor("w1b", [128, HC, 2048], BF16))
            w2b = EB(nc.sbuf_tensor("w2b", [128, FH, H], BF16))
            p_yA = EB(nc.psum_tensor("p_yA", [128, 1024], F32))
            p_yB = EB(nc.psum_tensor("p_yB", [128, 1024], F32))
            p_h = EB(nc.psum_tensor("p_h", [128, 1024], F32))
            p_rt = EB(nc.psum_tensor("p_rt", [128, 512], F32))
            p_tr2 = EB(nc.psum_tensor("p_tr2", [128, 1024], BF16))

            with nc.Block("Bf") as blkB:
                SUM, SQS, MEAN, EX2, VAR, MSQ, RSTD, MS, NEGMS = range(9)
                MX, SE, LNSE, RINV, GMX = range(5)

                regbox = {}

                def _mkregs(g):
                    r1 = g.register("bnd_slot").__enter__()
                    g.reg_mov(r1, NSLOT - 1)
                    r2 = g.register("bnd_tok").__enter__()
                    g.reg_mov(r2, T - 1)
                    regbox["slot"] = r1
                    regbox["tok"] = r2
                p.add("gpsimd", _mkregs)

                def emit_indirect(out_ap, idx_ap, in_ap, bkey, sem, k=16):
                    sem.n += k
                    def f(g, out_ap=out_ap, idx_ap=idx_ap, in_ap=in_ap, bkey=bkey, sh=sem.h, kk=k):
                        g.indirect_dma_start(out_ap, bass.IndirectOffsetOnAxis(ap=idx_ap, axis=0),
                                             in_ap, None, bounds_check=regbox[bkey],
                                             oob_is_err=False).then_inc(sh, kk)
                    p.add("gpsimd", f)
                    return sem.n

                wr_ld = p.emit("gpsimd", "dma_start", sem=sW, k=16, out=wr_sb[:, :, :],
                               in_=wr_d.rearrange("(c p) e -> p c e", p=128))
                br_ld = p.emit("gpsimd", "dma_start", sem=sW, k=16, out=br_r[:, :], in_=br_d)
                tk16_ld = p.emit("gpsimd", "dma_start", sem=sW, k=16, out=tok16_sb[:, :], in_=tok16_d)
                tk32_ld = p.emit("gpsimd", "dma_start", sem=sW, k=16, out=tok32_sb[:, :], in_=tok32_d)
                p.emit("gpsimd", "memset", ones128[:, :], 1.0, sem=sG)

                w1_thr = {}
                w2_thr = {}
                b_thr = {}

                def emit_weight_loads(e, hf, w1_free, w2_free, b_free):
                    for qp in range(2):
                        if w1_free:
                            p.wait("gpsimd", sT, w1_free)
                        w1_thr[(e, hf, qp)] = p.emit(
                            "gpsimd", "dma_start", sem=sW, k=16,
                            out=w1b[:, :, qp * 1024:(qp + 1) * 1024],
                            in_=w1_d[e, hf, qp])
                    for qp in range(2):
                        if w2_free:
                            p.wait("gpsimd", sT, w2_free)
                        w2_thr[(e, hf, qp)] = p.emit(
                            "gpsimd", "dma_start", sem=sW, k=16,
                            out=w2b[:, 8 * qp:8 * (qp + 1), :],
                            in_=w2_d[e, hf, qp])
                    if hf == 0:
                        if b_free is not None:
                            p.wait("gpsimd", sS, b_free[0])
                            p.wait("gpsimd", sV, b_free[1])
                            p.wait("gpsimd", sT, b_free[2])
                        b_thr[e] = (
                            p.emit("gpsimd", "dma_start", sem=sW, k=16, out=b1c[:, :], in_=b1_d[e]),
                            p.emit("gpsimd", "dma_start", sem=sW, k=16, out=b2r[:, :], in_=b2_d[e:e + 1, :]),
                        )

                emit_weight_loads(0, 0, 0, 0, None)

                p.emit("vector", "memset", tmp_epi[:, :], 0.0, sem=sV)
                zv = sV.n
                p.wait("sync", sV, zv)
                ysz = []
                for k in range(NT):
                    ysz.append(p.emit("sync", "dma_start", sem=sO, k=16,
                                      out=yscr.rearrange("(i p) h -> p i h", p=128)[:, k:k + 1, :],
                                      in_=tmp_epi[:, :].rearrange("p (i h) -> p i h", i=1)))
                p.wait("gpsimd", sV, zv)
                xpz = p.emit("gpsimd", "dma_start", sem=sW, k=16,
                             out=xlt_d[T:T + 16, :], in_=tmp_epi[0:16, 0:H])
                yz_done = sO.n

                with ExitStack() as ctxR:
                    ER = ctxR.enter_context
                    xln2 = ER(nc.sbuf_tensor("xln2", [128, HC, T], BF16))
                    xa2 = ER(nc.sbuf_tensor("xa2", [128, 2, H], BF16))
                    sqs_scr = ER(nc.sbuf_tensor("sqs_scr", [128, H], F32))
                    Lst_sb = ER(nc.sbuf_tensor("Lst_sb", [128, 128], BF16))
                    Lti_sb = ER(nc.sbuf_tensor("Lti_sb", [128, 128], BF16))
                    eoff_sb = ER(nc.sbuf_tensor("eoff_sb", [128, 128], F32))
                    rnk = ER(nc.sbuf_tensor("rnk", [128, 128], F32))
                    cnt_bf = ER(nc.sbuf_tensor("cnt_bf", [128, 1], BF16))
                    orow16 = ER(nc.sbuf_tensor("orow16", [1, 128], mybir.dt.float16))
                    ocol = ER(nc.sbuf_tensor("ocol", [1, 128], mybir.dt.float16))
                    obc = ER(nc.sbuf_tensor("obc", [128, 128], F32))
                    slotf = ER(nc.sbuf_tensor("slotf", [128, 128], F32))
                    slotg = ER(nc.sbuf_tensor("slotg", [128, NT], F32))
                    slotu = ER(nc.sbuf_tensor("slotu", [128, NT], U32))
                    pr16 = ER(nc.sbuf_tensor("pr16", [128, 32], I16))
                    pr32 = ER(nc.sbuf_tensor("pr32", [128, 32], U32))
                    identf_sb = ER(nc.sbuf_tensor("identf_sb", [128, 128], F32))
                    cntrow = ER(nc.sbuf_tensor("cntrow", [1, 128], F32))
                    ovf = ER(nc.sbuf_tensor("ovf", [128, 128], F32))
                    m8 = ER(nc.sbuf_tensor("m8", [128, 128], F32))
                    ri32 = ER(nc.sbuf_tensor("ri32", [128, 128], mybir.dt.int32))
                    pi2 = ER(nc.sbuf_tensor("pi2", [128, 128], F32))
                    slotg2 = ER(nc.sbuf_tensor("slotg2", [128, NT], F32))
                    slotu2 = ER(nc.sbuf_tensor("slotu2", [128, NT], U32))

                    idf_ld = p.emit("sync", "dma_start", sem=sL2, k=16, out=identf_sb[:, :], in_=identf_d)
                    lst_ld = p.emit("sync", "dma_start", sem=sL2, k=16, out=Lst_sb[:, :], in_=lstrict_d)
                    lti_ld = p.emit("sync", "dma_start", sem=sL2, k=16, out=Lti_sb[:, :], in_=ltile_d)
                    eo_ld = p.emit("sync", "dma_start", sem=sL2, k=16, out=eoff_sb[:, :], in_=eoff_d)
                    b_total = sL2.n

                    for tt in range(NT):
                        if tt == 0:
                            p.wait("vector", sV, x1_done)
                        p.emit("vector", "tensor_reduce", stats2[:, SUM, tt:tt + 1], x_tok[:, tt, :], axis=AX.X, op=OP.add, sem=sV)
                        p.emit("scalar", "activation", sqs_scr[:, :], x_tok[:, tt, :], AF.Square,
                               bias=c_zero[:, :], scale=1.0, accum_out=stats2[:, SQS, tt:tt + 1], sem=sS)
                    p.wait("scalar", sV, sV.n)
                    p.emit("scalar", "activation", stats2[:, MEAN, :], stats2[:, SUM, :], AF.Identity, bias=c_zero[:, :], scale=1.0 / H)
                    p.emit("scalar", "activation", stats2[:, EX2, :], stats2[:, SQS, :], AF.Identity, bias=c_zero[:, :], scale=1.0 / H)
                    p.emit("scalar", "activation", stats2[:, MSQ, :], stats2[:, MEAN, :], AF.Square, bias=c_zero[:, :], scale=1.0, sem=sS)
                    p.wait("vector", sS, sS.n)
                    p.emit("vector", "tensor_tensor", stats2[:, VAR, :], stats2[:, EX2, :], stats2[:, MSQ, :], OP.subtract, sem=sV)
                    p.wait("scalar", sV, sV.n)
                    p.emit("scalar", "activation", stats2[:, RSTD, :], stats2[:, VAR, :], AF.Ln, bias=c_eps[:, :], scale=1.0)
                    p.emit("scalar", "activation", stats2[:, RSTD, :], stats2[:, RSTD, :], AF.Exp, bias=c_zero[:, :], scale=-0.5, sem=sS)
                    p.wait("vector", sS, sS.n)
                    p.emit("vector", "tensor_tensor", stats2[:, MS, :], stats2[:, MEAN, :], stats2[:, RSTD, :], OP.mult, sem=sV)
                    p.wait("scalar", sV, sV.n)
                    p.emit("scalar", "activation", stats2[:, NEGMS, :], stats2[:, MS, :], AF.Identity, bias=c_zero[:, :], scale=-1.0, sem=sS)
                    negms2 = sS.n

                    tr_free = 0
                    aff2_free = {0: 0, 1: 0}
                    xs_done = []
                    for tt in range(NT):
                        pi = tt % 2
                        p.wait("scalar", sS, negms2)
                        p.wait("scalar", sT, aff2_free[pi])
                        if tt >= 2:
                            p.wait("scalar", sO, xst[tt - 2])
                        p.emit("scalar", "activation", xa2[:, pi, :], x_tok[:, tt, :], AF.Identity,
                               bias=stats2[:, NEGMS, tt:tt + 1], scale=stats2[:, RSTD, tt:tt + 1], sem=sS)
                        a_n = sS.n
                        if tt == 0:
                            xst = {}
                        p.wait("sync", sS, a_n)
                        xst[tt] = p.emit("sync", "dma_start", sem=sO, k=16,
                                         out=xlt_d[0:T, :].rearrange("(i p) h -> p i h", p=128)[:, tt:tt + 1, :],
                                         in_=xa2[:, pi, :].rearrange("p (i h) -> p i h", i=1))
                        for h in range(HC):
                            p.wait("tensor", sS, a_n)
                            p.wait("tensor", sV, tr_free)
                            p.emit("tensor", "transpose", p_tr2[:, 0:128], xa2[:, pi, h * 128:(h + 1) * 128], ident[:, :], sem=sT)
                            p.wait("vector", sT, sT.n)
                            p.emit("vector", "tensor_copy", xln2[:, h, tt * 128:(tt + 1) * 128], p_tr2[:, 0:128], sem=sV)
                            tr_free = sV.n
                        aff2_free[pi] = sT.n
                    xln2_done = sV.n
                    xst_all = sO.n

                    rt_free = 0
                    for tt in range(NT):
                        if tt == 0:
                            p.wait("tensor", sV, xln2_done)
                            p.wait("tensor", sW, br_ld)
                        p.wait("tensor", sS, rt_free)
                        p.emit("tensor", "matmul", p_rt[:, 0:NEXP], ones_col[:, :], br_r[:, :], start=True, stop=False)
                        for h in range(HC):
                            p.emit("tensor", "matmul", p_rt[:, 0:NEXP], xln2[:, h, tt * 128:(tt + 1) * 128],
                                   wr_sb[:, h, :], start=False, stop=(h == HC - 1), sem=(sT if h == HC - 1 else None))
                        p.wait("vector", sT, sT.n)
                        p.emit("vector", "tensor_reduce", rstat[:, MX, tt:tt + 1], p_rt[:, 0:NEXP], axis=AX.X, op=OP.max, negate=True, sem=sV)
                        p.wait("scalar", sV, sV.n)
                        p.emit("scalar", "activation", expt[:, :], p_rt[:, 0:NEXP], AF.Exp,
                               bias=rstat[:, MX, tt:tt + 1], scale=1.0, accum_out=rstat[:, SE, tt:tt + 1], sem=sS)
                        rt_free = sS.n
                        p.emit("scalar", "activation", rstat[:, LNSE, tt:tt + 1], rstat[:, SE, tt:tt + 1], AF.Ln, bias=c_zero[:, :], scale=1.0)
                        p.emit("scalar", "activation", rstat[:, RINV, tt:tt + 1], rstat[:, LNSE, tt:tt + 1], AF.Exp, bias=c_zero[:, :], scale=-1.0)
                        p.emit("scalar", "activation", probs[:, tt, :], expt[:, :], AF.Identity, bias=c_zero[:, :],
                               scale=rstat[:, RINV, tt:tt + 1], sem=sS)
                        p.wait("vector", sS, sS.n)
                        p.emit("vector", "tensor_reduce", rstat[:, GMX, tt:tt + 1], probs[:, tt, :], axis=AX.X, op=OP.max, sem=sV)
                    for tt in range(NT):
                        for e in range(NEXP):
                            p.emit("vector", "tensor_tensor", selm[:, tt, e:e + 1], probs[:, tt, e:e + 1],
                                   rstat[:, GMX, tt:tt + 1], OP.is_ge, sem=(sV if e == NEXP - 1 else None))
                    sel_done = sV.n

                    p.wait("tensor", sV, sel_done)
                    p.wait("tensor", sS, rt_free)
                    p.wait("tensor", sL2, b_total)
                    p.emit("tensor", "matmul", p_rt[:, 0:128], Lst_sb[:, :],
                           selm[:, :, :].rearrange("p t e -> p (t e)"), start=True, stop=True, sem=sT)
                    p.wait("vector", sT, sT.n)
                    p.emit("vector", "tensor_copy", rnk[:, :], p_rt[:, 0:128], sem=sV)
                    p.emit("tensor", "matmul", p_rt[0:1, 128:256], ones128[:, :],
                           selm[:, :, :].rearrange("p t e -> p (t e)"), start=True, stop=True, sem=sT)
                    cnt_mm = sT.n
                    p.wait("vector", sT, cnt_mm)
                    p.emit("vector", "tensor_copy", cntrow[:, :], p_rt[0:1, 128:256], sem=sV)
                    p.wait("tensor", sV, sV.n)
                    p.wait("tensor", sL2, b_total)
                    p.emit("tensor", "transpose", p_h[:, 0:1], cntrow[:, :], identf_sb[0:1, 0:1], sem=sT)
                    ct_mm = sT.n
                    p.wait("vector", sT, ct_mm)
                    p.emit("vector", "tensor_copy", cnt_bf[:, :], p_h[:, 0:1], sem=sV)
                    p.wait("tensor", sV, sV.n)
                    p.emit("tensor", "matmul", p_rt[0:1, 384:512], cnt_bf[:, :], Lti_sb[:, :], start=True, stop=True, sem=sT)
                    p.wait("vector", sT, sT.n)
                    p.emit("vector", "tensor_copy", orow16[:, :], p_rt[0:1, 384:512], sem=sV)
                    p.wait("tensor", sV, sV.n)
                    p.emit("tensor", "matmul", p_h[:, 128:256], ones_col[:, :], orow16[:, :], start=True, stop=True, sem=sT)
                    p.wait("vector", sT, sT.n)
                    p.emit("vector", "tensor_copy", obc[:, :], p_h[:, 128:256], sem=sV)
                    p.emit("vector", "tensor_tensor", slotf[:, :], rnk[:, :], obc[:, :], OP.add)
                    p.emit("vector", "tensor_scalar", ovf[:, :], slotf[:, :], float(CAP) - 0.5,
                           65536.0, OP.is_ge, OP.mult)
                    p.emit("vector", "tensor_scalar", pi2[:, :], slotf[:, :], -3.5, 0.125, OP.add, OP.mult)
                    p.emit("vector", "tensor_copy", ri32[:, :], pi2[:, :])
                    p.emit("vector", "tensor_copy", m8[:, :], ri32[:, :])
                    p.emit("vector", "tensor_scalar", pi2[:, :], m8[:, :], -8.0, 0.0, OP.mult, OP.add)
                    p.emit("vector", "tensor_tensor", pi2[:, :], pi2[:, :], slotf[:, :], OP.add)
                    p.emit("vector", "tensor_scalar", pi2[:, :], pi2[:, :], float(16 * (CAP // 128)), 0.0, OP.mult, OP.add)
                    p.emit("vector", "tensor_tensor", pi2[:, :], pi2[:, :], m8[:, :], OP.add)
                    p.emit("vector", "tensor_tensor", pi2[:, :], pi2[:, :], eoff_sb[:, :], OP.add)
                    p.emit("vector", "tensor_tensor", pi2[:, :], pi2[:, :], ovf[:, :], OP.add)
                    p.emit("vector", "tensor_tensor", pi2[:, :], pi2[:, :], selm[:, :, :].rearrange("p t e -> p (t e)"), OP.mult)
                    p.emit("vector", "tensor_reduce", slotg2[:, :],
                           pi2[:, :].rearrange("p (t e) -> p t e", e=NEXP), axis=AX.X, op=OP.add, sem=sV)
                    p.emit("vector", "tensor_tensor", slotf[:, :], slotf[:, :], eoff_sb[:, :], OP.add)
                    p.emit("vector", "tensor_tensor", slotf[:, :], slotf[:, :], ovf[:, :], OP.add)
                    p.emit("vector", "tensor_tensor", slotf[:, :], slotf[:, :], selm[:, :, :].rearrange("p t e -> p (t e)"), OP.mult)
                    p.emit("vector", "tensor_reduce", slotg[:, :],
                           slotf[:, :].rearrange("p (t e) -> p t e", e=NEXP), axis=AX.X, op=OP.add, sem=sV)
                    p.wait("vector", sV, sV.n)
                    p.emit("vector", "tensor_copy", slotu2[:, :], slotg2[:, :])
                    p.emit("vector", "tensor_copy", slotu[:, :], slotg[:, :], sem=sV)
                    slot_done = sV.n

                    p.emit("gpsimd", "memset", pr16[:, :], float(T), sem=sG)
                    p.emit("gpsimd", "memset", pr32[:, :], float(T), sem=sG)
                    p.emit("gpsimd", "dma_start", sem=sW, k=16,
                           out=inv16.rearrange("(p f) one -> p (f one)", p=128), in_=pr16[:, :])
                    p.emit("gpsimd", "dma_start", sem=sW, k=16,
                           out=inv32.rearrange("(p f) one -> p (f one)", p=128), in_=pr32[:, :])
                    p.wait("gpsimd", sW, sW.n)
                    p.wait("gpsimd", sV, slot_done)
                    for tt in range(NT):
                        emit_indirect(inv16, slotu[:, tt:tt + 1], tok16_sb[:, tt:tt + 1], "slot", sW)
                        emit_indirect(inv32, slotu2[:, tt:tt + 1], tok32_sb[:, tt:tt + 1], "slot", sW)
                    inv_done = sW.n
                    rank_T = sT.n
                    rank_V = sV.n
                    rank_S = sS.n

                C32 = CAP // 16
                NST = CAP // 128
                h1 = EB(nc.sbuf_tensor("h1", [128, FH, CAP], BF16))
                Xs0 = EB(nc.sbuf_tensor("Xs0", [128, HC, CAP], BF16))
                Xs1 = EB(nc.sbuf_tensor("Xs1", [128, HC, CAP], BF16))
                XsB = [Xs0, Xs1]
                Ys = EB(nc.sbuf_tensor("Ys", [128, NST, H], F32))
                g16 = EB(nc.sbuf_tensor("g16", [128, C32], I16))
                s32a = EB(nc.sbuf_tensor("s32a", [128, NST], U32))
                s32b = EB(nc.sbuf_tensor("s32b", [128, NST], U32))
                s32B = [s32a, s32b]

                idx_thr = {}
                xs_ld_by = {}
                s32_ld_by = {}
                xs_free_by = {}
                scat_done = {}

                def emit_idx_sync(e):
                    p.wait("gpsimd", sW, inv_done)
                    if e >= 1:
                        p.wait("gpsimd", sW, xs_ld_by[e - 1])
                    t = p.emit("gpsimd", "dma_start", sem=sW, k=16, out=g16[0:16, :],
                               in_=bass.AP(inv16.tensor, e * CAP, [[C32, 16], [1, C32]]))
                    p.wait("gpsimd", sW, t)
                    t = p.emit("gpsimd", "dma_start", sem=sW, k=16, out=g16[16:32, :], in_=g16[0:16, :])
                    p.wait("gpsimd", sW, t)
                    t = p.emit("gpsimd", "dma_start", sem=sW, k=16, out=g16[32:64, :], in_=g16[0:32, :])
                    p.wait("gpsimd", sW, t)
                    g16rep = p.emit("gpsimd", "dma_start", sem=sW, k=16, out=g16[64:128, :], in_=g16[0:64, :])
                    idx_thr[e] = g16rep

                def emit_gather(e):
                    p.wait("gpsimd", sW, idx_thr[e])
                    if e == 0:
                        p.wait("gpsimd", sT, rank_T)
                        p.wait("gpsimd", sV, rank_V)
                        p.wait("gpsimd", sS, rank_S)
                        p.wait("gpsimd", sO, xst_all)
                        p.wait("gpsimd", sW, inv_done)
                    elif e >= 2:
                        p.wait("gpsimd", sT, xs_free_by[e - 2])
                        p.wait("gpsimd", sW, scat_done[e - 2])
                    xs_ld_by[e] = p.emit("gpsimd", "dma_gather", XsB[e % 2][:, :, :], xlt_d,
                                         g16[:, :], CAP, CAP, H, transpose=True, sem=sW, k=16)
                    s32_ld_by[e] = p.emit("gpsimd", "dma_start", sem=sW, k=16, out=s32B[e % 2][:, :],
                                          in_=bass.AP(inv32.tensor, e * CAP, [[NST, 128], [1, NST]]))

                emit_idx_sync(0)
                emit_gather(0)

                h_free = [(sS, 0), (sV, 0)]
                ys_scat = {}
                ys_evac = {}
                for e in range(NEXP):
                    Xs = XsB[e % 2]
                    xs_ld = xs_ld_by[e]
                    s32 = s32B[e % 2]
                    for hf in range(2):
                        w1_free = 0
                        w2_free = 0
                        fgrp = 0
                        for f in range(FH):
                            hb = fgrp % 2
                            reg = (0, 512) if hb == 0 else (512, 1024)
                            if f % 8 == 0:
                                p.wait("tensor", sW, w1_thr[(e, hf, f // 8)])
                            if f == 0:
                                p.wait("tensor", sW, xs_ld)
                                p.wait("tensor", sW, b_thr[e][0])
                            semref, cnt = h_free[hb]
                            p.wait("tensor", semref, cnt)
                            for h in range(HC):
                                p.emit("tensor", "matmul", p_h[:, reg[0]:reg[1]],
                                       w1b[:, h, f * 128:(f + 1) * 128], Xs[:, h, :],
                                       start=(h == 0), stop=(h == HC - 1), sem=(sT if h == HC - 1 else None))
                            g_n = sT.n
                            if f == FH - 1:
                                w1_free = g_n
                                if hf == 1:
                                    xs_free_by[e] = g_n
                            if hb == 0:
                                p.wait("scalar", sT, g_n)
                                p.emit("scalar", "activation", h1[:, f, :], p_h[:, reg[0]:reg[1]], AF.Relu,
                                       bias=b1c[:, hf * FH + f:hf * FH + f + 1], scale=1.0, sem=sS)
                                h_free[0] = (sS, sS.n)
                            else:
                                p.wait("vector", sT, g_n)
                                p.emit("vector", "tensor_scalar", h1[:, f, :], p_h[:, reg[0]:reg[1]],
                                       b1c[:, hf * FH + f:hf * FH + f + 1], 0.0, OP.add, OP.max, sem=sV)
                                h_free[1] = (sV, sV.n)
                            fgrp += 1
                        h1_s = sS.n
                        h1_v = sV.n
                        for st in range(CAP // 128):
                            yp = p_yA if (st % 2 == 0) else p_yB
                            key = (e, hf, st)
                            prevk = None
                            if st >= 2:
                                prevk = (e, hf, st - 2)
                            elif hf == 1:
                                prevk = (e, 0, st + 2)
                            elif e >= 1:
                                prevk = (e - 1, 1, st + 2)
                            if prevk is not None and prevk in ys_evac:
                                eng, cnt = ys_evac[prevk]
                                p.wait("tensor", eng, cnt)
                            p.wait("tensor", sS, h1_s)
                            p.wait("tensor", sV, h1_v)
                            if hf == 0:
                                p.wait("tensor", sW, b_thr[e][1])
                                p.emit("tensor", "matmul", yp[:, 0:512], ones_col[:, :], b2r[:, 0:512], start=True, stop=False)
                                p.emit("tensor", "matmul", yp[:, 512:1024], ones_col[:, :], b2r[:, 512:1024], start=True, stop=False)
                            for f in range(FH):
                                if st == 0 and f % 8 == 0:
                                    p.wait("tensor", sW, w2_thr[(e, hf, f // 8)])
                                st0 = (hf == 1 and f == 0)
                                lastf = (f == FH - 1)
                                p.emit("tensor", "matmul", yp[:, 0:512], h1[:, f, st * 128:(st + 1) * 128],
                                       w2b[:, f, 0:512], start=st0, stop=lastf)
                                p.emit("tensor", "matmul", yp[:, 512:1024], h1[:, f, st * 128:(st + 1) * 128],
                                       w2b[:, f, 512:1024], start=st0, stop=lastf,
                                       sem=(sT if lastf else None))
                            mm2_n = sT.n
                            if st == CAP // 128 - 1:
                                w2_free = mm2_n
                            if hf == 0:
                                p.wait("scalar", sT, mm2_n)
                                if e >= 1:
                                    p.wait("scalar", sW, ys_scat[(e - 1, st)])
                                p.emit("scalar", "activation", Ys[:, st, :], yp[:, 0:1024], AF.Identity,
                                       bias=c_zero[:, :], scale=1.0, sem=sS)
                                ys_evac[key] = (sS, sS.n)
                            else:
                                p.wait("vector", sT, mm2_n)
                                p.emit("vector", "tensor_tensor", Ys[:, st, :], yp[:, 0:1024], Ys[:, st, :], OP.add, sem=sV)
                                ys_evac[key] = (sV, sV.n)
                        b_free = (sS.n, sV.n, sT.n)
                        if hf == 0:
                            emit_weight_loads(e, 1, w1_free, w2_free, b_free)
                            if e < NEXP - 1:
                                emit_idx_sync(e + 1)
                                emit_gather(e + 1)
                        elif e < NEXP - 1:
                            emit_weight_loads(e + 1, 0, w1_free, w2_free, b_free)
                    p.wait("gpsimd", sW, s32_ld_by[e])
                    p.wait("gpsimd", sO, yz_done)
                    for st in range(CAP // 128):
                        eng, cnt = ys_evac[(e, 1, st)]
                        p.wait("gpsimd", eng, cnt)
                        ys_scat[(e, st)] = emit_indirect(yscr, s32[:, st:st + 1], Ys[:, st, :], "tok", sW)
                    scat_done[e] = sW.n
                scat_all = sW.n

                yld = {}
                adds_by = {}

                def emit_store(k):
                    p.wait("sync", sV, adds_by[2 * k + 1])
                    p.emit("sync", "dma_start", sem=sO, k=16,
                           out=y_d.rearrange("(i p) h -> p i h", p=128)[:, 4 * k:4 * (k + 1), :],
                           in_=x_tok[:, 4 * k:4 * (k + 1), :])

                for u in range(8):
                    half = u % 2
                    p.wait("sync", sW, scat_all)
                    if u >= 2:
                        p.wait("sync", sV, adds_by[u - 2])
                    yld[u] = p.emit("sync", "dma_start", sem=sL2, k=16,
                                    out=Ys[:, 2 * half:2 * half + 2, :],
                                    in_=yscr.rearrange("(i p) h -> p i h", p=128)[:, 2 * u:2 * (u + 1), :])
                    for i in range(2):
                        tt = 2 * u + i
                        p.wait("scalar", sL2, yld[u])
                        p.emit("scalar", "activation", Ys[:, 2 * half + i, :], Ys[:, 2 * half + i, :], AF.Identity,
                               bias=c_zero[:, :], scale=rstat[:, GMX, tt:tt + 1], sem=sS)
                        p.wait("vector", sS, sS.n)
                        p.emit("vector", "tensor_tensor", x_tok[:, tt, :], Ys[:, 2 * half + i, :],
                               x_tok[:, tt, :], OP.add, sem=sV)
                    adds_by[u] = sV.n
                    if u >= 3 and u % 2 == 1:
                        emit_store((u - 3) // 2)
                emit_store(3)
                p.wait("sync", sO, sO.n)
                p.flush(blkB)
    import bass_rust as _bass_rust
    from concourse.library_config import all_libraries, standard
    lm = {}
    for lib in all_libraries:
        for it in lib.instructions:
            lm[it] = lm.get(it, 0) | (1 << lib.index)
    _bass_rust.insert_library_loads(nc, lm, len(all_libraries), standard.index)
    mybir.codegen_inst_isa_subclasses(nc)
    return nc


def _consts():
    lam = np.float64(DECAY)
    jl = np.arange(128)
    tl = np.arange(128)
    mk = np.where(jl[:, None] < tl[None, :], lam ** (tl[None, :] - 1 - jl[:, None]), 0.0) / B
    mask = np.concatenate([mk] * B, axis=1).astype(np.float32)
    dq = np.concatenate([np.broadcast_to(lam ** tl, (128, 128))] * B, axis=1).astype(np.float32)
    lamk = (lam ** (127 - jl) / B).astype(np.float32).reshape(128, 1)
    ident = np.eye(128, dtype=ml_dtypes.bfloat16)
    lstrict = np.triu(np.ones((128, 128)), 1).astype(ml_dtypes.bfloat16)
    tt_i = np.arange(128) // NEXP
    e_i = np.arange(128) % NEXP
    ltile = ((e_i[:, None] == e_i[None, :]) & (tt_i[:, None] < tt_i[None, :])).astype(ml_dtypes.bfloat16)
    eoff = np.broadcast_to((e_i * CAP).astype(np.float32), (128, 128)).copy()
    identf = np.eye(128, dtype=np.float32)
    tok16 = (np.arange(128)[:, None] + 128 * np.arange(NT)[None, :]).astype(np.int16)
    tok32 = tok16.astype(np.uint32)
    return mask, np.ascontiguousarray(dq), lamk, ident, lstrict, ltile, eoff, identf, tok16, tok32


_NC_CACHE = {}
LAST_RES = None


def kernel(L0_attn_x, L0_attn_Wq, L0_attn_Wk, L0_attn_Wv, L0_attn_Wo, L0_attn_gate, L0_attn_M0,
           L0_ffn_Wr, L0_ffn_br, L0_ffn_W1, L0_ffn_b1, L0_ffn_W2, L0_ffn_b2):
    x = np.asarray(L0_attn_x, dtype=np.float32)
    wqt = np.ascontiguousarray(np.asarray(L0_attn_Wq, np.float32).T)
    wkt = np.ascontiguousarray(np.asarray(L0_attn_Wk, np.float32).T)
    wvt = np.ascontiguousarray(np.asarray(L0_attn_Wv, np.float32).T)
    wot = np.ascontiguousarray(np.asarray(L0_attn_Wo, np.float32).T)
    gate = np.asarray(L0_attn_gate, np.float32).reshape(1, 1)
    m0t = np.ascontiguousarray(np.asarray(L0_attn_M0, np.float32).T)
    wr = np.ascontiguousarray(np.asarray(L0_ffn_Wr, np.float32))
    br = np.asarray(L0_ffn_br, np.float32).reshape(1, NEXP)
    w1 = np.ascontiguousarray(
        np.asarray(L0_ffn_W1, np.float32).reshape(NEXP, HC, 128, 2, 2, 1024)
        .transpose(0, 3, 4, 2, 1, 5))
    b1 = np.ascontiguousarray(np.asarray(L0_ffn_b1, np.float32).reshape(NEXP, FC, 128).transpose(0, 2, 1))
    w2 = np.ascontiguousarray(
        np.asarray(L0_ffn_W2, np.float32).reshape(NEXP, 2, 2, HC, 128, 1024)
        .transpose(0, 1, 2, 4, 3, 5))
    b2 = np.ascontiguousarray(np.asarray(L0_ffn_b2, np.float32))
    mask, dq, lamk, ident, lstrict, ltile, eoff, identf, tok16, tok32 = _consts()

    if "nc" not in _NC_CACHE:
        _NC_CACHE["nc"] = build()
    nc = _NC_CACHE["nc"]

    lam = np.float64(DECAY)
    in_maps = []
    for c in range(NCORE):
        xs = np.ascontiguousarray(x[:, c * SC:(c + 1) * SC, :]).reshape(T, H)
        w = np.zeros(9, np.float64)
        for k in range(c):
            w[k] = lam ** (SC * (c - 1 - k))
        w[8] = lam ** (SC * c)
        prefw = np.ascontiguousarray(np.tile(w.astype(np.float32)[None, :], (128, 1)))
        in_maps.append({
            "x": xs, "wqt": wqt, "wkt": wkt, "wvt": wvt, "wot": wot, "gate": gate, "m0t": m0t,
            "wr": wr, "br": br, "w1": w1, "b1": b1, "w2": w2, "b2": b2,
            "mask": mask, "dq": dq, "lamk": lamk, "prefw": prefw, "ident": ident,
            "lstrict": lstrict, "ltile": ltile, "eoff": eoff, "identf": identf,
            "tok16": tok16, "tok32": tok32,
        })
    res = run_bass_kernel_spmd(nc, in_maps, core_ids=list(range(NCORE)))
    global LAST_RES
    LAST_RES = res
    outs = [np.asarray(res.results[c]["y"], dtype=np.float32).reshape(B, SC, H) for c in range(NCORE)]
    return np.concatenate(outs, axis=1)
